# version 10
# baseline (speedup 1.0000x reference)
import numpy as np
import contextlib
import concourse.bass as bass
import concourse.mybir as mybir
from concourse.bass_utils import run_bass_kernel_spmd

F32 = mybir.dt.float32
F32R = mybir.dt.float32r
BF16 = mybir.dt.bfloat16
FP8 = mybir.dt.float8e4
AF = mybir.ActivationFunctionType
ALU = mybir.AluOpType
PM = mybir.MatmulPerfMode
AX = mybir.AxisListType

E = 2048
TS = 1024
H, L = 16, 64
HID = 8192
NCH = 16
INV = float(1.0 / np.sqrt(L))
RAH = float(0.5 / np.sqrt(L))
PAD = 78.0

ENGS = ("sync", "gpsimd", "tensor", "vector", "scalar")


class Plan:
    def __init__(self):
        self.pend = {e: [] for e in ENGS}
        self.cnt = dict(dS=0, dS2=0, dG=0, dG2=0, T=0, V=0, A=0, PL=0)
        self.rr = {"sync": 0, "gpsimd": 0}
        self.bank_started = {}
        self.barrier_waits = []
        self.barrier_pending = {e: False for e in ENGS}

    def now(self, *sems):
        return [(s, self.cnt[s]) for s in sems]

    def set_barrier(self):
        self.barrier_waits = [(s, self.cnt[s]) for s in self.cnt]
        self.barrier_pending = {e: True for e in ENGS}

    def emit(self, eng, fn, waits=(), inc=None, amt=None):
        if amt is None:
            amt = 16 if inc in ("dS", "dG", "dS2", "dG2") else 1
        w = list(waits)
        if inc in ("dS", "dG", "dS2", "dG2"):
            w = [(inc, self.cnt[inc])] + w
        if eng == "vector":
            w = [("V", self.cnt["V"])] + w
        elif eng == "scalar":
            w = [("A", self.cnt["A"])] + w
        elif eng == "gpsimd" and inc == "PL":
            w = [("PL", self.cnt["PL"])] + w
        if self.barrier_pending[eng]:
            w = self.barrier_waits + w
            self.barrier_pending[eng] = False
        self.pend[eng].append((fn, w, inc, amt))
        if inc is not None:
            self.cnt[inc] += amt
        return self.cnt.get(inc)

    def first_touch(self, bank_key):
        if bank_key in self.bank_started:
            return False
        self.bank_started[bank_key] = True
        return True

    def new_epoch(self):
        self.bank_started = {}


def build_program(debug=()):
    nc = bass.Bass("TRN2", target_bir_lowering=False, debug=False)
    dt = nc.dram_tensor
    xoT = dt("xoT", [E, TS], F32, kind="ExternalInput").ap()
    xrT = dt("xrT", [E, TS], F32, kind="ExternalInput").ap()
    wqT = dt("wqT", [E, E], F32, kind="ExternalInput").ap()
    q2li = dt("q2l", [H, 128, L], F32, kind="ExternalInput").ap()
    wkT = dt("wkT", [E, L], F32, kind="ExternalInput").ap()
    wvT = dt("wvT", [E, L], F32, kind="ExternalInput").ap()
    vupTi = dt("vupT", [H, 128, L], F32, kind="ExternalInput").ap()
    woT = dt("woT", [E, E], F32, kind="ExternalInput").ap()
    wg = dt("wg", [2, E, HID], F32, kind="ExternalInput").ap()
    wlin = dt("wlin", [HID, E], F32, kind="ExternalInput").ap()
    idn = dt("idn", [128, 128], F32, kind="ExternalInput").ap()
    mD_add = dt("mD_add", [128, 128], F32, kind="ExternalInput").ap()
    mO_add = dt("mO_add", [128, 128], F32, kind="ExternalInput").ap()
    mDB_mult = dt("mDB_mult", [128, 128], F32, kind="ExternalInput").ap()
    mOB_mult = dt("mOB_mult", [128, 256], F32, kind="ExternalInput").ap()
    outT = dt("outT", [E, TS], F32, kind="ExternalOutput").ap()
    dbg_out = {}
    for name, shape in debug:
        dbg_out[name] = dt("dbg_" + name, shape, F32, kind="ExternalOutput").ap()

    P = Plan()
    st_per = contextlib.ExitStack()
    st_kv = contextlib.ExitStack()
    st_p1 = contextlib.ExitStack()
    st_p2 = contextlib.ExitStack()
    st_p3 = contextlib.ExitStack()
    st_p4 = contextlib.ExitStack()
    st_p5 = contextlib.ExitStack()

    def hiv(ap_f32):
        return ap_f32.bitcast(BF16)[..., 1::2]

    with (
        nc.psum_tensor("pA", [128, 1024], F32) as pA,
        nc.psum_tensor("pB", [128, 1024], F32) as pB,
        nc.psum_tensor("pC", [128, 512], F32) as pC,
        nc.psum_tensor("pD", [128, 512], F32) as pD,
        nc.psum_tensor("pE", [128, 512], F32) as pE,
        nc.psum_tensor("pF", [128, 512], F32) as pF,
        nc.semaphore("dS") as sem_dS,
        nc.semaphore("dS2") as sem_dS2,
        nc.semaphore("dG") as sem_dG,
        nc.semaphore("dG2") as sem_dG2,
        nc.semaphore("T") as sem_T,
        nc.semaphore("V") as sem_V,
        nc.semaphore("A") as sem_A,
        nc.semaphore("PL") as sem_PL,
        nc.Block() as block,
    ):
        sems = {"dS": sem_dS, "dS2": sem_dS2, "dG": sem_dG, "dG2": sem_dG2,
                "T": sem_T, "V": sem_V, "A": sem_A, "PL": sem_PL}
        wms = {e: {} for e in ENGS}
        PSNAME = {id(pA): "A", id(pB): "B", id(pC): "C", id(pD): "D",
                  id(pE): "E", id(pF): "F"}

        def S(fn, waits=()):
            P.rr["sync"] ^= 1
            sem = "dS" if P.rr["sync"] else "dS2"
            return (sem, P.emit("sync", fn, waits, sem))

        def G(fn, waits=()):
            P.rr["gpsimd"] ^= 1
            sem = "dG" if P.rr["gpsimd"] else "dG2"
            return (sem, P.emit("gpsimd", fn, waits, sem))

        def T(fn, waits=()):
            return P.emit("tensor", fn, waits, "T")

        def GC(fn, waits=()):
            return P.emit("gpsimd", fn, waits, "PL")

        def V(fn, waits=()):
            return P.emit("vector", fn, waits, "V")

        def A(fn, waits=()):
            return P.emit("scalar", fn, waits, "A")

        def flush():
            for name in ENGS:
                ops = P.pend[name]
                if not ops:
                    continue
                P.pend[name] = []
                wm = wms[name]

                def body(eng, ops=ops, wm=wm):
                    for fn, waits, inc, amt in ops:
                        for s, v in waits:
                            if v > 0 and wm.get(s, 0) < v:
                                eng.wait_ge(sems[s], v)
                                wm[s] = v
                        ins = fn(eng)
                        if inc is not None:
                            ins.then_inc(sems[inc], amt)
                getattr(block, name)(body)

        def barrier():
            return P.now("dS", "dS2", "dG", "dG2", "T", "V", "A", "PL")

        def tap(name, src_ap, rows, cols):
            if name in dbg_out:
                w = barrier()
                G((lambda name, src_ap, rows, cols: lambda e: e.dma_start(
                    out=dbg_out[name][0:rows, 0:cols], in_=src_ap))(name, src_ap, rows, cols),
                  waits=w)

        def alloc(st, name, shape, dtype):
            return st.enter_context(nc.sbuf_tensor(name, shape, dtype))

        ones128 = alloc(st_per, "ones128", [128, 1], F32)
        ones128b = alloc(st_per, "ones128b", [128, 1], BF16)
        onesrow = alloc(st_per, "onesrow", [1, 128], F32)
        m65 = alloc(st_per, "m65", [65, 1], F32)
        mDadd = alloc(st_per, "mDadd", [128, 128], BF16)
        mOadd = alloc(st_per, "mOadd", [128, 128], BF16)
        mDBm = alloc(st_per, "mDBm", [128, 128], BF16)
        mOBm = alloc(st_per, "mOBm", [128, 256], BF16)
        bcM = alloc(st_per, "bcM", [128, TS], F32)
        bcR = alloc(st_per, "bcR", [128, TS], F32)
        srow = alloc(st_per, "srow", [16, TS], F32)
        d128 = alloc(st_per, "d128", [128, 128], F32)
        q128s = alloc(st_per, "q128s", [128, 128], F32)
        b128 = alloc(st_per, "b128", [128, 128], F32)
        s1s = alloc(st_per, "s1s", [128, 128], F32)
        AR = alloc(st_per, "AR", [128, 133120], mybir.dt.uint8)
        xho_hi = AR[:, 0:32768].bitcast(BF16).rearrange("p (c t) -> p c t", c=NCH)
        xho_lo = AR[:, 32768:65536].bitcast(BF16).rearrange("p (c t) -> p c t", c=NCH)
        ctxPK1 = AR[:, 0:32768].bitcast(BF16).rearrange("p (c t) -> p c t", c=H)
        ctxPK2 = AR[:, 32768:65536].bitcast(BF16).rearrange("p (c t) -> p c t", c=H)
        ctxh = ctxPK1[0:64]
        ctxl = ctxPK2[0:64]
        act8 = AR[:, 0:65536].bitcast(FP8).rearrange("p (c t) -> p c t", c=64)
        qhPK = AR[:, 65536:98304].bitcast(BF16).rearrange("p (c t) -> p c t", c=H)
        qh65 = qhPK[0:65]
        ql65 = AR[:, 98304:131072].bitcast(BF16).rearrange("p (c t) -> p c t", c=H)[0:65]
        qsum3p = AR[:, 131072:133120].bitcast(BF16)[0:48]
        xp = AR[:, 65536:131072].bitcast(F32).rearrange("p (c t) -> p c t", c=NCH)
        kh65 = alloc(st_kv, "kh65", [65, 2 * TS], BF16)
        khl = alloc(st_kv, "khl", [128, 2 * TS], BF16)
        onesP64 = alloc(st_kv, "onesP64", [65, 64], F32)
        vtmh = alloc(st_kv, "vtmh", [128, 16, 65], BF16)
        vtml = alloc(st_kv, "vtml", [128, 16, 65], BF16)
        konT = alloc(st_kv, "konT", [65, TS], F32)
        ksc3 = alloc(st_kv, "ksc3", [3, 2 * TS], BF16)
        kstat = alloc(st_kv, "kstat", [128, 32], BF16)
        wmx16 = alloc(st_kv, "wmx16", [128, 128], F32)
        wmn16 = alloc(st_kv, "wmn16", [128, 128], F32)
        wrow = alloc(st_kv, "wrow", [1, 48], F32)
        idns = alloc(st_kv, "idns", [128, 128], F32)
        segm = alloc(st_kv, "segm", [1, 64], F32)

        G(lambda e: e.dma_start(out=mDadd[:], in_=mD_add[:, :]))
        G(lambda e: e.dma_start(out=mOadd[:], in_=mO_add[:, :]))
        G(lambda e: e.dma_start(out=mDBm[:], in_=mDB_mult[:, :]))
        G(lambda e: e.dma_start(out=mOBm[:], in_=mOB_mult[:, :]))
        c0 = P.now("dS", "dS2", "dG", "dG2")
        V(lambda e: e.memset(ones128[:], 1.0), waits=c0)
        V(lambda e: e.memset(ones128b[:], 1.0))
        V(lambda e: e.memset(onesrow[:], 1.0))
        V(lambda e: e.memset(m65[0:64, :], 1.0))
        V(lambda e: e.memset(m65[64:65, :], 0.5))
        V(lambda e: e.memset(onesP64[64:65, :], 1.0))
        P.set_barrier()

        kvT = AR[:, 65536:73728].bitcast(F32)
        kvTlo = AR[:, 73728:77824].bitcast(BF16)
        kscrow = AR[:, 77824:86016].bitcast(F32)[0:1]
        xc = AR[:, 86016:94208].bitcast(F32).rearrange("p (a t) -> p a t", a=2)
        kminb = AR[:, 94208:94272].bitcast(BF16)[0:64]
        tr1 = AR[:, 94272:98368].bitcast(BF16)[0:1]
        mrow = alloc(st_p1, "mrow", [1, TS], F32)
        rrow = alloc(st_p1, "rrow", [1, TS], F32)
        scr1 = alloc(st_p1, "scr1", [1, TS], F32)
        scr2 = alloc(st_p1, "scr2", [1, TS], F32)
        sq = alloc(st_p1, "sq", [128, TS], F32)
        uu = sq
        xrh = alloc(st_p1, "xrh", [128, 2, TS], BF16)
        xrl = alloc(st_p1, "xrl", [128, 2, TS], BF16)
        kvW = AR[:, 98368:106560].bitcast(F32).rearrange("p (c l) -> p c l", c=NCH)
        kvWlo = AR[:, 106560:110656].bitcast(BF16).rearrange("p (c l) -> p c l", c=NCH)
        kW65 = AR[:, 110656:114816].bitcast(F32).rearrange("p (c l) -> p c l", c=NCH)
        kW65lo = AR[:, 114816:116896].bitcast(BF16).rearrange("p (c l) -> p c l", c=NCH)

        S(lambda e: e.dma_start(out=idns[:], in_=idn[:, :]))
        S(lambda e: e.dma_start(out=kvW[:, :, 0:64],
                                in_=wkT.rearrange("(c p) l -> p c l", p=128)))
        S(lambda e: e.dma_start(out=kvW[:, :, 64:128],
                                in_=wvT.rearrange("(c p) l -> p c l", p=128)))
        S(lambda e: e.dma_start(out=kW65[:, :, 0:64],
                                in_=wkT.rearrange("(c p) l -> p c l", p=128)))
        wl_ = P.now("dS", "dS2")
        V(lambda e: e.tensor_tensor(out=kvWlo[:].rearrange("p c l -> p (c l)"),
                                    in0=kvW[:].rearrange("p c l -> p (c l)"),
                                    in1=hiv(kvW[:].rearrange("p c l -> p (c l)")),
                                    op=ALU.subtract), waits=wl_)
        for c in range(NCH):
            V((lambda c: lambda e: e.tensor_reduce(
                out=kW65[:, c, 64:65], in_=kW65[:, c, 0:64], axis=AX.X, op=ALU.add))(c),
              waits=wl_)
        V(lambda e: e.tensor_tensor(out=kW65lo[:].rearrange("p c l -> p (c l)"),
                                    in0=kW65[:].rearrange("p c l -> p (c l)"),
                                    in1=hiv(kW65[:].rearrange("p c l -> p (c l)")),
                                    op=ALU.subtract))

        def ln_rows(src_dram):
            P.new_epoch()
            sw = P.now("T", "V", "A")
            done = {-2: sw, -1: sw}
            for c in range(NCH):
                buf = c % 2
                ld = S((lambda c, buf: lambda e: e.dma_start(
                    out=xc[:, buf, :], in_=src_dram[128 * c:128 * (c + 1), :]))(c, buf),
                    waits=done.get(c - 2, []))
                lw = [ld]
                A((lambda buf: lambda e: e.activation(
                    out=sq[:].bitcast(BF16)[:, 0:TS], in_=xc[:, buf, :],
                    func=AF.Square))(buf),
                  waits=lw + (P.now("T") if c > 0 else []))
                for qb in range(2):
                    T((lambda c, buf, qb, st: lambda e: e.matmul(
                        pA[0:1, 512 * qb:512 * (qb + 1)], ones128b[:],
                        hiv(xc[:, buf, 512 * qb:512 * (qb + 1)]),
                        start=st, stop=False, skip_group_check=True))(
                        c, buf, qb, P.first_touch(("A", qb))), waits=lw)
                aw = P.now("A")
                for qb in range(2):
                    T((lambda qb, st: lambda e: e.matmul(
                        pA[32:33, 512 * qb:512 * (qb + 1)], ones128b[:],
                        sq[:].bitcast(BF16)[:, 512 * qb:512 * (qb + 1)],
                        start=st, stop=False, skip_group_check=True))(
                        qb, P.first_touch(("A32", qb))), waits=aw)
                done[c] = P.now("T", "A")
            tw = P.now("T")
            V(lambda e: e.tensor_scalar(out=mrow[:, :], in0=pA[0:1, :], scalar1=1.0 / E,
                                        scalar2=0.0, op0=ALU.mult, op1=ALU.add), waits=tw)
            V(lambda e: e.tensor_scalar(out=rrow[:, :], in0=pA[32:33, :],
                                        scalar1=1.0 / E, scalar2=1e-5,
                                        op0=ALU.mult, op1=ALU.add))
            V(lambda e: e.scalar_tensor_tensor(out=scr1[:, :], in0=mrow[:, :],
                                               scalar=-1.0, in1=mrow[:, :],
                                               op0=ALU.mult, op1=ALU.mult))
            V(lambda e: e.tensor_tensor(out=rrow[:, :], in0=rrow[:, :], in1=scr1[:, :],
                                        op=ALU.add))
            V(lambda e: e.reciprocal(out=scr1[:, :], in_=rrow[:, :]))
            vw = P.now("V")
            A(lambda e: e.activation(out=scr1[:, :], in_=scr1[:, :], func=AF.Sqrt),
              waits=vw)
            aw = P.now("A")
            V(lambda e: e.tensor_tensor(out=scr2[:, :], in0=scr1[:, :], in1=scr1[:, :],
                                        op=ALU.mult), waits=aw)
            V(lambda e: e.tensor_tensor(out=scr2[:, :], in0=scr2[:, :], in1=rrow[:, :],
                                        op=ALU.mult))
            V(lambda e: e.tensor_scalar(out=scr2[:, :], in0=scr2[:, :], scalar1=-0.5,
                                        scalar2=1.5, op0=ALU.mult, op1=ALU.add))
            V(lambda e: e.tensor_tensor(out=rrow[:, :], in0=scr1[:, :], in1=scr2[:, :],
                                        op=ALU.mult))
            P.new_epoch()
            vw = P.now("V")
            for srcrow, dst in ((mrow, bcM), (rrow, bcR)):
                for qb in range(2):
                    T((lambda srcrow, qb, st: lambda e: e.matmul(
                        pB[:, 512 * qb:512 * (qb + 1)], onesrow[:, :],
                        srcrow[:, 512 * qb:512 * (qb + 1)], start=st, stop=False,
                        skip_group_check=True))(srcrow, qb, True), waits=vw)
                tw = P.now("T")
                V((lambda dst: lambda e: e.tensor_copy(out=dst[:], in_=pB[:]))(dst),
                  waits=tw)
                vw = P.now("V")

        def ln_apply(src_dram, hi_dst, lo_dst, mm_fn):
            sw = P.now("T", "V", "A")
            done = {-2: sw, -1: sw}
            for c in range(NCH):
                buf = c % 2
                ld = S((lambda c, buf: lambda e: e.dma_start(
                    out=xc[:, buf, :], in_=src_dram[128 * c:128 * (c + 1), :]))(c, buf),
                    waits=done.get(c - 2, []))
                V((lambda buf: lambda e: e.tensor_tensor(
                    out=uu[:], in0=xc[:, buf, :], in1=bcM[:], op=ALU.subtract))(buf),
                  waits=[ld])
                V(lambda e: e.tensor_tensor(out=uu[:], in0=uu[:], in1=bcR[:],
                                            op=ALU.mult))
                hd, ldst = hi_dst(c), lo_dst(c)
                V((lambda hd: lambda e: e.tensor_copy(out=hd, in_=uu[:]))(hd))
                V((lambda hd, ldst: lambda e: e.tensor_tensor(
                    out=ldst, in0=uu[:], in1=hd, op=ALU.subtract))(hd, ldst))
                mm_fn(c)
                done[c] = P.now("T", "V")

        ln_rows(xoT)
        P.new_epoch()

        def latent_own_mm(c):
            vw = P.now("V")
            for j in range(8):
                ps, col = (pC, 128 * j) if j < 4 else (pD, 128 * (j - 4))
                for pi, (wap, xap) in enumerate((
                        (hiv(kvW[:, c, :]), xho_hi[:, c, 128 * j:128 * (j + 1)]),
                        (hiv(kvW[:, c, :]), xho_lo[:, c, 128 * j:128 * (j + 1)]),
                        (kvWlo[:, c, :], xho_hi[:, c, 128 * j:128 * (j + 1)]))):
                    st = P.first_touch((PSNAME[id(ps)], col // 512))
                    T((lambda ps, col, wap, xap, st: lambda e: e.matmul(
                        ps[:, col:col + 128], wap, xap, start=st, stop=False,
                        skip_group_check=True))(ps, col, wap, xap, st), waits=vw)
            for qb in range(2):
                ps = pE if qb == 0 else pF
                for pi, (wap, xap) in enumerate((
                        (hiv(kW65[:, c, :]), xho_hi[:, c, 512 * qb:512 * (qb + 1)]),
                        (hiv(kW65[:, c, :]), xho_lo[:, c, 512 * qb:512 * (qb + 1)]),
                        (kW65lo[:, c, :], xho_hi[:, c, 512 * qb:512 * (qb + 1)]))):
                    st = P.first_touch((PSNAME[id(ps)], 0))
                    T((lambda ps, wap, xap, st: lambda e: e.matmul(
                        ps[0:65, :], wap, xap, start=st, stop=False,
                        skip_group_check=True))(ps, wap, xap, st), waits=vw)

        ln_apply(xoT, lambda c: xho_hi[:, c, :], lambda c: xho_lo[:, c, :], latent_own_mm)
        tw = P.now("T")
        for j in range(8):
            ps, col = (pC, 128 * j) if j < 4 else (pD, 128 * (j - 4))
            V((lambda j, ps, col: lambda e: e.tensor_scalar(
                out=kvT[0:64, 128 * j:128 * (j + 1)], in0=ps[0:64, col:col + 128],
                scalar1=INV, scalar2=0.0, op0=ALU.mult, op1=ALU.add))(j, ps, col),
              waits=tw)
            V((lambda j, ps, col: lambda e: e.tensor_copy(
                out=kvT[64:128, 128 * j:128 * (j + 1)],
                in_=ps[64:128, col:col + 128]))(j, ps, col))
        V(lambda e: e.tensor_scalar(out=konT[:, 0:512], in0=pE[0:65, :], scalar1=INV,
                                    scalar2=0.0, op0=ALU.mult, op1=ALU.add), waits=tw)
        V(lambda e: e.tensor_scalar(out=konT[:, 512:1024], in0=pF[0:65, :], scalar1=INV,
                                    scalar2=0.0, op0=ALU.mult, op1=ALU.add))

        ln_rows(xrT)
        P.new_epoch()

        def latent_oth_mm(c):
            vw = P.now("V")
            buf = c % 2
            for j in range(8):
                ps, col = (pC, 128 * j) if j < 4 else (pD, 128 * (j - 4))
                for pi, (wap, xap) in enumerate((
                        (hiv(kvW[:, c, :]), xrh[:, buf, 128 * j:128 * (j + 1)]),
                        (hiv(kvW[:, c, :]), xrl[:, buf, 128 * j:128 * (j + 1)]),
                        (kvWlo[:, c, :], xrh[:, buf, 128 * j:128 * (j + 1)]))):
                    st = P.first_touch((PSNAME[id(ps)], col // 512))
                    T((lambda ps, col, wap, xap, st: lambda e: e.matmul(
                        ps[:, col:col + 128], wap, xap, start=st, stop=False,
                        skip_group_check=True))(ps, col, wap, xap, st), waits=vw)

        ln_apply(xrT, lambda c: xrh[:, c % 2, :], lambda c: xrl[:, c % 2, :], latent_oth_mm)
        tw = P.now("T")
        for j in range(8):
            ps, col = (pC, 128 * j) if j < 4 else (pD, 128 * (j - 4))
            V((lambda j, ps, col: lambda e: e.tensor_scalar(
                out=kvT[0:64, 1024 + 128 * j:1024 + 128 * (j + 1)],
                in0=ps[0:64, col:col + 128], scalar1=INV, scalar2=0.0,
                op0=ALU.mult, op1=ALU.add))(j, ps, col), waits=tw)
            V((lambda j, ps, col: lambda e: e.tensor_copy(
                out=kvT[64:128, 1024 + 128 * j:1024 + 128 * (j + 1)],
                in_=ps[64:128, col:col + 128]))(j, ps, col))

        V(lambda e: e.tensor_tensor(out=kvTlo[:], in0=kvT[:], in1=hiv(kvT[:]),
                                    op=ALU.subtract))
        V(lambda e: e.tensor_copy(out=kh65[0:64, :], in_=hiv(kvT[0:64, :])))
        V(lambda e: e.memset(kh65[64:65, :], 1.0))
        V(lambda e: e.tensor_copy(out=khl[0:64, :], in_=hiv(kvT[0:64, :])))
        vwk = P.now("V")
        G(lambda e: e.dma_start(out=khl[64:128, :], in_=kvTlo[0:64, :]), waits=vwk)
        P.new_epoch()
        vw = P.now("V")
        for qq in range(4):
            ps = pA if qq < 2 else pB
            col = 512 * (qq % 2)
            for half in range(2):
                src = (hiv(kvT[0:64, 512 * qq:512 * (qq + 1)]) if half == 0
                       else kvTlo[0:64, 512 * qq:512 * (qq + 1)])
                st = P.first_touch((PSNAME[id(ps)], col // 512))
                T((lambda ps, col, src, st: lambda e: e.matmul(
                    ps[0:1, col:col + 512], ones128b[0:64, :], src, start=st,
                    stop=False, skip_group_check=True))(ps, col, src, st), waits=vw)
        tw = P.now("T")
        V(lambda e: e.tensor_scalar(out=kscrow[0:1, 0:1024], in0=pA[0:1, :], scalar1=0.5,
                                    scalar2=0.0, op0=ALU.mult, op1=ALU.add), waits=tw)
        V(lambda e: e.tensor_scalar(out=kscrow[0:1, 1024:2048], in0=pB[0:1, :],
                                    scalar1=0.5, scalar2=0.0, op0=ALU.mult, op1=ALU.add))
        V(lambda e: e.tensor_copy(out=ksc3[0:1, :], in_=hiv(kscrow[:])))
        V(lambda e: e.tensor_tensor(out=tr1[0:1, :], in0=kscrow[:], in1=hiv(kscrow[:]),
                                    op=ALU.subtract))
        vw = P.now("V")
        G(lambda e: e.dma_start(out=ksc3[1:2, :], in_=tr1[0:1, :]), waits=vw)
        G(lambda e: e.dma_start(out=ksc3[2:3, :], in_=ksc3[0:1, :]), waits=vw)
        V(lambda e: e.tensor_reduce(out=segm[0:1, 0:32],
                                    in_=kscrow[:].rearrange("o (s w) -> o s w", w=64),
                                    axis=AX.X, op=ALU.max))
        V(lambda e: e.tensor_reduce(out=segm[0:1, 32:64],
                                    in_=kscrow[:].rearrange("o (s w) -> o s w", w=64),
                                    axis=AX.X, op=ALU.min))
        for r, op in ((0, ALU.max), (1, ALU.min)):
            o = 24 * r
            b = 32 * r
            V((lambda op, o, b: lambda e: e.tensor_tensor(
                out=wrow[0:1, o:o + 8], in0=segm[0:1, b + 0:b + 16:2],
                in1=segm[0:1, b + 1:b + 16:2], op=op))(op, o, b))
            V((lambda op, o, b: lambda e: e.tensor_tensor(
                out=wrow[0:1, o + 8:o + 16], in0=segm[0:1, b + 16:b + 32:2],
                in1=segm[0:1, b + 17:b + 32:2], op=op))(op, o, b))
            V((lambda op, o, b: lambda e: e.tensor_tensor(
                out=wrow[0:1, o + 16:o + 24], in0=segm[0:1, b + 14:b + 30:2],
                in1=segm[0:1, b + 15:b + 30:2], op=op))(op, o, b))
            V((lambda op, o: lambda e: e.tensor_tensor(
                out=wrow[0:1, o:o + 8], in0=wrow[0:1, o:o + 8],
                in1=wrow[0:1, o + 8:o + 16], op=op))(op, o))
            V((lambda op, o: lambda e: e.tensor_tensor(
                out=wrow[0:1, o:o + 8], in0=wrow[0:1, o:o + 8],
                in1=wrow[0:1, o + 16:o + 24], op=op))(op, o))
        P.new_epoch()
        vw = P.now("V")
        T((lambda st: lambda e: e.matmul(pC[:, 0:8], onesrow[:, :], wrow[0:1, 0:8],
                                         start=st, stop=False, skip_group_check=True))(
            P.first_touch(("C", 0))), waits=vw)
        T(lambda e: e.matmul(pC[:, 8:16], onesrow[:, :], wrow[0:1, 24:32], start=False,
                             stop=False, skip_group_check=True))
        tw = P.now("T")
        for h in range(H):
            V((lambda h: lambda e: e.tensor_copy(out=wmx16[:, 8 * h:8 * h + 8],
                                                 in_=pC[:, 0:8]))(h), waits=tw)
            V((lambda h: lambda e: e.tensor_copy(out=wmn16[:, 8 * h:8 * h + 8],
                                                 in_=pC[:, 8:16]))(h))
        for b in range(32):
            V((lambda b: lambda e: e.tensor_reduce(
                out=kstat[0:64, b:b + 1], in_=kvT[0:64, 64 * b:64 * (b + 1)],
                axis=AX.X, op=ALU.max))(b))
            V((lambda b: lambda e: e.tensor_reduce(
                out=kminb[:, b:b + 1], in_=kvT[0:64, 64 * b:64 * (b + 1)],
                axis=AX.X, op=ALU.min))(b))
        vw = P.now("V")
        G(lambda e: e.dma_start(out=kstat[64:128, :], in_=kminb[:, :]), waits=vw)
        for kt in range(16):
            vw2 = P.now("V")
            T((lambda kt: lambda e: e.transpose(
                pD[:, 0:64], kvT[64:128, 128 * kt:128 * (kt + 1)],
                idns[64:128, 64:128]))(kt), waits=vw2)
            tw = P.now("T")
            V((lambda kt: lambda e: e.tensor_copy(
                out=vtmh[:, kt, 0:64], in_=pD[:, 0:64]))(kt), waits=tw)
            V((lambda kt: lambda e: e.tensor_tensor(
                out=vtml[:, kt, 0:64], in0=pD[:, 0:64], in1=vtmh[:, kt, 0:64],
                op=ALU.subtract))(kt))
        V(lambda e: e.memset(vtmh[:, :, 64:65].rearrange("p c l -> p (c l)"), 1.0))
        V(lambda e: e.memset(vtml[:, :, 64:65].rearrange("p c l -> p (c l)"), 0.0))
        tap("xho_hi", xho_hi[:, 0, :], 128, TS)
        tap("kvT", kvT[:, :], 128, 2 * TS)
        tap("konT", konT[0:65, :], 65, TS)
        tap("kscrow", kscrow[0:1, :], 1, 2 * TS)
        flush()
        st_p1.close()
        P.set_barrier()

        qlatH = alloc(st_p2, "qlatH", [65, TS], F32)
        wqc = alloc(st_p2, "wqc", [128, 2, E], F32)
        wqlo = alloc(st_p2, "wqlo", [128, 512], BF16)
        q2l65 = alloc(st_p2, "q2l65", [128, H, 65], F32)
        q2l65lo = alloc(st_p2, "q2l65lo", [128, H, 65], BF16)
        qevh = alloc(st_p2, "qevh", [128, 512], BF16)
        qevl = alloc(st_p2, "qevl", [128, 512], BF16)
        qpn = alloc(st_p2, "qpn", [128, TS], BF16)
        qntmp = alloc(st_p2, "qntmp", [64, TS], BF16)
        prodx = alloc(st_p2, "prodx", [65, TS], F32)
        qloS = alloc(st_p2, "qloS", [65, TS], BF16)

        bar = barrier()
        S(lambda e: e.dma_start(out=q2l65[:, :, 0:64],
                                in_=q2li.rearrange("h d l -> d h l")), waits=bar)
        q2w = P.now("dS", "dS2")
        for h in range(H):
            V((lambda h: lambda e: e.tensor_reduce(
                out=q2l65[:, h, 64:65], in_=q2l65[:, h, 0:64], axis=AX.X, op=ALU.add))(h),
              waits=q2w + bar)
        V(lambda e: e.tensor_tensor(out=q2l65lo[:].rearrange("p h l -> p (h l)"),
                                    in0=q2l65[:].rearrange("p h l -> p (h l)"),
                                    in1=hiv(q2l65[:].rearrange("p h l -> p (h l)")),
                                    op=ALU.subtract))
        V(lambda e: e.memset(qh65[:, :, :][64:65].rearrange("o h t -> o (h t)"), 0.0))

        PSQ = [(pA, 0), (pA, 512), (pB, 0), (pB, 512), (pC, 0), (pD, 0), (pE, 0),
               (pF, 0)]
        for grp in range(4):
            heads = list(range(4 * grp, 4 * grp + 4))
            P.new_epoch()
            sw = P.now("T", "V")
            done = {-2: sw, -1: sw}
            for c in range(NCH):
                buf = c % 2
                ld = S((lambda c, buf: lambda e: e.dma_start(
                    out=wqc[:, buf, :], in_=wqT[128 * c:128 * (c + 1), :]))(c, buf),
                    waits=done.get(c - 2, []))
                V((lambda buf, grp: lambda e: e.tensor_tensor(
                    out=wqlo[:, :], in0=wqc[:, buf, 512 * grp:512 * (grp + 1)],
                    in1=hiv(wqc[:, buf, 512 * grp:512 * (grp + 1)]),
                    op=ALU.subtract))(buf, grp), waits=[ld] + P.now("T"))
                vw = P.now("V")
                for hi_, h in enumerate(heads):
                    for qb in range(2):
                        ps, col = PSQ[hi_ * 2 + qb]
                        for pi in range(3):
                            wap = (hiv(wqc[:, buf, 128 * h:128 * (h + 1)]) if pi < 2
                                   else wqlo[:, 128 * hi_:128 * (hi_ + 1)])
                            xap = (xho_hi if pi != 1 else xho_lo)[
                                :, c, 512 * qb:512 * (qb + 1)]
                            st = P.first_touch((PSNAME[id(ps)], col // 512))
                            T((lambda ps, col, wap, xap, st: lambda e: e.matmul(
                                ps[:, col:col + 512], wap, xap, start=st, stop=False,
                                skip_group_check=True))(ps, col, wap, xap, st), waits=vw)
                done[c] = P.now("T", "V")
            for hi_, h in enumerate(heads):
                for qb in range(2):
                    ps, col = PSQ[hi_ * 2 + qb]
                    tw = P.now("T")
                    V((lambda ps, col: lambda e: e.tensor_copy(
                        out=qevh[:, :], in_=ps[:, col:col + 512]))(ps, col),
                      waits=tw)
                    V((lambda ps, col: lambda e: e.tensor_tensor(
                        out=qevl[:, :], in0=ps[:, col:col + 512], in1=qevh[:, :],
                        op=ALU.subtract))(ps, col))
                    vw = P.now("V")
                    for pi in range(3):
                        wap = (hiv(q2l65[:, h, :]) if pi < 2 else q2l65lo[:, h, :])
                        xap = qevh[:, :] if pi != 1 else qevl[:, :]
                        T((lambda ps, col, wap, xap, st: lambda e: e.matmul(
                            ps[0:65, col:col + 512], wap, xap, start=st, stop=False,
                            skip_group_check=True))(ps, col, wap, xap, pi == 0),
                          waits=vw)
                    tw = P.now("T")
                    V((lambda ps, col, qb: lambda e: e.tensor_copy(
                        out=qlatH[:, 512 * qb:512 * (qb + 1)],
                        in_=ps[0:65, col:col + 512]))(ps, col, qb), waits=tw)
                V((lambda h: lambda e: e.tensor_copy(
                    out=qh65[:, h, :][0:64], in_=hiv(qlatH[0:64, :])))(h))
                vwq = P.now("V")
                G((lambda h: lambda e: e.dma_start(
                    out=qhPK[64:128, h, :], in_=qhPK[0:64, h, :]))(h), waits=vwq)
                V((lambda h: lambda e: e.tensor_tensor(
                    out=ql65[:, h, :][0:64], in0=qlatH[0:64, :],
                    in1=qh65[:, h, :][0:64], op=ALU.subtract))(h))
                V(lambda e: e.tensor_tensor(
                    out=prodx[0:64, :], in0=qlatH[0:64, :], in1=konT[0:64, :],
                    op=ALU.mult))
                V(lambda e: e.tensor_tensor(
                    out=prodx[64:65, :], in0=qlatH[64:65, :], in1=konT[64:65, :],
                    op=ALU.mult))
                V(lambda e: e.tensor_scalar(
                    out=qpn[0:64, :], in0=qlatH[0:64, :], scalar1=0.0, scalar2=0.0,
                    op0=ALU.max, op1=ALU.add))
                V(lambda e: e.tensor_scalar(
                    out=qntmp[:, :], in0=qlatH[0:64, :], scalar1=0.0, scalar2=0.0,
                    op0=ALU.min, op1=ALU.add))
                V(lambda e: e.tensor_copy(out=qloS[64:65, :],
                                          in_=hiv(qlatH[64:65, :])))
                vw = P.now("V")
                G(lambda e: e.dma_start(out=qpn[64:128, :], in_=qntmp[:, :]), waits=vw)
                G((lambda h: lambda e: e.dma_start(
                    out=qsum3p[3 * h:3 * h + 1, :], in_=qloS[64:65, :]))(h))
                G((lambda h: lambda e: e.dma_start(
                    out=qsum3p[3 * h + 1:3 * h + 2, :], in_=qloS[64:65, :]))(h))
                ghw = P.now("dG", "dG2")
                V(lambda e: e.tensor_tensor(
                    out=qloS[64:65, :], in0=qlatH[64:65, :],
                    in1=hiv(qlatH[64:65, :]), op=ALU.subtract), waits=ghw)
                vw2 = P.now("V")
                G((lambda h: lambda e: e.dma_start(
                    out=qsum3p[3 * h + 2:3 * h + 3, :], in_=qloS[64:65, :]))(h),
                  waits=vw2)
                gw2 = P.now("dG", "dG2")
                ps0, col0 = PSQ[hi_ * 2]
                for qt in range(8):
                    T((lambda ps0, col0, qt, st: lambda e: e.matmul(
                        ps0[:, col0 + 256 + qt:col0 + 257 + qt],
                        prodx[:, 128 * qt:128 * (qt + 1)], m65[:, :],
                        start=st, stop=False, skip_group_check=True))(
                        ps0, col0, qt, qt == 0), waits=vw)
                    T((lambda ps0, col0, qt: lambda e: e.matmul(
                        ps0[:, col0 + 264 + qt:col0 + 265 + qt],
                        qlatH[64:65, 128 * qt:128 * (qt + 1)], ones128[64:65, :],
                        start=False, stop=False, skip_group_check=True))(ps0, col0, qt),
                      waits=vw)
                    T((lambda ps0, col0, qt: lambda e: e.matmul(
                        ps0[:, col0 + 32 * qt:col0 + 32 * (qt + 1)],
                        qpn[:, 128 * qt:128 * (qt + 1)], kstat[:, :],
                        start=False, stop=False, skip_group_check=True))(ps0, col0, qt),
                      waits=vw + gw2)
                tw = P.now("T")
                V((lambda ps0, col0, h: lambda e: e.tensor_reduce(
                    out=b128[:, 8 * h:8 * h + 8],
                    in_=ps0[:, col0:col0 + 256].rearrange("p (q b) -> p q b", b=32),
                    axis=AX.X, op=ALU.max))(ps0, col0, h), waits=tw)
                V((lambda ps0, col0, h: lambda e: e.tensor_copy(
                    out=d128[:, 8 * h:8 * h + 8],
                    in_=ps0[:, col0 + 256:col0 + 264]))(ps0, col0, h))
                V((lambda ps0, col0, h: lambda e: e.tensor_copy(
                    out=q128s[:, 8 * h:8 * h + 8],
                    in_=ps0[:, col0 + 264:col0 + 272]))(ps0, col0, h))
        flush()
        tap("qlat", qlatH[:, :], 65, TS)

        V(lambda e: e.tensor_scalar(out=b128[:, :], in0=b128[:, :], scalar1=1.0,
                                    scalar2=2.0 - PAD, op0=ALU.mult, op1=ALU.add))
        V(lambda e: e.tensor_tensor(out=s1s[:, :], in0=q128s[:, :], in1=wmx16[:, :],
                                    op=ALU.mult))
        V(lambda e: e.tensor_tensor(out=wmx16[:, :], in0=q128s[:, :], in1=wmn16[:, :],
                                    op=ALU.mult))
        V(lambda e: e.tensor_tensor(out=s1s[:, :], in0=s1s[:, :], in1=wmx16[:, :],
                                    op=ALU.max))
        V(lambda e: e.tensor_tensor(out=b128[:, :], in0=b128[:, :], in1=s1s[:, :],
                                    op=ALU.add))
        V(lambda e: e.tensor_tensor(out=b128[:, :], in0=b128[:, :], in1=d128[:, :],
                                    op=ALU.max))
        V(lambda e: e.tensor_scalar(out=b128[:, :], in0=b128[:, :], scalar1=-1.0,
                                    scalar2=0.0, op0=ALU.mult, op1=ALU.add))
        vw = P.now("V")
        T(lambda e: e.matmul(pC[:, 0:128], b128[:, :], idns[:, :], start=True,
                             stop=False, skip_group_check=True, is_transpose=True),
          waits=vw)
        tw = P.now("T")
        V(lambda e: e.tensor_copy(out=s1s[:, :], in_=pC[:, 0:128]), waits=tw)
        vw = P.now("V")
        G(lambda e: e.dma_start(
            out=ql65[:, :, :][64:65].rearrange("o h (q p) -> o (h q p)", p=128),
            in_=s1s[:, :]), waits=vw)
        flush()
        tap("S128", b128[:, :], 128, 128)
        flush()
        st_p2.close()
        P.set_barrier()

        expf = alloc(st_p3, "expf", [128, 2, TS], F32)
        expl = alloc(st_p3, "expl", [128, 2, TS], BF16)
        mtmp = alloc(st_p3, "mtmp", [128, 256], F32)
        qs3b = alloc(st_p3, "qs3b", [3, 2, TS], BF16)
        rbc = alloc(st_p3, "rbc", [64, TS], F32)
        cxs = alloc(st_p3, "cxs", [64, TS], F32)
        cxu = alloc(st_p3, "cxu", [65, TS], F32)

        bar = barrier()
        P.new_epoch()
        mask_v = {}
        exp_a = {}
        lo_v = {}
        av_t = {}
        epi_v = {}
        epi_free = {}
        ktlist = []
        for j in range(8):
            ktlist.append(("own", j))
            ktlist.append(("oth", j))
        qs3w_d = {}
        seg_d = {}

        def emit_qdma(h):
            hb = h % 2
            G((lambda h, hb: lambda e: e.dma_start(
                out=qs3b[:, hb, :], in_=qsum3p[3 * h:3 * h + 3, :]))(h, hb),
              waits=(bar if h < 2 else av_t[(h - 2) * 16 + 15]))
            qs3w_d[h] = P.now("dG", "dG2")

        def emit_lr(g):
            h, idx = g // 16, g % 16
            if idx == 0:
                emit_qdma(h)
            typ, j = ktlist[idx]
            pL = pA if idx % 2 == 0 else pB
            kt = j if typ == "own" else 8 + j
            lcol = 128 * kt
            sfx = 128 * j
            W = TS - sfx
            hb = h % 2
            lw = (bar if g == 0 else []) + \
                ((exp_a[g - 2] + mask_v[g - 2]) if g >= 2 else [])
            cuts = sorted({sfx, TS} | ({512} if sfx < 512 else set())
                          | ({sfx + 512} if sfx + 512 < TS else set()))
            segs = [(cuts[i], cuts[i + 1] - cuts[i]) for i in range(len(cuts) - 1)]
            seg_d[g] = (segs, sfx, kt, W)
            seen_banks = set()
            for q0, w in segs:
                bank = (q0 - sfx) // 512
                fresh = bank not in seen_banks
                seen_banks.add(bank)
                for pi in range(2):
                    wap = (khl[:, lcol:lcol + 128] if pi == 0
                           else kh65[:, lcol:lcol + 128])
                    xap = (qhPK if pi == 0 else ql65)[:, h, q0:q0 + w]
                    T((lambda pL, q0, w, sfx, wap, xap, st: lambda e: e.matmul(
                        pL[:, q0 - sfx:q0 - sfx + w], wap, xap, start=st,
                        stop=False, skip_group_check=True))(
                        pL, q0, w, sfx, wap, xap, pi == 0 and fresh),
                      waits=lw)
            pR = pC if idx % 2 == 0 else pD
            wB = 128 if typ == "own" else min(256, W)
            T((lambda pR, lcol, hb, sfx, wB: lambda e: e.matmul(
                pR[:, 0:wB], ksc3[:, lcol:lcol + 128],
                qs3b[:, hb, sfx:sfx + wB], start=True, stop=False,
                skip_group_check=True))(pR, lcol, hb, sfx, wB),
              waits=lw + qs3w_d[h] + (mask_v[g - 2] if g >= 2 else []))
            tw = P.now("T")
            mm = mDBm[:, 0:128] if typ == "own" else mOBm[:, 0:wB]
            V((lambda pR, wB, mm: lambda e: e.tensor_tensor(
                out=mtmp[:, 0:wB], in0=pR[:, 0:wB], in1=mm, op=ALU.mult))(
                pR, wB, mm), waits=tw)
            V((lambda pL, wB: lambda e: e.tensor_tensor(
                out=pL[:, 0:wB], in0=pL[:, 0:wB], in1=mtmp[:, 0:wB],
                op=ALU.add))(pL, wB))
            ma = mDadd if typ == "own" else mOadd
            V((lambda pL, ma: lambda e: e.tensor_tensor(
                out=pL[:, 0:128], in0=pL[:, 0:128], in1=ma[:, :], op=ALU.add))(
                pL, ma))
            mask_v[g] = P.now("V")
            eb = idx % 2
            A((lambda pL, eb, W: lambda e: e.activation(
                out=expf[:, eb, 0:W], in_=pL[:, 0:W], func=AF.Exp))(pL, eb, W),
              waits=mask_v[g] + (av_t[g - 2] if g >= 2 else []))
            exp_a[g] = [("A", P.cnt["A"])]
            V((lambda eb, W: lambda e: e.tensor_tensor(
                out=expl[:, eb, 0:W], in0=expf[:, eb, 0:W],
                in1=hiv(expf[:, eb, 0:W]), op=ALU.subtract))(eb, W),
              waits=exp_a[g])
            lo_v[g] = P.now("V")

        def emit_av(g):
            h, idx = g // 16, g % 16
            segs, sfx, kt, W = seg_d.pop(g)
            eb = idx % 2
            avw0 = (epi_free.get(h - 1, []) if idx == 0 else [])
            for pi in (0, 2, 1):
                for q0, w in segs:
                    ps = pE if q0 < 512 else pF
                    pcol = q0 - (0 if q0 < 512 else 512)
                    wap = (vtmh if pi < 2 else vtml)[:, kt, :]
                    xsrc = hiv(expf[:, eb, q0 - sfx:q0 - sfx + w]) if pi != 1 \
                        else expl[:, eb, q0 - sfx:q0 - sfx + w]
                    st = (idx == 0 and pi == 0)
                    T((lambda ps, pcol, w, wap, xsrc, st: lambda e: e.matmul(
                        ps[0:65, pcol:pcol + w], wap, xsrc, start=st, stop=False,
                        skip_group_check=True))(ps, pcol, w, wap, xsrc, st),
                      waits=(exp_a[g] if pi != 1 else lo_v[g]) + avw0)
            av_t[g] = [("T", P.cnt["T"])]

        def emit_epi(h):
            tw = P.now("T")
            A(lambda e: e.activation(out=cxu[64:65, 0:512], in_=pE[64:65, :],
                                     func=AF.Copy, bias=1e-38, scale=1.0),
              waits=tw)
            A(lambda e: e.activation(out=cxu[64:65, 512:1024], in_=pF[64:65, :],
                                     func=AF.Copy, bias=1e-38, scale=1.0))
            aw = P.now("A")
            V(lambda e: e.reciprocal(out=cxu[64:65, :], in_=cxu[64:65, :]),
              waits=aw)
            vw = P.now("V")
            T(lambda e: e.matmul(pE[64:128, :], onesP64[64:65, :],
                                 cxu[64:65, 0:512], start=True, stop=False,
                                 skip_group_check=True), waits=vw)
            T(lambda e: e.matmul(pF[64:128, :], onesP64[64:65, :],
                                 cxu[64:65, 512:1024], start=True, stop=False,
                                 skip_group_check=True))
            tw = P.now("T")
            V(lambda e: e.tensor_copy(out=rbc[:, 0:512], in_=pE[64:128, :]),
              waits=tw)
            V(lambda e: e.tensor_copy(out=rbc[:, 512:1024], in_=pF[64:128, :]))
            V((lambda h: lambda e: e.tensor_tensor(
                out=cxs[:, 0:512], in0=pE[0:64, :], in1=rbc[0:64, 0:512],
                op=ALU.mult))(h), waits=epi_v.get(h - 1, []))
            V(lambda e: e.tensor_tensor(
                out=cxs[:, 512:1024], in0=pF[0:64, :], in1=rbc[0:64, 512:1024],
                op=ALU.mult))
            epi_free[h] = P.now("V")
            V((lambda h: lambda e: e.tensor_copy(
                out=ctxh[:, h, :], in_=cxs[:, :]))(h))
            V((lambda h: lambda e: e.tensor_tensor(
                out=ctxl[:, h, :], in0=cxs[:, :], in1=ctxh[:, h, :],
                op=ALU.subtract))(h))
            epi_v[h] = P.now("V")
            G((lambda h: lambda e: e.dma_start(
                out=ctxPK1[64:128, h, :], in_=ctxPK1[0:64, h, :]))(h), waits=epi_v[h])
            if h % 2 == 1:
                G((lambda h: lambda e: e.dma_start(
                    out=ctxPK2[64:128, h - 1, :], in_=ctxPK2[0:64, h, :]))(h))

        for gg in range(16 * H + 2):
            if gg >= 2:
                emit_av(gg - 2)
            if gg < 16 * H:
                emit_lr(gg)
            if gg >= 2 and (gg - 2) % 16 == 15:
                emit_epi((gg - 2) // 16)
        P.new_epoch()
        flush()
        tap("ctxf", ctxh[0:64, 0, :], 64, TS)
        flush()
        st_p3.close()
        st_kv.close()
        P.set_barrier()

        woc = alloc(st_p4, "woc", [128, 16, 128], F32)
        woc2 = alloc(st_p4, "woc2", [128, 16, 128], F32)
        woclo = alloc(st_p4, "woclo", [128, 16, 128], BF16)
        woclo2 = alloc(st_p4, "woclo2", [128, 16, 128], BF16)
        vupS = alloc(st_p4, "vupS", [128, H, L], F32)
        vupSlo = alloc(st_p4, "vupSlo", [128, H, L], BF16)
        wvoPK = alloc(st_p4, "wvoPK", [128, 2, H, 128], BF16)
        wvoPR = alloc(st_p4, "wvoPR", [128, 2, 8, 128], BF16)
        wvol = alloc(st_p4, "wvol", [64, 2, H, 128], BF16)
        xoc2 = alloc(st_p4, "xoc2", [128, 2, TS], F32)

        bar = barrier()
        S(lambda e: e.dma_start(out=vupS[:, :, :],
                                in_=vupTi.rearrange("h d l -> d h l")), waits=bar)
        vw0 = P.now("dS", "dS2")
        V(lambda e: e.tensor_tensor(out=vupSlo[:].rearrange("p h l -> p (h l)"),
                                    in0=vupS[:].rearrange("p h l -> p (h l)"),
                                    in1=hiv(vupS[:].rearrange("p h l -> p (h l)")),
                                    op=ALU.subtract), waits=vw0 + bar)

        wocB = [woc, woc2]
        wocloB = [woclo, woclo2]
        wocw = {}
        bT = {}
        aT = {}
        dwD = {}
        xvD = {}

        def p4_dma(eo):
            ld = S((lambda eo: lambda e: e.dma_start(
                out=wocB[eo % 2][:, :, :],
                in_=woT[:, 128 * eo:128 * (eo + 1)].rearrange("(c p) m -> p c m",
                                                              p=128)))(eo),
                waits=bT.get(eo - 2, []) + (bar if eo < 2 else []))
            wocw[eo] = [ld]

        def p4_lo(eo):
            b = eo % 2
            V((lambda b: lambda e: e.tensor_tensor(
                out=wocloB[b][:].rearrange("p c m -> p (c m)"),
                in0=wocB[b][:].rearrange("p c m -> p (c m)"),
                in1=hiv(wocB[b][:].rearrange("p c m -> p (c m)")),
                op=ALU.subtract))(b), waits=wocw[eo] + bT.get(eo - 2, []))

        def p4_build(eo):
            b = eo % 2
            for h in range(H):
                vw = P.now("V")
                ps, col = (pC, 128 * (h % 4)) if h % 8 < 4 else (pD, 128 * (h % 8 - 4))
                for pi in range(3):
                    wap = (hiv(vupS[:, h, :]) if pi < 2 else vupSlo[:, h, :])
                    xap = (hiv(wocB[b][:, h, :]) if pi != 1
                           else wocloB[b][:, h, :])
                    st = (pi == 0 and h % 4 == 0)
                    T((lambda ps, col, wap, xap, st: lambda e: e.matmul(
                        ps[0:64, col:col + 128], wap, xap, start=st, stop=False,
                        skip_group_check=True))(ps, col, wap, xap, st), waits=vw)
                if h % 8 == 7:
                    tw = P.now("T")
                    for ps2, h0 in ((pC, h - 7), (pD, h - 3)):
                        V((lambda b, ps2, h0: lambda e: e.tensor_copy(
                            out=wvoPK[0:64, b, h0:h0 + 4, :],
                            in_=ps2[0:64, 0:512]))(b, ps2, h0),
                          waits=tw + dwD.get(eo - 2, []))
                        V((lambda b, ps2, h0: lambda e: e.tensor_tensor(
                            out=wvol[:, b, h0:h0 + 4, :],
                            in0=ps2[0:64, 0:512].rearrange("p (a m) -> p a m", a=4),
                            in1=wvoPK[0:64, b, h0:h0 + 4, :],
                            op=ALU.subtract))(b, ps2, h0))
            bT[eo] = [("T", P.cnt["T"])]

        def p4_pack(eo):
            b = eo % 2
            vw = P.now("V")
            V((lambda b: lambda e: e.tensor_copy(
                out=wvoPR[0:64, b, :, :],
                in_=wvoPK[0:64, b, :, :][:, 0::2, :]))(b),
              waits=vw + aT.get(eo - 2, []))
            vw = P.now("V")
            dlo = G((lambda b: lambda e: e.dma_start(
                out=wvoPK[64:128, b, :, :],
                in_=wvol[:, b, :, :]))(b),
                waits=vw + aT.get(eo - 2, []))
            dhi = G((lambda b: lambda e: e.dma_start(
                out=wvoPR[64:128, b, :, :],
                in_=wvoPK[0:64, b, :, :][:, 1::2, :]))(b))
            dwD[eo] = [dlo, dhi]

        def p4_apply(eo):
            b = eo % 2
            ld = S((lambda eo: lambda e: e.dma_start(
                out=xoc2[:, eo % 2, :], in_=xoT[128 * eo:128 * (eo + 1), :]))(eo),
                waits=xvD.get(eo - 2, []))
            for qb in range(2):
                ps = pA if qb == 0 else pB
                for h in range(H):
                    st = (h == 0)
                    T((lambda b, ps, h, qb, st: lambda e: e.matmul(
                        ps[:, 0:512], wvoPK[:, b, h, :],
                        ctxPK1[:, h, 512 * qb:512 * (qb + 1)], start=st, stop=False,
                        skip_group_check=True))(b, ps, h, qb, st),
                      waits=dwD[eo] + (xvD.get(eo - 1, []) if st else []))
                for he in range(8):
                    T((lambda b, ps, he, qb: lambda e: e.matmul(
                        ps[:, 0:512], wvoPR[:, b, he, :],
                        ctxPK2[:, 2 * he, 512 * qb:512 * (qb + 1)], start=False,
                        stop=False, skip_group_check=True))(b, ps, he, qb),
                      waits=dwD[eo])
            aT[eo] = [("T", P.cnt["T"])]
            tw = P.now("T")
            V((lambda eo: lambda e: e.tensor_tensor(
                out=xp[:, eo, 0:512], in0=pA[:, 0:512], in1=xoc2[:, eo % 2, 0:512],
                op=ALU.add))(eo), waits=tw + [ld])
            V((lambda eo: lambda e: e.tensor_tensor(
                out=xp[:, eo, 512:1024], in0=pB[:, 0:512],
                in1=xoc2[:, eo % 2, 512:1024], op=ALU.add))(eo))
            xvD[eo] = P.now("V")

        p4_dma(0)
        p4_lo(0)
        p4_build(0)
        p4_pack(0)
        p4_dma(1)
        p4_lo(1)
        for eo in range(NCH):
            if eo + 1 < NCH:
                p4_build(eo + 1)
                p4_pack(eo + 1)
            p4_apply(eo)
            if eo + 2 < NCH:
                p4_dma(eo + 2)
                p4_lo(eo + 2)
        P.new_epoch()
        flush()
        tap("xp", xp[:, 0, :], 128, TS)
        flush()
        st_p4.close()
        P.set_barrier()

        x8 = alloc(st_p5, "x8", [128, NCH, TS], FP8)
        mrow = alloc(st_p5, "mrow5", [1, TS], F32)
        rrow = alloc(st_p5, "rrow5", [1, TS], F32)
        scr1 = srow[0:1, :]
        scr2 = alloc(st_p5, "scr2_5", [1, TS], F32)
        w0c = alloc(st_p5, "w0c", [128, 2, 8, 2, 128], FP8)
        wst = alloc(st_p5, "wst", [128, 2, 8, 2, 128], F32)
        w1c = alloc(st_p5, "w1c", [128, 2, 8, 2, 128], FP8)
        wlc = alloc(st_p5, "wlc", [128, 2, 16, 2, 128], FP8)
        gb = alloc(st_p5, "gb", [128, TS], BF16)
        f1b = AR[:, 131072:133120].bitcast(BF16)

        bar = barrier()
        P.new_epoch()
        for c in range(NCH):
            cw = bar if c == 0 else P.now("T")
            A((lambda c: lambda e: e.activation(
                out=gb[:], in_=xp[:, c, :], func=AF.Square))(c),
              waits=cw)
            for qb in range(2):
                T((lambda c, qb, st: lambda e: e.matmul(
                    pA[0:1, 512 * qb:512 * (qb + 1)], ones128b[:],
                    hiv(xp[:, c, 512 * qb:512 * (qb + 1)]),
                    start=st, stop=False, skip_group_check=True))(
                    c, qb, P.first_touch(("A", qb))), waits=bar if c == 0 else [])
            aw = P.now("A")
            for qb in range(2):
                T((lambda c, qb, st: lambda e: e.matmul(
                    pA[32:33, 512 * qb:512 * (qb + 1)], ones128b[:],
                    gb[:, 512 * qb:512 * (qb + 1)], start=st, stop=False,
                    skip_group_check=True))(c, qb, P.first_touch(("A32", qb))),
                  waits=aw)
        tw = P.now("T")
        V(lambda e: e.tensor_scalar(out=mrow[:, :], in0=pA[0:1, :], scalar1=1.0 / E,
                                    scalar2=0.0, op0=ALU.mult, op1=ALU.add), waits=tw)
        V(lambda e: e.tensor_scalar(out=rrow[:, :], in0=pA[32:33, :], scalar1=1.0 / E,
                                    scalar2=1e-5, op0=ALU.mult, op1=ALU.add))
        V(lambda e: e.scalar_tensor_tensor(out=scr1[:, :], in0=mrow[:, :], scalar=-1.0,
                                           in1=mrow[:, :], op0=ALU.mult, op1=ALU.mult))
        V(lambda e: e.tensor_tensor(out=rrow[:, :], in0=rrow[:, :], in1=scr1[:, :],
                                    op=ALU.add))
        V(lambda e: e.reciprocal(out=scr1[:, :], in_=rrow[:, :]))
        vw = P.now("V")
        A(lambda e: e.activation(out=scr1[:, :], in_=scr1[:, :], func=AF.Sqrt),
          waits=vw)
        aw = P.now("A")
        V(lambda e: e.tensor_tensor(out=scr2[:, :], in0=scr1[:, :], in1=scr1[:, :],
                                    op=ALU.mult), waits=aw)
        V(lambda e: e.tensor_tensor(out=scr2[:, :], in0=scr2[:, :], in1=rrow[:, :],
                                    op=ALU.mult))
        V(lambda e: e.tensor_scalar(out=scr2[:, :], in0=scr2[:, :], scalar1=-0.5,
                                    scalar2=1.5, op0=ALU.mult, op1=ALU.add))
        V(lambda e: e.tensor_tensor(out=rrow[:, :], in0=scr1[:, :], in1=scr2[:, :],
                                    op=ALU.mult))
        P.new_epoch()
        vw = P.now("V")
        for srcrow, dst in ((mrow, bcM), (rrow, bcR)):
            for qb in range(2):
                T((lambda srcrow, qb: lambda e: e.matmul(
                    pB[:, 512 * qb:512 * (qb + 1)], onesrow[:, :],
                    srcrow[:, 512 * qb:512 * (qb + 1)], start=True, stop=False,
                    skip_group_check=True))(srcrow, qb), waits=vw)
            tw = P.now("T")
            V((lambda dst: lambda e: e.tensor_copy(out=dst[:], in_=pB[:]))(dst), waits=tw)
            vw = P.now("V")
        for c in range(NCH):
            V((lambda c: lambda e: e.tensor_tensor(
                out=pB[:, :], in0=xp[:, c, :], in1=bcM[:], op=ALU.subtract))(c))
            V((lambda c: lambda e: e.tensor_tensor(
                out=x8[:, c, :], in0=pB[:, :], in1=bcR[:], op=ALU.mult))(c))
        flush()
        tap("x8", x8[:, 0, :].bitcast(BF16)[:, 0:512].bitcast(F32)[:, 0:128], 128, 128)

        P.new_epoch()
        sw = P.now("T", "V")
        done = {-2: sw, -1: sw}
        gelu_done = {}
        mm_done = {}
        castA = {}
        ldD = {}

        def g_dma(hc):
            buf = hc % 2
            l0 = S((lambda hc, buf: lambda e: e.dma_start(
                out=wst[:, buf, :, :, :],
                in_=wg[0, :, 128 * hc:128 * (hc + 1)].rearrange(
                    "(k j p) m -> p k j m", j=2, p=128)))(hc, buf),
                waits=castA.get(hc - 2, []))
            l1 = G((lambda hc, buf: lambda e: e.dma_start(
                out=w1c[:, buf, :, :, :],
                in_=wg[1, :, 128 * hc:128 * (hc + 1)].rearrange(
                    "(k j p) m -> p k j m", j=2, p=128)))(hc, buf),
                waits=mm_done.get(hc - 2, []))
            ldD[hc] = [l0, l1]
            A((lambda buf: lambda e: e.activation(
                out=w0c[:, buf, :, :, :].rearrange("p k j m -> p (k j m)"),
                in_=wst[:, buf, :, :, :].rearrange("p k j m -> p (k j m)"),
                func=AF.Copy))(buf),
              waits=[l0] + mm_done.get(hc - 2, []))
            castA[hc] = [("A", P.cnt["A"])]

        g_dma(0)
        g_dma(1)
        for hc in range(64):
            buf = hc % 2
            lw = [ldD[hc][1]] + castA[hc] + \
                gelu_done.get(hc - 2, (P.now("V") if hc < 2 else []))
            if buf == 0:
                psg = [(pA, 0), (pA, 512)]
                psf = [(pB, 0), (pB, 512)]
            else:
                psg = [(pC, 0), (pD, 0)]
                psf = [(pE, 0), (pF, 0)]
            for qq in range(4):
                psG, colG = psg[qq // 2]
                psF, colF = psf[qq // 2]
                cg = colG + 256 * (qq % 2)
                cf = colF + 256 * (qq % 2)
                for kc in range(8):
                    T((lambda buf, qq, kc, psG, cg, st: lambda e: e.matmul(
                        psG[:, cg:cg + 256], w0c[:, buf, kc, :, :],
                        x8[:, 2 * kc:2 * kc + 2, 256 * qq:256 * (qq + 1)],
                        start=st, stop=False, skip_group_check=True,
                        perf_mode=PM.DoubleRow))(buf, qq, kc, psG, cg,
                                                 (kc == 0 and qq % 2 == 0)), waits=lw)
                    T((lambda buf, qq, kc, psF, cf, st: lambda e: e.matmul(
                        psF[:, cf:cf + 256], w1c[:, buf, kc, :, :],
                        x8[:, 2 * kc:2 * kc + 2, 256 * qq:256 * (qq + 1)],
                        start=st, stop=False, skip_group_check=True,
                        perf_mode=PM.DoubleRow))(buf, qq, kc, psF, cf,
                                                 (kc == 0 and qq % 2 == 0)), waits=lw)
            mm_done[hc] = P.now("T")
            if hc + 2 < 64:
                g_dma(hc + 2)
            tw = mm_done[hc]
            gsrc = [(psg[0][0], psg[0][1]), (psg[1][0], psg[1][1])]
            fsrc = [(psf[0][0], psf[0][1]), (psf[1][0], psf[1][1])]
            V((lambda gsrc: lambda e: e.tensor_copy(
                out=gb[:, 0:512], in_=gsrc[0][0][:, gsrc[0][1]:gsrc[0][1] + 512]))(gsrc),
              waits=tw)
            V((lambda gsrc: lambda e: e.tensor_copy(
                out=gb[:, 512:1024], in_=gsrc[1][0][:, gsrc[1][1]:gsrc[1][1] + 512]))(gsrc))
            V((lambda fsrc: lambda e: e.tensor_copy(
                out=f1b[:, 0:512], in_=fsrc[0][0][:, fsrc[0][1]:fsrc[0][1] + 512]))(fsrc))
            V((lambda fsrc: lambda e: e.tensor_copy(
                out=f1b[:, 512:1024], in_=fsrc[1][0][:, fsrc[1][1]:fsrc[1][1] + 512]))(fsrc))
            vw = P.now("V")
            A(lambda e: e.activation(out=gb[:], in_=gb[:], func=AF.Gelu_apprx_tanh),
              waits=vw)
            aw = P.now("A")
            V(lambda e: e.tensor_tensor(out=gb[:], in0=gb[:], in1=f1b[:],
                                        op=ALU.mult), waits=aw)
            V((lambda hc: lambda e: e.tensor_copy(out=act8[:, hc, :], in_=gb[:]))(hc))
            gelu_done[hc] = P.now("V")
            done[hc] = P.now("T", "V")
        flush()

        outsW = x8[:, 0:4, :].rearrange("p a b -> p (a b)").bitcast(F32)
        P.new_epoch()
        sw = P.now("T", "V")
        done = {-2: sw, -1: sw}
        outdma = {}
        castL = {}
        ldL = {}
        fcV = {}

        def wl_dma(eo):
            buf = eo % 2
            w = done.get(eo - 2, [])
            ld1 = G((lambda eo, buf: lambda e: e.dma_start(
                out=wlc[:, buf, 0:16, :, :],
                in_=wlin[0:4096, 128 * eo:128 * (eo + 1)].rearrange(
                    "(k j p) m -> p k j m", j=2, p=128)))(eo, buf), waits=w)
            ld2 = G((lambda eo, buf: lambda e: e.dma_start(
                out=w0c[:, buf, :, :, :],
                in_=wlin[4096:6144, 128 * eo:128 * (eo + 1)].rearrange(
                    "(k j p) m -> p k j m", j=2, p=128)))(eo, buf))
            lr = S((lambda eo, buf: lambda e: e.dma_start(
                out=wst[:, buf, :, :, :],
                in_=wlin[6144:8192, 128 * eo:128 * (eo + 1)].rearrange(
                    "(k j p) m -> p k j m", j=2, p=128)))(eo, buf),
                waits=castL.get(eo - 2, castA.get(62 + eo, [])))
            ldL[eo] = [ld1, ld2]
            A((lambda buf: lambda e: e.activation(
                out=w1c[:, buf, :, :, :].rearrange("p k j m -> p (k j m)"),
                in_=wst[:, buf, :, :, :].rearrange("p k j m -> p (k j m)"),
                func=AF.Copy))(buf),
              waits=[lr] + w)
            castL[eo] = [("A", P.cnt["A"])]

        wl_dma(0)
        for eo in range(NCH):
            buf = eo % 2
            if eo + 1 < NCH:
                wl_dma(eo + 1)
            lw = ldL[eo] + castL[eo] + fcV.get(eo - 2, [])
            ps = pC if eo % 2 == 0 else pD
            for qq in range(4):
                w_mm = lw + (P.now("V") if qq >= 2 else [])
                for kc in range(32):
                    st = P.first_touch((PSNAME[id(ps)], 0)) if (qq % 2 == 0 and kc == 0) \
                        else False
                    wsrc = (wlc[:, buf, kc, :, :] if kc < 16 else
                            (w0c if kc < 24 else w1c)[:, buf, kc % 8, :, :])
                    T((lambda ps, buf, qq, kc, st, wsrc: lambda e: e.matmul(
                        ps[:, 256 * (qq % 2):256 * (qq % 2) + 256],
                        wsrc,
                        act8[:, 2 * kc:2 * kc + 2, 256 * qq:256 * (qq + 1)],
                        start=st, stop=False, skip_group_check=True,
                        perf_mode=PM.DoubleRow))(ps, buf, qq, kc, st, wsrc),
                      waits=w_mm)
                if qq % 2 == 1:
                    tw = P.now("T")
                    hb = qq // 2
                    ow = [outdma[eo - 1]] if eo - 1 in outdma else []
                    V((lambda ps: lambda e: e.tensor_copy(
                        out=f1b[:, 0:512], in_=ps[:, 0:512]))(ps), waits=tw)
                    V((lambda eo, hb: lambda e: e.tensor_tensor(
                        out=outsW[:, 512 * hb:512 * (hb + 1)],
                        in0=xp[:, eo, 512 * hb:512 * (hb + 1)], in1=f1b[:, 0:512],
                        op=ALU.add))(eo, hb), waits=ow)
                    P.new_epoch()
            vw = P.now("V")
            fcV[eo] = vw
            done[eo] = P.now("T")
            outdma[eo] = S((lambda eo: lambda e: e.dma_start(
                out=outT[128 * eo:128 * (eo + 1), :], in_=outsW[:, :]))(eo),
                waits=vw)
        fin = P.now("dS", "dS2")
        P.emit("gpsimd", lambda e: e.memset(scr2[0:1, 0:1], 0.0), fin, None)
        flush()

        for st in (st_p5, st_per):
            st.close()
    return nc


_MASK_CACHE = {}


def _masks(pi):
    if pi in _MASK_CACHE:
        return _MASK_CACHE[pi]
    s = np.arange(128)[:, None]
    q = np.arange(128)[None, :]
    mD_add = np.where(s > q, np.float32(-1e30), np.float32(0.0))
    mDB = ((q - s >= 0) & (q - s <= 64)).astype(np.float32)
    sub = (s - q >= 64).astype(np.float32)
    if pi == 0:
        mO_add = np.full((128, 128), -1e30, np.float32)
        mOB = np.concatenate([np.zeros((128, 128), np.float32), sub], axis=1)
    else:
        mO_add = np.zeros((128, 128), np.float32)
        mOB = np.concatenate([sub, np.zeros((128, 128), np.float32)], axis=1)
    out = (mD_add, mO_add, mDB.astype(np.float32), mOB)
    _MASK_CACHE[pi] = out
    return out


_PROG = None


def kernel(L0_attn_hidden_states, L0_attn_Wq, L0_attn_Wk, L0_attn_Wv,
           L0_attn_q_to_latent, L0_attn_v_up, L0_attn_Wo, L0_ffn_w_gating,
           L0_ffn_w_linear):
    global _PROG
    x = np.ascontiguousarray(L0_attn_hidden_states, dtype=np.float32)
    wq = np.ascontiguousarray(L0_attn_Wq, dtype=np.float32)
    wqT = np.ascontiguousarray(wq.T)
    wk = np.ascontiguousarray(L0_attn_Wk, dtype=np.float32)
    wv = np.ascontiguousarray(L0_attn_Wv, dtype=np.float32)
    wkT = np.ascontiguousarray(wk.T)
    wvT = np.ascontiguousarray(wv.T)
    q2l = np.ascontiguousarray(L0_attn_q_to_latent, dtype=np.float32)
    vupT = np.ascontiguousarray(np.transpose(L0_attn_v_up, (0, 2, 1)))
    woT = np.ascontiguousarray(np.asarray(L0_attn_Wo, dtype=np.float32).T)
    wg = np.ascontiguousarray(L0_ffn_w_gating, dtype=np.float32)
    wlin = np.ascontiguousarray(L0_ffn_w_linear, dtype=np.float32)
    idn = np.eye(128, dtype=np.float32)

    if _PROG is None:
        _PROG = build_program()
    nc = _PROG

    in_maps = []
    for c in range(8):
        b, pi = c // 2, c % 2
        own = np.concatenate([np.arange(128 * (2 * i + pi), 128 * (2 * i + pi) + 128)
                              for i in range(8)])
        oth = np.concatenate([np.arange(128 * (2 * i + 1 - pi),
                                        128 * (2 * i + 1 - pi) + 128)
                              for i in range(8)])
        xoT = np.ascontiguousarray(x[b][own].T)
        xrT = np.ascontiguousarray(x[b][oth].T)
        mD_add, mO_add, mDB, mOB = _masks(pi)
        in_maps.append({
            "xoT": xoT, "xrT": xrT, "wqT": wqT, "q2l": q2l, "wkT": wkT, "wvT": wvT,
            "vupT": vupT, "woT": woT, "wg": wg, "wlin": wlin, "idn": idn,
            "mD_add": mD_add, "mO_add": mO_add, "mDB_mult": mDB, "mOB_mult": mOB,
        })
    res = run_bass_kernel_spmd(nc, in_maps, core_ids=list(range(8)))
    out = np.empty((4, 2048, 2048), dtype=np.float32)
    for c in range(8):
        b, pi = c // 2, c % 2
        own = np.concatenate([np.arange(128 * (2 * i + pi), 128 * (2 * i + pi) + 128)
                              for i in range(8)])
        out[b][own] = res.results[c]["outT"].T
    return out


# version 11
# speedup vs baseline: 1.0559x; 1.0559x over previous
import numpy as np
import contextlib
import concourse.bass as bass
import concourse.mybir as mybir
from concourse.bass_utils import run_bass_kernel_spmd

F32 = mybir.dt.float32
F32R = mybir.dt.float32r
BF16 = mybir.dt.bfloat16
FP8 = mybir.dt.float8e4
AF = mybir.ActivationFunctionType
ALU = mybir.AluOpType
PM = mybir.MatmulPerfMode
AX = mybir.AxisListType

E = 2048
TS = 1024
H, L = 16, 64
HID = 8192
NCH = 16
INV = float(1.0 / np.sqrt(L))
RAH = float(0.5 / np.sqrt(L))
PAD = 78.0

ENGS = ("sync", "gpsimd", "tensor", "vector", "scalar")


class Plan:
    def __init__(self):
        self.pend = {e: [] for e in ENGS}
        self.cnt = dict(dS=0, dS2=0, dG=0, dG2=0, T=0, V=0, A=0, PL=0)
        self.rr = {"sync": 0, "gpsimd": 0}
        self.bank_started = {}
        self.barrier_waits = []
        self.barrier_pending = {e: False for e in ENGS}

    def now(self, *sems):
        return [(s, self.cnt[s]) for s in sems]

    def set_barrier(self):
        self.barrier_waits = [(s, self.cnt[s]) for s in self.cnt]
        self.barrier_pending = {e: True for e in ENGS}

    def emit(self, eng, fn, waits=(), inc=None, amt=None):
        if amt is None:
            amt = 16 if inc in ("dS", "dG", "dS2", "dG2") else 1
        w = list(waits)
        if inc in ("dS", "dG", "dS2", "dG2"):
            w = [(inc, self.cnt[inc])] + w
        if eng == "vector":
            w = [("V", self.cnt["V"])] + w
        elif eng == "scalar":
            w = [("A", self.cnt["A"])] + w
        elif eng == "gpsimd" and inc == "PL":
            w = [("PL", self.cnt["PL"])] + w
        if self.barrier_pending[eng]:
            w = self.barrier_waits + w
            self.barrier_pending[eng] = False
        self.pend[eng].append((fn, w, inc, amt))
        if inc is not None:
            self.cnt[inc] += amt
        return self.cnt.get(inc)

    def first_touch(self, bank_key):
        if bank_key in self.bank_started:
            return False
        self.bank_started[bank_key] = True
        return True

    def new_epoch(self):
        self.bank_started = {}


def build_program(debug=()):
    nc = bass.Bass("TRN2", target_bir_lowering=False, debug=False)
    dt = nc.dram_tensor
    xoT = dt("xoT", [E, TS], F32, kind="ExternalInput").ap()
    xrT = dt("xrT", [E, TS], F32, kind="ExternalInput").ap()
    wqT = dt("wqT", [E, E], F32, kind="ExternalInput").ap()
    q2li = dt("q2l", [H, 128, L], F32, kind="ExternalInput").ap()
    wkT = dt("wkT", [E, L], F32, kind="ExternalInput").ap()
    wvT = dt("wvT", [E, L], F32, kind="ExternalInput").ap()
    vupTi = dt("vupT", [H, 128, L], F32, kind="ExternalInput").ap()
    woT = dt("woT", [E, E], F32, kind="ExternalInput").ap()
    wg = dt("wg", [2, E, HID], F32, kind="ExternalInput").ap()
    wlin = dt("wlin", [HID, E], F32, kind="ExternalInput").ap()
    idn = dt("idn", [128, 128], F32, kind="ExternalInput").ap()
    mD_add = dt("mD_add", [128, 128], F32, kind="ExternalInput").ap()
    mO_add = dt("mO_add", [128, 128], F32, kind="ExternalInput").ap()
    mDB_mult = dt("mDB_mult", [128, 128], F32, kind="ExternalInput").ap()
    mOB_mult = dt("mOB_mult", [128, 256], F32, kind="ExternalInput").ap()
    outT = dt("outT", [E, TS], F32, kind="ExternalOutput").ap()
    dbg_out = {}
    for name, shape in debug:
        dbg_out[name] = dt("dbg_" + name, shape, F32, kind="ExternalOutput").ap()

    P = Plan()
    st_per = contextlib.ExitStack()
    st_kv = contextlib.ExitStack()
    st_p1 = contextlib.ExitStack()
    st_p2 = contextlib.ExitStack()
    st_p3 = contextlib.ExitStack()
    st_p4 = contextlib.ExitStack()
    st_p5 = contextlib.ExitStack()

    def hiv(ap_f32):
        return ap_f32.bitcast(BF16)[..., 1::2]

    with (
        nc.psum_tensor("pA", [128, 1024], F32) as pA,
        nc.psum_tensor("pB", [128, 1024], F32) as pB,
        nc.psum_tensor("pC", [128, 512], F32) as pC,
        nc.psum_tensor("pD", [128, 512], F32) as pD,
        nc.psum_tensor("pE", [128, 512], F32) as pE,
        nc.psum_tensor("pF", [128, 512], F32) as pF,
        nc.semaphore("dS") as sem_dS,
        nc.semaphore("dS2") as sem_dS2,
        nc.semaphore("dG") as sem_dG,
        nc.semaphore("dG2") as sem_dG2,
        nc.semaphore("T") as sem_T,
        nc.semaphore("V") as sem_V,
        nc.semaphore("A") as sem_A,
        nc.semaphore("PL") as sem_PL,
        nc.Block() as block,
    ):
        sems = {"dS": sem_dS, "dS2": sem_dS2, "dG": sem_dG, "dG2": sem_dG2,
                "T": sem_T, "V": sem_V, "A": sem_A, "PL": sem_PL}
        wms = {e: {} for e in ENGS}
        PSNAME = {id(pA): "A", id(pB): "B", id(pC): "C", id(pD): "D",
                  id(pE): "E", id(pF): "F"}

        def S(fn, waits=()):
            P.rr["sync"] ^= 1
            sem = "dS" if P.rr["sync"] else "dS2"
            return (sem, P.emit("sync", fn, waits, sem))

        def G(fn, waits=()):
            P.rr["gpsimd"] ^= 1
            sem = "dG" if P.rr["gpsimd"] else "dG2"
            return (sem, P.emit("gpsimd", fn, waits, sem))

        def T(fn, waits=()):
            return P.emit("tensor", fn, waits, "T")

        def GC(fn, waits=()):
            return P.emit("gpsimd", fn, waits, "PL")

        def V(fn, waits=()):
            return P.emit("vector", fn, waits, "V")

        def A(fn, waits=()):
            return P.emit("scalar", fn, waits, "A")

        def flush():
            for name in ENGS:
                ops = P.pend[name]
                if not ops:
                    continue
                P.pend[name] = []
                wm = wms[name]

                def body(eng, ops=ops, wm=wm):
                    for fn, waits, inc, amt in ops:
                        for s, v in waits:
                            if v > 0 and wm.get(s, 0) < v:
                                eng.wait_ge(sems[s], v)
                                wm[s] = v
                        ins = fn(eng)
                        if inc is not None:
                            ins.then_inc(sems[inc], amt)
                getattr(block, name)(body)

        def barrier():
            return P.now("dS", "dS2", "dG", "dG2", "T", "V", "A", "PL")

        def tap(name, src_ap, rows, cols):
            if name in dbg_out:
                w = barrier()
                G((lambda name, src_ap, rows, cols: lambda e: e.dma_start(
                    out=dbg_out[name][0:rows, 0:cols], in_=src_ap))(name, src_ap, rows, cols),
                  waits=w)

        def alloc(st, name, shape, dtype):
            return st.enter_context(nc.sbuf_tensor(name, shape, dtype))

        ones128 = alloc(st_per, "ones128", [128, 1], F32)
        ones128b = alloc(st_per, "ones128b", [128, 1], BF16)
        onesrow = alloc(st_per, "onesrow", [1, 128], F32)
        m65 = alloc(st_per, "m65", [65, 1], BF16)
        mDadd = alloc(st_per, "mDadd", [128, 128], BF16)
        mOadd = alloc(st_per, "mOadd", [128, 128], BF16)
        mDBm = alloc(st_per, "mDBm", [128, 128], BF16)
        mOBm = alloc(st_per, "mOBm", [128, 256], BF16)
        bcM = alloc(st_per, "bcM", [128, TS], F32)
        bcR = alloc(st_per, "bcR", [128, TS], F32)
        srow = alloc(st_per, "srow", [16, TS], F32)
        d128 = alloc(st_per, "d128", [128, 128], F32)
        q128s = alloc(st_per, "q128s", [128, 128], F32)
        b128 = alloc(st_per, "b128", [128, 128], F32)
        s1s = alloc(st_per, "s1s", [128, 128], F32)
        AR = alloc(st_per, "AR", [128, 133120], mybir.dt.uint8)
        xho_hi = AR[:, 0:32768].bitcast(BF16).rearrange("p (c t) -> p c t", c=NCH)
        xho_lo = AR[:, 32768:65536].bitcast(BF16).rearrange("p (c t) -> p c t", c=NCH)
        ctxPK1 = AR[:, 0:32768].bitcast(BF16).rearrange("p (c t) -> p c t", c=H)
        ctxPK2 = AR[:, 32768:65536].bitcast(BF16).rearrange("p (c t) -> p c t", c=H)
        ctxh = ctxPK1[0:64]
        ctxl = ctxPK2[0:64]
        act8 = AR[:, 0:65536].bitcast(FP8).rearrange("p (c t) -> p c t", c=64)
        qhPK = AR[:, 65536:98304].bitcast(BF16).rearrange("p (c t) -> p c t", c=H)
        qh65 = qhPK[0:65]
        ql65 = AR[:, 98304:131072].bitcast(BF16).rearrange("p (c t) -> p c t", c=H)[0:65]
        qsum3p = AR[:, 131072:133120].bitcast(BF16)[0:48]
        xp = AR[:, 65536:131072].bitcast(F32).rearrange("p (c t) -> p c t", c=NCH)
        kh65 = alloc(st_kv, "kh65", [65, 2 * TS], BF16)
        khl = alloc(st_kv, "khl", [128, 2 * TS], BF16)
        onesP64 = alloc(st_kv, "onesP64", [65, 64], F32)
        vtmh = alloc(st_kv, "vtmh", [128, 16, 65], BF16)
        vtml = alloc(st_kv, "vtml", [128, 16, 65], BF16)
        konT = alloc(st_kv, "konT", [65, TS], F32)
        ksc3 = alloc(st_kv, "ksc3", [3, 2 * TS], BF16)
        kstat = alloc(st_kv, "kstat", [128, 32], BF16)
        wmx16 = alloc(st_kv, "wmx16", [128, 128], F32)
        wmn16 = alloc(st_kv, "wmn16", [128, 128], F32)
        wrow = alloc(st_kv, "wrow", [1, 48], F32)
        idns = alloc(st_kv, "idns", [128, 128], F32)
        segm = alloc(st_kv, "segm", [1, 64], F32)

        G(lambda e: e.dma_start(out=mDadd[:], in_=mD_add[:, :]))
        G(lambda e: e.dma_start(out=mOadd[:], in_=mO_add[:, :]))
        G(lambda e: e.dma_start(out=mDBm[:], in_=mDB_mult[:, :]))
        G(lambda e: e.dma_start(out=mOBm[:], in_=mOB_mult[:, :]))
        c0 = P.now("dS", "dS2", "dG", "dG2")
        V(lambda e: e.memset(ones128[:], 1.0), waits=c0)
        V(lambda e: e.memset(ones128b[:], 1.0))
        V(lambda e: e.memset(onesrow[:], 1.0))
        V(lambda e: e.memset(m65[0:64, :], 1.0))
        V(lambda e: e.memset(m65[64:65, :], 0.5))
        V(lambda e: e.memset(onesP64[64:65, :], 1.0))
        P.set_barrier()

        kvT = AR[:, 65536:73728].bitcast(F32)
        kvTlo = AR[:, 73728:77824].bitcast(BF16)
        kscrow = AR[:, 77824:86016].bitcast(F32)[0:1]
        xc = AR[:, 86016:94208].bitcast(F32).rearrange("p (a t) -> p a t", a=2)
        kminb = AR[:, 94208:94272].bitcast(BF16)[0:64]
        tr1 = AR[:, 94272:98368].bitcast(BF16)[0:1]
        mrow = alloc(st_p1, "mrow", [1, TS], F32)
        rrow = alloc(st_p1, "rrow", [1, TS], F32)
        scr1 = alloc(st_p1, "scr1", [1, TS], F32)
        scr2 = alloc(st_p1, "scr2", [1, TS], F32)
        sq = alloc(st_p1, "sq", [128, TS], F32)
        uu = sq
        xrh = alloc(st_p1, "xrh", [128, 2, TS], BF16)
        xrl = alloc(st_p1, "xrl", [128, 2, TS], BF16)
        kvW = AR[:, 98368:106560].bitcast(F32).rearrange("p (c l) -> p c l", c=NCH)
        kvWlo = AR[:, 106560:110656].bitcast(BF16).rearrange("p (c l) -> p c l", c=NCH)
        kW65 = AR[:, 110656:114816].bitcast(F32).rearrange("p (c l) -> p c l", c=NCH)
        kW65lo = AR[:, 114816:116896].bitcast(BF16).rearrange("p (c l) -> p c l", c=NCH)

        S(lambda e: e.dma_start(out=idns[:], in_=idn[:, :]))
        S(lambda e: e.dma_start(out=kvW[:, :, 0:64],
                                in_=wkT.rearrange("(c p) l -> p c l", p=128)))
        S(lambda e: e.dma_start(out=kvW[:, :, 64:128],
                                in_=wvT.rearrange("(c p) l -> p c l", p=128)))
        S(lambda e: e.dma_start(out=kW65[:, :, 0:64],
                                in_=wkT.rearrange("(c p) l -> p c l", p=128)))
        wl_ = P.now("dS", "dS2")
        V(lambda e: e.tensor_tensor(out=kvWlo[:].rearrange("p c l -> p (c l)"),
                                    in0=kvW[:].rearrange("p c l -> p (c l)"),
                                    in1=hiv(kvW[:].rearrange("p c l -> p (c l)")),
                                    op=ALU.subtract), waits=wl_)
        for c in range(NCH):
            V((lambda c: lambda e: e.tensor_reduce(
                out=kW65[:, c, 64:65], in_=kW65[:, c, 0:64], axis=AX.X, op=ALU.add))(c),
              waits=wl_)
        V(lambda e: e.tensor_tensor(out=kW65lo[:].rearrange("p c l -> p (c l)"),
                                    in0=kW65[:].rearrange("p c l -> p (c l)"),
                                    in1=hiv(kW65[:].rearrange("p c l -> p (c l)")),
                                    op=ALU.subtract))

        def ln_rows(src_dram):
            P.new_epoch()
            sw = P.now("T", "V", "A")
            done = {-2: sw, -1: sw}
            for c in range(NCH):
                buf = c % 2
                ld = S((lambda c, buf: lambda e: e.dma_start(
                    out=xc[:, buf, :], in_=src_dram[128 * c:128 * (c + 1), :]))(c, buf),
                    waits=done.get(c - 2, []))
                lw = [ld]
                A((lambda buf: lambda e: e.activation(
                    out=sq[:].bitcast(BF16)[:, 0:TS], in_=xc[:, buf, :],
                    func=AF.Square))(buf),
                  waits=lw + (P.now("T") if c > 0 else []))
                for qb in range(2):
                    T((lambda c, buf, qb, st: lambda e: e.matmul(
                        pA[0:1, 512 * qb:512 * (qb + 1)], ones128b[:],
                        hiv(xc[:, buf, 512 * qb:512 * (qb + 1)]),
                        start=st, stop=False, skip_group_check=True))(
                        c, buf, qb, P.first_touch(("A", qb))), waits=lw)
                aw = P.now("A")
                for qb in range(2):
                    T((lambda qb, st: lambda e: e.matmul(
                        pA[32:33, 512 * qb:512 * (qb + 1)], ones128b[:],
                        sq[:].bitcast(BF16)[:, 512 * qb:512 * (qb + 1)],
                        start=st, stop=False, skip_group_check=True))(
                        qb, P.first_touch(("A32", qb))), waits=aw)
                done[c] = P.now("T", "A")
            tw = P.now("T")
            V(lambda e: e.tensor_scalar(out=mrow[:, :], in0=pA[0:1, :], scalar1=1.0 / E,
                                        scalar2=0.0, op0=ALU.mult, op1=ALU.add), waits=tw)
            V(lambda e: e.tensor_scalar(out=rrow[:, :], in0=pA[32:33, :],
                                        scalar1=1.0 / E, scalar2=1e-5,
                                        op0=ALU.mult, op1=ALU.add))
            V(lambda e: e.scalar_tensor_tensor(out=scr1[:, :], in0=mrow[:, :],
                                               scalar=-1.0, in1=mrow[:, :],
                                               op0=ALU.mult, op1=ALU.mult))
            V(lambda e: e.tensor_tensor(out=rrow[:, :], in0=rrow[:, :], in1=scr1[:, :],
                                        op=ALU.add))
            V(lambda e: e.reciprocal(out=scr1[:, :], in_=rrow[:, :]))
            vw = P.now("V")
            A(lambda e: e.activation(out=scr1[:, :], in_=scr1[:, :], func=AF.Sqrt),
              waits=vw)
            aw = P.now("A")
            V(lambda e: e.tensor_tensor(out=scr2[:, :], in0=scr1[:, :], in1=scr1[:, :],
                                        op=ALU.mult), waits=aw)
            V(lambda e: e.tensor_tensor(out=scr2[:, :], in0=scr2[:, :], in1=rrow[:, :],
                                        op=ALU.mult))
            V(lambda e: e.tensor_scalar(out=scr2[:, :], in0=scr2[:, :], scalar1=-0.5,
                                        scalar2=1.5, op0=ALU.mult, op1=ALU.add))
            V(lambda e: e.tensor_tensor(out=rrow[:, :], in0=scr1[:, :], in1=scr2[:, :],
                                        op=ALU.mult))
            P.new_epoch()
            vw = P.now("V")
            for srcrow, dst in ((mrow, bcM), (rrow, bcR)):
                for qb in range(2):
                    T((lambda srcrow, qb, st: lambda e: e.matmul(
                        pB[:, 512 * qb:512 * (qb + 1)], onesrow[:, :],
                        srcrow[:, 512 * qb:512 * (qb + 1)], start=st, stop=False,
                        skip_group_check=True))(srcrow, qb, True), waits=vw)
                tw = P.now("T")
                V((lambda dst: lambda e: e.tensor_copy(out=dst[:], in_=pB[:]))(dst),
                  waits=tw)
                vw = P.now("V")

        def ln_apply(src_dram, hi_dst, lo_dst, mm_fn):
            sw = P.now("T", "V", "A")
            done = {-2: sw, -1: sw}
            for c in range(NCH):
                buf = c % 2
                ld = S((lambda c, buf: lambda e: e.dma_start(
                    out=xc[:, buf, :], in_=src_dram[128 * c:128 * (c + 1), :]))(c, buf),
                    waits=done.get(c - 2, []))
                V((lambda buf: lambda e: e.tensor_tensor(
                    out=uu[:], in0=xc[:, buf, :], in1=bcM[:], op=ALU.subtract))(buf),
                  waits=[ld])
                V(lambda e: e.tensor_tensor(out=uu[:], in0=uu[:], in1=bcR[:],
                                            op=ALU.mult))
                hd, ldst = hi_dst(c), lo_dst(c)
                V((lambda hd: lambda e: e.tensor_copy(out=hd, in_=uu[:]))(hd))
                V((lambda hd, ldst: lambda e: e.tensor_tensor(
                    out=ldst, in0=uu[:], in1=hd, op=ALU.subtract))(hd, ldst))
                mm_fn(c)
                done[c] = P.now("T", "V")

        ln_rows(xoT)
        P.new_epoch()

        def latent_own_mm(c):
            vw = P.now("V")
            for j in range(8):
                ps, col = (pC, 128 * j) if j < 4 else (pD, 128 * (j - 4))
                for pi, (wap, xap) in enumerate((
                        (hiv(kvW[:, c, :]), xho_hi[:, c, 128 * j:128 * (j + 1)]),
                        (hiv(kvW[:, c, :]), xho_lo[:, c, 128 * j:128 * (j + 1)]),
                        (kvWlo[:, c, :], xho_hi[:, c, 128 * j:128 * (j + 1)]))):
                    st = P.first_touch((PSNAME[id(ps)], col // 512))
                    T((lambda ps, col, wap, xap, st: lambda e: e.matmul(
                        ps[:, col:col + 128], wap, xap, start=st, stop=False,
                        skip_group_check=True))(ps, col, wap, xap, st), waits=vw)
            for qb in range(2):
                ps = pE if qb == 0 else pF
                for pi, (wap, xap) in enumerate((
                        (hiv(kW65[:, c, :]), xho_hi[:, c, 512 * qb:512 * (qb + 1)]),
                        (hiv(kW65[:, c, :]), xho_lo[:, c, 512 * qb:512 * (qb + 1)]),
                        (kW65lo[:, c, :], xho_hi[:, c, 512 * qb:512 * (qb + 1)]))):
                    st = P.first_touch((PSNAME[id(ps)], 0))
                    T((lambda ps, wap, xap, st: lambda e: e.matmul(
                        ps[0:65, :], wap, xap, start=st, stop=False,
                        skip_group_check=True))(ps, wap, xap, st), waits=vw)

        ln_apply(xoT, lambda c: xho_hi[:, c, :], lambda c: xho_lo[:, c, :], latent_own_mm)
        tw = P.now("T")
        for j in range(8):
            ps, col = (pC, 128 * j) if j < 4 else (pD, 128 * (j - 4))
            V((lambda j, ps, col: lambda e: e.tensor_scalar(
                out=kvT[0:64, 128 * j:128 * (j + 1)], in0=ps[0:64, col:col + 128],
                scalar1=INV, scalar2=0.0, op0=ALU.mult, op1=ALU.add))(j, ps, col),
              waits=tw)
            V((lambda j, ps, col: lambda e: e.tensor_copy(
                out=kvT[64:128, 128 * j:128 * (j + 1)],
                in_=ps[64:128, col:col + 128]))(j, ps, col))
        V(lambda e: e.tensor_scalar(out=konT[:, 0:512], in0=pE[0:65, :], scalar1=INV,
                                    scalar2=0.0, op0=ALU.mult, op1=ALU.add), waits=tw)
        V(lambda e: e.tensor_scalar(out=konT[:, 512:1024], in0=pF[0:65, :], scalar1=INV,
                                    scalar2=0.0, op0=ALU.mult, op1=ALU.add))

        ln_rows(xrT)
        P.new_epoch()

        def latent_oth_mm(c):
            vw = P.now("V")
            buf = c % 2
            for j in range(8):
                ps, col = (pC, 128 * j) if j < 4 else (pD, 128 * (j - 4))
                for pi, (wap, xap) in enumerate((
                        (hiv(kvW[:, c, :]), xrh[:, buf, 128 * j:128 * (j + 1)]),
                        (hiv(kvW[:, c, :]), xrl[:, buf, 128 * j:128 * (j + 1)]),
                        (kvWlo[:, c, :], xrh[:, buf, 128 * j:128 * (j + 1)]))):
                    st = P.first_touch((PSNAME[id(ps)], col // 512))
                    T((lambda ps, col, wap, xap, st: lambda e: e.matmul(
                        ps[:, col:col + 128], wap, xap, start=st, stop=False,
                        skip_group_check=True))(ps, col, wap, xap, st), waits=vw)

        ln_apply(xrT, lambda c: xrh[:, c % 2, :], lambda c: xrl[:, c % 2, :], latent_oth_mm)
        tw = P.now("T")
        for j in range(8):
            ps, col = (pC, 128 * j) if j < 4 else (pD, 128 * (j - 4))
            V((lambda j, ps, col: lambda e: e.tensor_scalar(
                out=kvT[0:64, 1024 + 128 * j:1024 + 128 * (j + 1)],
                in0=ps[0:64, col:col + 128], scalar1=INV, scalar2=0.0,
                op0=ALU.mult, op1=ALU.add))(j, ps, col), waits=tw)
            V((lambda j, ps, col: lambda e: e.tensor_copy(
                out=kvT[64:128, 1024 + 128 * j:1024 + 128 * (j + 1)],
                in_=ps[64:128, col:col + 128]))(j, ps, col))

        V(lambda e: e.tensor_tensor(out=kvTlo[:], in0=kvT[:], in1=hiv(kvT[:]),
                                    op=ALU.subtract))
        V(lambda e: e.tensor_copy(out=kh65[0:64, :], in_=hiv(kvT[0:64, :])))
        V(lambda e: e.memset(kh65[64:65, :], 1.0))
        V(lambda e: e.tensor_copy(out=khl[0:64, :], in_=hiv(kvT[0:64, :])))
        vwk = P.now("V")
        G(lambda e: e.dma_start(out=khl[64:128, :], in_=kvTlo[0:64, :]), waits=vwk)
        P.new_epoch()
        vw = P.now("V")
        for qq in range(4):
            ps = pA if qq < 2 else pB
            col = 512 * (qq % 2)
            for half in range(2):
                src = (hiv(kvT[0:64, 512 * qq:512 * (qq + 1)]) if half == 0
                       else kvTlo[0:64, 512 * qq:512 * (qq + 1)])
                st = P.first_touch((PSNAME[id(ps)], col // 512))
                T((lambda ps, col, src, st: lambda e: e.matmul(
                    ps[0:1, col:col + 512], ones128b[0:64, :], src, start=st,
                    stop=False, skip_group_check=True))(ps, col, src, st), waits=vw)
        tw = P.now("T")
        V(lambda e: e.tensor_scalar(out=kscrow[0:1, 0:1024], in0=pA[0:1, :], scalar1=0.5,
                                    scalar2=0.0, op0=ALU.mult, op1=ALU.add), waits=tw)
        V(lambda e: e.tensor_scalar(out=kscrow[0:1, 1024:2048], in0=pB[0:1, :],
                                    scalar1=0.5, scalar2=0.0, op0=ALU.mult, op1=ALU.add))
        V(lambda e: e.tensor_copy(out=ksc3[0:1, :], in_=hiv(kscrow[:])))
        V(lambda e: e.tensor_tensor(out=tr1[0:1, :], in0=kscrow[:], in1=hiv(kscrow[:]),
                                    op=ALU.subtract))
        vw = P.now("V")
        G(lambda e: e.dma_start(out=ksc3[1:2, :], in_=tr1[0:1, :]), waits=vw)
        G(lambda e: e.dma_start(out=ksc3[2:3, :], in_=ksc3[0:1, :]), waits=vw)
        V(lambda e: e.tensor_reduce(out=segm[0:1, 0:32],
                                    in_=kscrow[:].rearrange("o (s w) -> o s w", w=64),
                                    axis=AX.X, op=ALU.max))
        V(lambda e: e.tensor_reduce(out=segm[0:1, 32:64],
                                    in_=kscrow[:].rearrange("o (s w) -> o s w", w=64),
                                    axis=AX.X, op=ALU.min))
        for r, op in ((0, ALU.max), (1, ALU.min)):
            o = 24 * r
            b = 32 * r
            V((lambda op, o, b: lambda e: e.tensor_tensor(
                out=wrow[0:1, o:o + 8], in0=segm[0:1, b + 0:b + 16:2],
                in1=segm[0:1, b + 1:b + 16:2], op=op))(op, o, b))
            V((lambda op, o, b: lambda e: e.tensor_tensor(
                out=wrow[0:1, o + 8:o + 16], in0=segm[0:1, b + 16:b + 32:2],
                in1=segm[0:1, b + 17:b + 32:2], op=op))(op, o, b))
            V((lambda op, o, b: lambda e: e.tensor_tensor(
                out=wrow[0:1, o + 16:o + 24], in0=segm[0:1, b + 14:b + 30:2],
                in1=segm[0:1, b + 15:b + 30:2], op=op))(op, o, b))
            V((lambda op, o: lambda e: e.tensor_tensor(
                out=wrow[0:1, o:o + 8], in0=wrow[0:1, o:o + 8],
                in1=wrow[0:1, o + 8:o + 16], op=op))(op, o))
            V((lambda op, o: lambda e: e.tensor_tensor(
                out=wrow[0:1, o:o + 8], in0=wrow[0:1, o:o + 8],
                in1=wrow[0:1, o + 16:o + 24], op=op))(op, o))
        P.new_epoch()
        vw = P.now("V")
        T((lambda st: lambda e: e.matmul(pC[:, 0:8], onesrow[:, :], wrow[0:1, 0:8],
                                         start=st, stop=False, skip_group_check=True))(
            P.first_touch(("C", 0))), waits=vw)
        T(lambda e: e.matmul(pC[:, 8:16], onesrow[:, :], wrow[0:1, 24:32], start=False,
                             stop=False, skip_group_check=True))
        tw = P.now("T")
        for h in range(H):
            V((lambda h: lambda e: e.tensor_copy(out=wmx16[:, 8 * h:8 * h + 8],
                                                 in_=pC[:, 0:8]))(h), waits=tw)
            V((lambda h: lambda e: e.tensor_copy(out=wmn16[:, 8 * h:8 * h + 8],
                                                 in_=pC[:, 8:16]))(h))
        for b in range(32):
            V((lambda b: lambda e: e.tensor_reduce(
                out=kstat[0:64, b:b + 1], in_=kvT[0:64, 64 * b:64 * (b + 1)],
                axis=AX.X, op=ALU.max))(b))
            V((lambda b: lambda e: e.tensor_reduce(
                out=kminb[:, b:b + 1], in_=kvT[0:64, 64 * b:64 * (b + 1)],
                axis=AX.X, op=ALU.min))(b))
        vw = P.now("V")
        G(lambda e: e.dma_start(out=kstat[64:128, :], in_=kminb[:, :]), waits=vw)
        for kt in range(16):
            vw2 = P.now("V")
            T((lambda kt: lambda e: e.transpose(
                pD[:, 0:64], kvT[64:128, 128 * kt:128 * (kt + 1)],
                idns[64:128, 64:128]))(kt), waits=vw2)
            tw = P.now("T")
            V((lambda kt: lambda e: e.tensor_copy(
                out=vtmh[:, kt, 0:64], in_=pD[:, 0:64]))(kt), waits=tw)
            V((lambda kt: lambda e: e.tensor_tensor(
                out=vtml[:, kt, 0:64], in0=pD[:, 0:64], in1=vtmh[:, kt, 0:64],
                op=ALU.subtract))(kt))
        V(lambda e: e.memset(vtmh[:, :, 64:65].rearrange("p c l -> p (c l)"), 1.0))
        V(lambda e: e.memset(vtml[:, :, 64:65].rearrange("p c l -> p (c l)"), 0.0))
        tap("xho_hi", xho_hi[:, 0, :], 128, TS)
        tap("kvT", kvT[:, :], 128, 2 * TS)
        tap("konT", konT[0:65, :], 65, TS)
        tap("kscrow", kscrow[0:1, :], 1, 2 * TS)
        flush()
        st_p1.close()
        P.set_barrier()

        qlatH = alloc(st_p2, "qlatH", [65, 2, TS], F32)
        wqc = alloc(st_p2, "wqc", [128, 2, 512], F32)
        wqlo = alloc(st_p2, "wqlo", [128, 2, 512], BF16)
        q2l65 = alloc(st_p2, "q2l65", [128, H, 65], F32)
        q2l65lo = alloc(st_p2, "q2l65lo", [128, H, 65], BF16)
        qevh = alloc(st_p2, "qevh", [128, 2, 512], BF16)
        qevl = alloc(st_p2, "qevl", [128, 2, 512], BF16)
        qpn = alloc(st_p2, "qpn", [128, 2, TS], BF16)
        qntmp = alloc(st_p2, "qntmp", [64, TS], BF16)
        prodx = alloc(st_p2, "prodx", [65, 2, TS], BF16)
        qloS = alloc(st_p2, "qloS", [65, TS], BF16)

        bar = barrier()
        S(lambda e: e.dma_start(out=q2l65[:, :, 0:64],
                                in_=q2li.rearrange("h d l -> d h l")), waits=bar)
        q2w = P.now("dS", "dS2")
        for h in range(H):
            V((lambda h: lambda e: e.tensor_reduce(
                out=q2l65[:, h, 64:65], in_=q2l65[:, h, 0:64], axis=AX.X, op=ALU.add))(h),
              waits=q2w + bar)
        V(lambda e: e.tensor_tensor(out=q2l65lo[:].rearrange("p h l -> p (h l)"),
                                    in0=q2l65[:].rearrange("p h l -> p (h l)"),
                                    in1=hiv(q2l65[:].rearrange("p h l -> p (h l)")),
                                    op=ALU.subtract))
        V(lambda e: e.memset(qh65[:, :, :][64:65].rearrange("o h t -> o (h t)"), 0.0))

        PSQ = [(pA, 0), (pA, 512), (pB, 0), (pB, 512), (pC, 0), (pD, 0), (pE, 0),
               (pF, 0)]
        doneQ = {}
        mach_t = {}
        rowvw = {}
        gw2D = {}
        qdma_w = {}

        def emit_eq(grp, hi_):
            h = 4 * grp + hi_
            hb = h % 2
            for qb in range(2):
                ps, col = PSQ[hi_ * 2 + qb]
                tw = P.now("T")
                V((lambda ps, col, qb: lambda e: e.tensor_copy(
                    out=qevh[:, qb, :], in_=ps[:, col:col + 512]))(ps, col, qb),
                  waits=tw)
                V((lambda ps, col, qb: lambda e: e.tensor_tensor(
                    out=qevl[:, qb, :], in0=ps[:, col:col + 512], in1=qevh[:, qb, :],
                    op=ALU.subtract))(ps, col, qb))
                vw = P.now("V")
                for pi in range(3):
                    wap = (hiv(q2l65[:, h, :]) if pi < 2 else q2l65lo[:, h, :])
                    xap = (qevh if pi != 1 else qevl)[:, qb, :]
                    T((lambda ps, col, wap, xap, st: lambda e: e.matmul(
                        ps[0:65, col:col + 512], wap, xap, start=st, stop=False,
                        skip_group_check=True))(ps, col, wap, xap, pi == 0),
                      waits=vw)
                tw = P.now("T")
                V((lambda ps, col, qb, hb: lambda e: e.tensor_copy(
                    out=qlatH[:, hb, 512 * qb:512 * (qb + 1)],
                    in_=ps[0:65, col:col + 512]))(ps, col, qb, hb), waits=tw)
            mw = mach_t.get(h - 2, [])
            V((lambda h, hb: lambda e: e.tensor_copy(
                out=qh65[:, h, :][0:64], in_=hiv(qlatH[0:64, hb, :])))(h, hb))
            vwq = P.now("V")
            G((lambda h: lambda e: e.dma_start(
                out=qhPK[64:128, h, :], in_=qhPK[0:64, h, :]))(h), waits=vwq)
            V((lambda h, hb: lambda e: e.tensor_tensor(
                out=ql65[:, h, :][0:64], in0=qlatH[0:64, hb, :],
                in1=qh65[:, h, :][0:64], op=ALU.subtract))(h, hb))
            V((lambda hb: lambda e: e.tensor_tensor(
                out=prodx[0:64, hb, :], in0=qlatH[0:64, hb, :], in1=konT[0:64, :],
                op=ALU.mult))(hb), waits=mw)
            V((lambda hb: lambda e: e.tensor_tensor(
                out=prodx[64:65, hb, :], in0=qlatH[64:65, hb, :], in1=konT[64:65, :],
                op=ALU.mult))(hb))
            V((lambda hb: lambda e: e.tensor_scalar(
                out=qpn[0:64, hb, :], in0=qlatH[0:64, hb, :], scalar1=0.0,
                scalar2=0.0, op0=ALU.max, op1=ALU.add))(hb))
            V((lambda hb: lambda e: e.tensor_scalar(
                out=qntmp[:, :], in0=qlatH[0:64, hb, :], scalar1=0.0, scalar2=0.0,
                op0=ALU.min, op1=ALU.add))(hb), waits=qdma_w.get(h - 1, []))
            V((lambda hb: lambda e: e.tensor_copy(
                out=qloS[64:65, :], in_=hiv(qlatH[64:65, hb, :])))(hb))
            vw = P.now("V")
            rowvw[h] = vw
            G((lambda hb: lambda e: e.dma_start(
                out=qpn[64:128, hb, :], in_=qntmp[:, :]))(hb), waits=vw)
            G((lambda h: lambda e: e.dma_start(
                out=qsum3p[3 * h:3 * h + 1, :], in_=qloS[64:65, :]))(h))
            G((lambda h: lambda e: e.dma_start(
                out=qsum3p[3 * h + 1:3 * h + 2, :], in_=qloS[64:65, :]))(h))
            ghw = P.now("dG", "dG2")
            V((lambda hb: lambda e: e.tensor_tensor(
                out=qloS[64:65, :], in0=qlatH[64:65, hb, :],
                in1=hiv(qlatH[64:65, hb, :]), op=ALU.subtract))(hb), waits=ghw)
            vw2 = P.now("V")
            G((lambda h: lambda e: e.dma_start(
                out=qsum3p[3 * h + 2:3 * h + 3, :], in_=qloS[64:65, :]))(h),
              waits=vw2)
            gw2D[h] = P.now("dG", "dG2")
            qdma_w[h] = gw2D[h]

        def emit_mach(grp, hi_):
            h = 4 * grp + hi_
            hb = h % 2
            ps0, col0 = PSQ[hi_ * 2]
            vw = rowvw[h]
            gw2 = gw2D[h]
            for qt in range(8):
                T((lambda ps0, col0, qt, hb, st: lambda e: e.matmul(
                    ps0[:, col0 + 256 + qt:col0 + 257 + qt],
                    prodx[:, hb, 128 * qt:128 * (qt + 1)], m65[:, :],
                    start=st, stop=False, skip_group_check=True))(
                    ps0, col0, qt, hb, qt == 0), waits=vw)
                T((lambda ps0, col0, qt, hb: lambda e: e.matmul(
                    ps0[:, col0 + 264 + qt:col0 + 265 + qt],
                    qlatH[64:65, hb, 128 * qt:128 * (qt + 1)], ones128[64:65, :],
                    start=False, stop=False, skip_group_check=True))(ps0, col0, qt, hb),
                  waits=vw)
                T((lambda ps0, col0, qt, hb: lambda e: e.matmul(
                    ps0[:, col0 + 32 * qt:col0 + 32 * (qt + 1)],
                    qpn[:, hb, 128 * qt:128 * (qt + 1)], kstat[:, :],
                    start=False, stop=False, skip_group_check=True))(ps0, col0, qt, hb),
                  waits=vw + gw2)
            mach_t[h] = [("T", P.cnt["T"])]
            tw = P.now("T")
            V((lambda ps0, col0, h: lambda e: e.tensor_reduce(
                out=b128[:, 8 * h:8 * h + 8],
                in_=ps0[:, col0:col0 + 256].rearrange("p (q b) -> p q b", b=32),
                axis=AX.X, op=ALU.max))(ps0, col0, h), waits=tw)
            V((lambda ps0, col0, h: lambda e: e.tensor_copy(
                out=d128[:, 8 * h:8 * h + 8],
                in_=ps0[:, col0 + 256:col0 + 264]))(ps0, col0, h))
            V((lambda ps0, col0, h: lambda e: e.tensor_copy(
                out=q128s[:, 8 * h:8 * h + 8],
                in_=ps0[:, col0 + 264:col0 + 272]))(ps0, col0, h))

        for grp in range(4):
            heads = list(range(4 * grp, 4 * grp + 4))
            P.new_epoch()
            for c in range(NCH):
                gc = grp * 16 + c
                buf = gc % 2
                ld = S((lambda c, buf, grp: lambda e: e.dma_start(
                    out=wqc[:, buf, :],
                    in_=wqT[128 * c:128 * (c + 1),
                            512 * grp:512 * (grp + 1)]))(c, buf, grp),
                    waits=doneQ.get(gc - 2, bar))
                V((lambda buf: lambda e: e.tensor_tensor(
                    out=wqlo[:, buf, :], in0=wqc[:, buf, :],
                    in1=hiv(wqc[:, buf, :]),
                    op=ALU.subtract))(buf), waits=[ld])
                vw = P.now("V")
                for hi_, h in enumerate(heads):
                    for qb in range(2):
                        ps, col = PSQ[hi_ * 2 + qb]
                        for pi in range(3):
                            wap = (hiv(wqc[:, buf, 128 * hi_:128 * (hi_ + 1)])
                                   if pi < 2
                                   else wqlo[:, buf, 128 * hi_:128 * (hi_ + 1)])
                            xap = (xho_hi if pi != 1 else xho_lo)[
                                :, c, 512 * qb:512 * (qb + 1)]
                            st = P.first_touch((PSNAME[id(ps)], col // 512))
                            T((lambda ps, col, wap, xap, st: lambda e: e.matmul(
                                ps[:, col:col + 512], wap, xap, start=st, stop=False,
                                skip_group_check=True))(ps, col, wap, xap, st), waits=vw)
                doneQ[gc] = P.now("T", "V")
            for hi_ in range(4):
                emit_eq(grp, hi_)
                if hi_ >= 1:
                    emit_mach(grp, hi_ - 1)
            emit_mach(grp, 3)
        flush()
        tap("qlat", qlatH[:, 1, :], 65, TS)

        V(lambda e: e.tensor_scalar(out=b128[:, :], in0=b128[:, :], scalar1=1.0,
                                    scalar2=2.0 - PAD, op0=ALU.mult, op1=ALU.add))
        V(lambda e: e.tensor_tensor(out=s1s[:, :], in0=q128s[:, :], in1=wmx16[:, :],
                                    op=ALU.mult))
        V(lambda e: e.tensor_tensor(out=wmx16[:, :], in0=q128s[:, :], in1=wmn16[:, :],
                                    op=ALU.mult))
        V(lambda e: e.tensor_tensor(out=s1s[:, :], in0=s1s[:, :], in1=wmx16[:, :],
                                    op=ALU.max))
        V(lambda e: e.tensor_tensor(out=b128[:, :], in0=b128[:, :], in1=s1s[:, :],
                                    op=ALU.add))
        V(lambda e: e.tensor_tensor(out=b128[:, :], in0=b128[:, :], in1=d128[:, :],
                                    op=ALU.max))
        V(lambda e: e.tensor_scalar(out=b128[:, :], in0=b128[:, :], scalar1=-1.0,
                                    scalar2=0.0, op0=ALU.mult, op1=ALU.add))
        vw = P.now("V")
        T(lambda e: e.matmul(pC[:, 0:128], b128[:, :], idns[:, :], start=True,
                             stop=False, skip_group_check=True, is_transpose=True),
          waits=vw)
        tw = P.now("T")
        V(lambda e: e.tensor_copy(out=s1s[:, :], in_=pC[:, 0:128]), waits=tw)
        vw = P.now("V")
        G(lambda e: e.dma_start(
            out=ql65[:, :, :][64:65].rearrange("o h (q p) -> o (h q p)", p=128),
            in_=s1s[:, :]), waits=vw)
        flush()
        tap("S128", b128[:, :], 128, 128)
        flush()
        st_p2.close()
        P.set_barrier()

        expf = alloc(st_p3, "expf", [128, 2, TS], F32)
        expl = alloc(st_p3, "expl", [128, 2, TS], BF16)
        mtmp = alloc(st_p3, "mtmp", [128, 256], F32)
        qs3b = alloc(st_p3, "qs3b", [3, 2, TS], BF16)
        rbc = alloc(st_p3, "rbc", [64, TS], F32)
        cxs = alloc(st_p3, "cxs", [64, TS], F32)
        cxu = alloc(st_p3, "cxu", [65, TS], F32)

        bar = barrier()
        P.new_epoch()
        mask_v = {}
        exp_a = {}
        lo_v = {}
        av_t = {}
        epi_v = {}
        epi_free = {}
        ktlist = []
        for j in range(8):
            ktlist.append(("own", j))
            ktlist.append(("oth", j))
        qs3w_d = {}
        seg_d = {}

        def emit_qdma(h):
            hb = h % 2
            G((lambda h, hb: lambda e: e.dma_start(
                out=qs3b[:, hb, :], in_=qsum3p[3 * h:3 * h + 3, :]))(h, hb),
              waits=(bar if h < 2 else av_t[(h - 2) * 16 + 15]))
            qs3w_d[h] = P.now("dG", "dG2")

        def emit_lr(g):
            h, idx = g // 16, g % 16
            if idx == 0:
                emit_qdma(h)
            typ, j = ktlist[idx]
            pL = pA if idx % 2 == 0 else pB
            kt = j if typ == "own" else 8 + j
            lcol = 128 * kt
            sfx = 128 * j
            W = TS - sfx
            hb = h % 2
            lw = (bar if g == 0 else []) + \
                ((exp_a[g - 2] + mask_v[g - 2]) if g >= 2 else [])
            cuts = sorted({sfx, TS} | ({512} if sfx < 512 else set())
                          | ({sfx + 512} if sfx + 512 < TS else set()))
            segs = [(cuts[i], cuts[i + 1] - cuts[i]) for i in range(len(cuts) - 1)]
            seg_d[g] = (segs, sfx, kt, W)
            seen_banks = set()
            for q0, w in segs:
                bank = (q0 - sfx) // 512
                fresh = bank not in seen_banks
                seen_banks.add(bank)
                for pi in range(2):
                    wap = (khl[:, lcol:lcol + 128] if pi == 0
                           else kh65[:, lcol:lcol + 128])
                    xap = (qhPK if pi == 0 else ql65)[:, h, q0:q0 + w]
                    T((lambda pL, q0, w, sfx, wap, xap, st: lambda e: e.matmul(
                        pL[:, q0 - sfx:q0 - sfx + w], wap, xap, start=st,
                        stop=False, skip_group_check=True))(
                        pL, q0, w, sfx, wap, xap, pi == 0 and fresh),
                      waits=lw)
            pR = pC if idx % 2 == 0 else pD
            wB = 128 if typ == "own" else min(256, W)
            T((lambda pR, lcol, hb, sfx, wB: lambda e: e.matmul(
                pR[:, 0:wB], ksc3[:, lcol:lcol + 128],
                qs3b[:, hb, sfx:sfx + wB], start=True, stop=False,
                skip_group_check=True))(pR, lcol, hb, sfx, wB),
              waits=lw + qs3w_d[h] + (mask_v[g - 2] if g >= 2 else []))
            tw = P.now("T")
            mm = mDBm[:, 0:128] if typ == "own" else mOBm[:, 0:wB]
            V((lambda pR, wB, mm: lambda e: e.tensor_tensor(
                out=mtmp[:, 0:wB], in0=pR[:, 0:wB], in1=mm, op=ALU.mult))(
                pR, wB, mm), waits=tw)
            V((lambda pL, wB: lambda e: e.tensor_tensor(
                out=pL[:, 0:wB], in0=pL[:, 0:wB], in1=mtmp[:, 0:wB],
                op=ALU.add))(pL, wB))
            ma = mDadd if typ == "own" else mOadd
            V((lambda pL, ma: lambda e: e.tensor_tensor(
                out=pL[:, 0:128], in0=pL[:, 0:128], in1=ma[:, :], op=ALU.add))(
                pL, ma))
            mask_v[g] = P.now("V")
            eb = idx % 2
            A((lambda pL, eb, W: lambda e: e.activation(
                out=expf[:, eb, 0:W], in_=pL[:, 0:W], func=AF.Exp))(pL, eb, W),
              waits=mask_v[g] + (av_t[g - 2] if g >= 2 else []))
            exp_a[g] = [("A", P.cnt["A"])]
            V((lambda eb, W: lambda e: e.tensor_tensor(
                out=expl[:, eb, 0:W], in0=expf[:, eb, 0:W],
                in1=hiv(expf[:, eb, 0:W]), op=ALU.subtract))(eb, W),
              waits=exp_a[g])
            lo_v[g] = P.now("V")

        def emit_av(g):
            h, idx = g // 16, g % 16
            segs, sfx, kt, W = seg_d.pop(g)
            eb = idx % 2
            avw0 = (epi_free.get(h - 1, []) if idx == 0 else [])
            for pi in (0, 2, 1):
                for q0, w in segs:
                    ps = pE if q0 < 512 else pF
                    pcol = q0 - (0 if q0 < 512 else 512)
                    wap = (vtmh if pi < 2 else vtml)[:, kt, :]
                    xsrc = hiv(expf[:, eb, q0 - sfx:q0 - sfx + w]) if pi != 1 \
                        else expl[:, eb, q0 - sfx:q0 - sfx + w]
                    st = (idx == 0 and pi == 0)
                    T((lambda ps, pcol, w, wap, xsrc, st: lambda e: e.matmul(
                        ps[0:65, pcol:pcol + w], wap, xsrc, start=st, stop=False,
                        skip_group_check=True))(ps, pcol, w, wap, xsrc, st),
                      waits=(exp_a[g] if pi != 1 else lo_v[g]) + avw0)
            av_t[g] = [("T", P.cnt["T"])]

        def emit_epi(h):
            tw = P.now("T")
            A(lambda e: e.activation(out=cxu[64:65, 0:512], in_=pE[64:65, :],
                                     func=AF.Copy, bias=1e-38, scale=1.0),
              waits=tw)
            A(lambda e: e.activation(out=cxu[64:65, 512:1024], in_=pF[64:65, :],
                                     func=AF.Copy, bias=1e-38, scale=1.0))
            aw = P.now("A")
            V(lambda e: e.reciprocal(out=cxu[64:65, :], in_=cxu[64:65, :]),
              waits=aw)
            vw = P.now("V")
            T(lambda e: e.matmul(pE[64:128, :], onesP64[64:65, :],
                                 cxu[64:65, 0:512], start=True, stop=False,
                                 skip_group_check=True), waits=vw)
            T(lambda e: e.matmul(pF[64:128, :], onesP64[64:65, :],
                                 cxu[64:65, 512:1024], start=True, stop=False,
                                 skip_group_check=True))
            tw = P.now("T")
            V(lambda e: e.tensor_copy(out=rbc[:, 0:512], in_=pE[64:128, :]),
              waits=tw)
            V(lambda e: e.tensor_copy(out=rbc[:, 512:1024], in_=pF[64:128, :]))
            V((lambda h: lambda e: e.tensor_tensor(
                out=cxs[:, 0:512], in0=pE[0:64, :], in1=rbc[0:64, 0:512],
                op=ALU.mult))(h), waits=epi_v.get(h - 1, []))
            V(lambda e: e.tensor_tensor(
                out=cxs[:, 512:1024], in0=pF[0:64, :], in1=rbc[0:64, 512:1024],
                op=ALU.mult))
            epi_free[h] = P.now("V")
            V((lambda h: lambda e: e.tensor_copy(
                out=ctxh[:, h, :], in_=cxs[:, :]))(h))
            V((lambda h: lambda e: e.tensor_tensor(
                out=ctxl[:, h, :], in0=cxs[:, :], in1=ctxh[:, h, :],
                op=ALU.subtract))(h))
            epi_v[h] = P.now("V")
            G((lambda h: lambda e: e.dma_start(
                out=ctxPK1[64:128, h, :], in_=ctxPK1[0:64, h, :]))(h), waits=epi_v[h])
            if h % 2 == 1:
                G((lambda h: lambda e: e.dma_start(
                    out=ctxPK2[64:128, h - 1, :], in_=ctxPK2[0:64, h, :]))(h))

        for gg in range(16 * H + 2):
            if gg >= 2:
                emit_av(gg - 2)
            if gg < 16 * H:
                emit_lr(gg)
            if gg >= 2 and (gg - 2) % 16 == 15:
                emit_epi((gg - 2) // 16)
        P.new_epoch()
        flush()
        tap("ctxf", ctxh[0:64, 0, :], 64, TS)
        flush()
        st_p3.close()
        st_kv.close()
        P.set_barrier()

        woc = alloc(st_p4, "woc", [128, 16, 128], F32)
        woc2 = alloc(st_p4, "woc2", [128, 16, 128], F32)
        woclo = alloc(st_p4, "woclo", [128, 16, 128], BF16)
        woclo2 = alloc(st_p4, "woclo2", [128, 16, 128], BF16)
        vupS = alloc(st_p4, "vupS", [128, H, L], F32)
        vupSlo = alloc(st_p4, "vupSlo", [128, H, L], BF16)
        wvoPK = alloc(st_p4, "wvoPK", [128, 2, H, 128], BF16)
        wvoPR = alloc(st_p4, "wvoPR", [128, 2, 8, 128], BF16)
        wvol = alloc(st_p4, "wvol", [64, 2, H, 128], BF16)
        xoc2 = alloc(st_p4, "xoc2", [128, 2, TS], F32)

        bar = barrier()
        S(lambda e: e.dma_start(out=vupS[:, :, :],
                                in_=vupTi.rearrange("h d l -> d h l")), waits=bar)
        vw0 = P.now("dS", "dS2")
        V(lambda e: e.tensor_tensor(out=vupSlo[:].rearrange("p h l -> p (h l)"),
                                    in0=vupS[:].rearrange("p h l -> p (h l)"),
                                    in1=hiv(vupS[:].rearrange("p h l -> p (h l)")),
                                    op=ALU.subtract), waits=vw0 + bar)

        wocB = [woc, woc2]
        wocloB = [woclo, woclo2]
        wocw = {}
        bT = {}
        aT = {}
        dwD = {}
        xvD = {}

        def p4_dma(eo):
            ld = S((lambda eo: lambda e: e.dma_start(
                out=wocB[eo % 2][:, :, :],
                in_=woT[:, 128 * eo:128 * (eo + 1)].rearrange("(c p) m -> p c m",
                                                              p=128)))(eo),
                waits=bT.get(eo - 2, []) + (bar if eo < 2 else []))
            wocw[eo] = [ld]

        def p4_lo(eo):
            b = eo % 2
            V((lambda b: lambda e: e.tensor_tensor(
                out=wocloB[b][:].rearrange("p c m -> p (c m)"),
                in0=wocB[b][:].rearrange("p c m -> p (c m)"),
                in1=hiv(wocB[b][:].rearrange("p c m -> p (c m)")),
                op=ALU.subtract))(b), waits=wocw[eo] + bT.get(eo - 2, []))

        def p4_build(eo):
            b = eo % 2
            for h in range(H):
                vw = P.now("V")
                ps, col = (pC, 128 * (h % 4)) if h % 8 < 4 else (pD, 128 * (h % 8 - 4))
                for pi in range(3):
                    wap = (hiv(vupS[:, h, :]) if pi < 2 else vupSlo[:, h, :])
                    xap = (hiv(wocB[b][:, h, :]) if pi != 1
                           else wocloB[b][:, h, :])
                    st = (pi == 0 and h % 4 == 0)
                    T((lambda ps, col, wap, xap, st: lambda e: e.matmul(
                        ps[0:64, col:col + 128], wap, xap, start=st, stop=False,
                        skip_group_check=True))(ps, col, wap, xap, st), waits=vw)
                if h % 8 == 7:
                    tw = P.now("T")
                    for ps2, h0 in ((pC, h - 7), (pD, h - 3)):
                        V((lambda b, ps2, h0: lambda e: e.tensor_copy(
                            out=wvoPK[0:64, b, h0:h0 + 4, :],
                            in_=ps2[0:64, 0:512]))(b, ps2, h0),
                          waits=tw + dwD.get(eo - 2, []))
                        V((lambda b, ps2, h0: lambda e: e.tensor_tensor(
                            out=wvol[:, b, h0:h0 + 4, :],
                            in0=ps2[0:64, 0:512].rearrange("p (a m) -> p a m", a=4),
                            in1=wvoPK[0:64, b, h0:h0 + 4, :],
                            op=ALU.subtract))(b, ps2, h0))
            bT[eo] = [("T", P.cnt["T"])]

        def p4_pack(eo):
            b = eo % 2
            vw = P.now("V")
            V((lambda b: lambda e: e.tensor_copy(
                out=wvoPR[0:64, b, :, :],
                in_=wvoPK[0:64, b, :, :][:, 0::2, :]))(b),
              waits=vw + aT.get(eo - 2, []))
            vw = P.now("V")
            dlo = G((lambda b: lambda e: e.dma_start(
                out=wvoPK[64:128, b, :, :],
                in_=wvol[:, b, :, :]))(b),
                waits=vw + aT.get(eo - 2, []))
            dhi = G((lambda b: lambda e: e.dma_start(
                out=wvoPR[64:128, b, :, :],
                in_=wvoPK[0:64, b, :, :][:, 1::2, :]))(b))
            dwD[eo] = [dlo, dhi]

        def p4_apply(eo):
            b = eo % 2
            ld = S((lambda eo: lambda e: e.dma_start(
                out=xoc2[:, eo % 2, :], in_=xoT[128 * eo:128 * (eo + 1), :]))(eo),
                waits=xvD.get(eo - 2, []))
            for qb in range(2):
                ps = pA if qb == 0 else pB
                for h in range(H):
                    st = (h == 0)
                    T((lambda b, ps, h, qb, st: lambda e: e.matmul(
                        ps[:, 0:512], wvoPK[:, b, h, :],
                        ctxPK1[:, h, 512 * qb:512 * (qb + 1)], start=st, stop=False,
                        skip_group_check=True))(b, ps, h, qb, st),
                      waits=dwD[eo] + (xvD.get(eo - 1, []) if st else []))
                for he in range(8):
                    T((lambda b, ps, he, qb: lambda e: e.matmul(
                        ps[:, 0:512], wvoPR[:, b, he, :],
                        ctxPK2[:, 2 * he, 512 * qb:512 * (qb + 1)], start=False,
                        stop=False, skip_group_check=True))(b, ps, he, qb),
                      waits=dwD[eo])
            aT[eo] = [("T", P.cnt["T"])]
            tw = P.now("T")
            V((lambda eo: lambda e: e.tensor_tensor(
                out=xp[:, eo, 0:512], in0=pA[:, 0:512], in1=xoc2[:, eo % 2, 0:512],
                op=ALU.add))(eo), waits=tw + [ld])
            V((lambda eo: lambda e: e.tensor_tensor(
                out=xp[:, eo, 512:1024], in0=pB[:, 0:512],
                in1=xoc2[:, eo % 2, 512:1024], op=ALU.add))(eo))
            xvD[eo] = P.now("V")

        p4_dma(0)
        p4_lo(0)
        p4_build(0)
        p4_pack(0)
        p4_dma(1)
        p4_lo(1)
        for eo in range(NCH):
            if eo + 1 < NCH:
                p4_build(eo + 1)
                p4_pack(eo + 1)
            p4_apply(eo)
            if eo + 2 < NCH:
                p4_dma(eo + 2)
                p4_lo(eo + 2)
        P.new_epoch()
        flush()
        tap("xp", xp[:, 0, :], 128, TS)
        flush()
        st_p4.close()
        P.set_barrier()

        x8 = alloc(st_p5, "x8", [128, NCH, TS], FP8)
        mrow = alloc(st_p5, "mrow5", [1, TS], F32)
        rrow = alloc(st_p5, "rrow5", [1, TS], F32)
        scr1 = srow[0:1, :]
        scr2 = alloc(st_p5, "scr2_5", [1, TS], F32)
        w0c = alloc(st_p5, "w0c", [128, 2, 8, 2, 128], FP8)
        wst = alloc(st_p5, "wst", [128, 2, 8, 2, 128], F32)
        w1c = alloc(st_p5, "w1c", [128, 2, 8, 2, 128], FP8)
        wlc = alloc(st_p5, "wlc", [128, 2, 16, 2, 128], FP8)
        gb = alloc(st_p5, "gb", [128, TS], BF16)
        f1b = AR[:, 131072:133120].bitcast(BF16)

        bar = barrier()
        P.new_epoch()
        for c in range(NCH):
            cw = bar if c == 0 else P.now("T")
            A((lambda c: lambda e: e.activation(
                out=gb[:], in_=xp[:, c, :], func=AF.Square))(c),
              waits=cw)
            for qb in range(2):
                T((lambda c, qb, st: lambda e: e.matmul(
                    pA[0:1, 512 * qb:512 * (qb + 1)], ones128b[:],
                    hiv(xp[:, c, 512 * qb:512 * (qb + 1)]),
                    start=st, stop=False, skip_group_check=True))(
                    c, qb, P.first_touch(("A", qb))), waits=bar if c == 0 else [])
            aw = P.now("A")
            for qb in range(2):
                T((lambda c, qb, st: lambda e: e.matmul(
                    pA[32:33, 512 * qb:512 * (qb + 1)], ones128b[:],
                    gb[:, 512 * qb:512 * (qb + 1)], start=st, stop=False,
                    skip_group_check=True))(c, qb, P.first_touch(("A32", qb))),
                  waits=aw)
        tw = P.now("T")
        V(lambda e: e.tensor_scalar(out=mrow[:, :], in0=pA[0:1, :], scalar1=1.0 / E,
                                    scalar2=0.0, op0=ALU.mult, op1=ALU.add), waits=tw)
        V(lambda e: e.tensor_scalar(out=rrow[:, :], in0=pA[32:33, :], scalar1=1.0 / E,
                                    scalar2=1e-5, op0=ALU.mult, op1=ALU.add))
        V(lambda e: e.scalar_tensor_tensor(out=scr1[:, :], in0=mrow[:, :], scalar=-1.0,
                                           in1=mrow[:, :], op0=ALU.mult, op1=ALU.mult))
        V(lambda e: e.tensor_tensor(out=rrow[:, :], in0=rrow[:, :], in1=scr1[:, :],
                                    op=ALU.add))
        V(lambda e: e.reciprocal(out=scr1[:, :], in_=rrow[:, :]))
        vw = P.now("V")
        A(lambda e: e.activation(out=scr1[:, :], in_=scr1[:, :], func=AF.Sqrt),
          waits=vw)
        aw = P.now("A")
        V(lambda e: e.tensor_tensor(out=scr2[:, :], in0=scr1[:, :], in1=scr1[:, :],
                                    op=ALU.mult), waits=aw)
        V(lambda e: e.tensor_tensor(out=scr2[:, :], in0=scr2[:, :], in1=rrow[:, :],
                                    op=ALU.mult))
        V(lambda e: e.tensor_scalar(out=scr2[:, :], in0=scr2[:, :], scalar1=-0.5,
                                    scalar2=1.5, op0=ALU.mult, op1=ALU.add))
        V(lambda e: e.tensor_tensor(out=rrow[:, :], in0=scr1[:, :], in1=scr2[:, :],
                                    op=ALU.mult))
        P.new_epoch()
        vw = P.now("V")
        for srcrow, dst in ((mrow, bcM), (rrow, bcR)):
            for qb in range(2):
                T((lambda srcrow, qb: lambda e: e.matmul(
                    pB[:, 512 * qb:512 * (qb + 1)], onesrow[:, :],
                    srcrow[:, 512 * qb:512 * (qb + 1)], start=True, stop=False,
                    skip_group_check=True))(srcrow, qb), waits=vw)
            tw = P.now("T")
            V((lambda dst: lambda e: e.tensor_copy(out=dst[:], in_=pB[:]))(dst), waits=tw)
            vw = P.now("V")
        for c in range(NCH):
            V((lambda c: lambda e: e.tensor_tensor(
                out=pB[:, :], in0=xp[:, c, :], in1=bcM[:], op=ALU.subtract))(c))
            V((lambda c: lambda e: e.tensor_tensor(
                out=x8[:, c, :], in0=pB[:, :], in1=bcR[:], op=ALU.mult))(c))
        flush()
        tap("x8", x8[:, 0, :].bitcast(BF16)[:, 0:512].bitcast(F32)[:, 0:128], 128, 128)

        P.new_epoch()
        sw = P.now("T", "V")
        done = {-2: sw, -1: sw}
        gelu_done = {}
        mm_done = {}
        castA = {}
        ldD = {}

        def g_dma(hc):
            buf = hc % 2
            l0 = S((lambda hc, buf: lambda e: e.dma_start(
                out=wst[:, buf, :, :, :],
                in_=wg[0, :, 128 * hc:128 * (hc + 1)].rearrange(
                    "(k j p) m -> p k j m", j=2, p=128)))(hc, buf),
                waits=castA.get(hc - 2, []))
            l1 = G((lambda hc, buf: lambda e: e.dma_start(
                out=w1c[:, buf, :, :, :],
                in_=wg[1, :, 128 * hc:128 * (hc + 1)].rearrange(
                    "(k j p) m -> p k j m", j=2, p=128)))(hc, buf),
                waits=mm_done.get(hc - 2, []))
            ldD[hc] = [l0, l1]
            A((lambda buf: lambda e: e.activation(
                out=w0c[:, buf, :, :, :].rearrange("p k j m -> p (k j m)"),
                in_=wst[:, buf, :, :, :].rearrange("p k j m -> p (k j m)"),
                func=AF.Copy))(buf),
              waits=[l0] + mm_done.get(hc - 2, []))
            castA[hc] = [("A", P.cnt["A"])]

        g_dma(0)
        g_dma(1)
        for hc in range(64):
            buf = hc % 2
            lw = [ldD[hc][1]] + castA[hc] + \
                gelu_done.get(hc - 2, (P.now("V") if hc < 2 else []))
            if buf == 0:
                psg = [(pA, 0), (pA, 512)]
                psf = [(pB, 0), (pB, 512)]
            else:
                psg = [(pC, 0), (pD, 0)]
                psf = [(pE, 0), (pF, 0)]
            for qq in range(4):
                psG, colG = psg[qq // 2]
                psF, colF = psf[qq // 2]
                cg = colG + 256 * (qq % 2)
                cf = colF + 256 * (qq % 2)
                for kc in range(8):
                    T((lambda buf, qq, kc, psG, cg, st: lambda e: e.matmul(
                        psG[:, cg:cg + 256], w0c[:, buf, kc, :, :],
                        x8[:, 2 * kc:2 * kc + 2, 256 * qq:256 * (qq + 1)],
                        start=st, stop=False, skip_group_check=True,
                        perf_mode=PM.DoubleRow))(buf, qq, kc, psG, cg,
                                                 (kc == 0 and qq % 2 == 0)), waits=lw)
                    T((lambda buf, qq, kc, psF, cf, st: lambda e: e.matmul(
                        psF[:, cf:cf + 256], w1c[:, buf, kc, :, :],
                        x8[:, 2 * kc:2 * kc + 2, 256 * qq:256 * (qq + 1)],
                        start=st, stop=False, skip_group_check=True,
                        perf_mode=PM.DoubleRow))(buf, qq, kc, psF, cf,
                                                 (kc == 0 and qq % 2 == 0)), waits=lw)
            mm_done[hc] = P.now("T")
            if hc + 2 < 64:
                g_dma(hc + 2)
            tw = mm_done[hc]
            gsrc = [(psg[0][0], psg[0][1]), (psg[1][0], psg[1][1])]
            fsrc = [(psf[0][0], psf[0][1]), (psf[1][0], psf[1][1])]
            V((lambda gsrc: lambda e: e.tensor_copy(
                out=gb[:, 0:512], in_=gsrc[0][0][:, gsrc[0][1]:gsrc[0][1] + 512]))(gsrc),
              waits=tw)
            V((lambda gsrc: lambda e: e.tensor_copy(
                out=gb[:, 512:1024], in_=gsrc[1][0][:, gsrc[1][1]:gsrc[1][1] + 512]))(gsrc))
            V((lambda fsrc: lambda e: e.tensor_copy(
                out=f1b[:, 0:512], in_=fsrc[0][0][:, fsrc[0][1]:fsrc[0][1] + 512]))(fsrc))
            V((lambda fsrc: lambda e: e.tensor_copy(
                out=f1b[:, 512:1024], in_=fsrc[1][0][:, fsrc[1][1]:fsrc[1][1] + 512]))(fsrc))
            vw = P.now("V")
            A(lambda e: e.activation(out=gb[:], in_=gb[:], func=AF.Gelu_apprx_tanh),
              waits=vw)
            aw = P.now("A")
            V(lambda e: e.tensor_tensor(out=gb[:], in0=gb[:], in1=f1b[:],
                                        op=ALU.mult), waits=aw)
            V((lambda hc: lambda e: e.tensor_copy(out=act8[:, hc, :], in_=gb[:]))(hc))
            gelu_done[hc] = P.now("V")
            done[hc] = P.now("T", "V")
        flush()

        outsW = x8[:, 0:4, :].rearrange("p a b -> p (a b)").bitcast(F32)
        P.new_epoch()
        sw = P.now("T", "V")
        done = {-2: sw, -1: sw}
        outdma = {}
        castL = {}
        ldL = {}
        fcV = {}

        def wl_dma(eo):
            buf = eo % 2
            w = done.get(eo - 2, [])
            ld1 = G((lambda eo, buf: lambda e: e.dma_start(
                out=wlc[:, buf, 0:16, :, :],
                in_=wlin[0:4096, 128 * eo:128 * (eo + 1)].rearrange(
                    "(k j p) m -> p k j m", j=2, p=128)))(eo, buf), waits=w)
            ld2 = G((lambda eo, buf: lambda e: e.dma_start(
                out=w0c[:, buf, :, :, :],
                in_=wlin[4096:6144, 128 * eo:128 * (eo + 1)].rearrange(
                    "(k j p) m -> p k j m", j=2, p=128)))(eo, buf))
            lr = S((lambda eo, buf: lambda e: e.dma_start(
                out=wst[:, buf, :, :, :],
                in_=wlin[6144:8192, 128 * eo:128 * (eo + 1)].rearrange(
                    "(k j p) m -> p k j m", j=2, p=128)))(eo, buf),
                waits=castL.get(eo - 2, castA.get(62 + eo, [])))
            ldL[eo] = [ld1, ld2]
            A((lambda buf: lambda e: e.activation(
                out=w1c[:, buf, :, :, :].rearrange("p k j m -> p (k j m)"),
                in_=wst[:, buf, :, :, :].rearrange("p k j m -> p (k j m)"),
                func=AF.Copy))(buf),
              waits=[lr] + w)
            castL[eo] = [("A", P.cnt["A"])]

        wl_dma(0)
        for eo in range(NCH):
            buf = eo % 2
            if eo + 1 < NCH:
                wl_dma(eo + 1)
            lw = ldL[eo] + castL[eo] + fcV.get(eo - 2, [])
            ps = pC if eo % 2 == 0 else pD
            for qq in range(4):
                w_mm = lw + (P.now("V") if qq >= 2 else [])
                for kc in range(32):
                    st = P.first_touch((PSNAME[id(ps)], 0)) if (qq % 2 == 0 and kc == 0) \
                        else False
                    wsrc = (wlc[:, buf, kc, :, :] if kc < 16 else
                            (w0c if kc < 24 else w1c)[:, buf, kc % 8, :, :])
                    T((lambda ps, buf, qq, kc, st, wsrc: lambda e: e.matmul(
                        ps[:, 256 * (qq % 2):256 * (qq % 2) + 256],
                        wsrc,
                        act8[:, 2 * kc:2 * kc + 2, 256 * qq:256 * (qq + 1)],
                        start=st, stop=False, skip_group_check=True,
                        perf_mode=PM.DoubleRow))(ps, buf, qq, kc, st, wsrc),
                      waits=w_mm)
                if qq % 2 == 1:
                    tw = P.now("T")
                    hb = qq // 2
                    ow = [outdma[eo - 1]] if eo - 1 in outdma else []
                    V((lambda ps: lambda e: e.tensor_copy(
                        out=f1b[:, 0:512], in_=ps[:, 0:512]))(ps), waits=tw)
                    V((lambda eo, hb: lambda e: e.tensor_tensor(
                        out=outsW[:, 512 * hb:512 * (hb + 1)],
                        in0=xp[:, eo, 512 * hb:512 * (hb + 1)], in1=f1b[:, 0:512],
                        op=ALU.add))(eo, hb), waits=ow)
                    P.new_epoch()
            vw = P.now("V")
            fcV[eo] = vw
            done[eo] = P.now("T")
            outdma[eo] = S((lambda eo: lambda e: e.dma_start(
                out=outT[128 * eo:128 * (eo + 1), :], in_=outsW[:, :]))(eo),
                waits=vw)
        fin = P.now("dS", "dS2")
        P.emit("gpsimd", lambda e: e.memset(scr2[0:1, 0:1], 0.0), fin, None)
        flush()

        for st in (st_p5, st_per):
            st.close()
    return nc


_MASK_CACHE = {}


def _masks(pi):
    if pi in _MASK_CACHE:
        return _MASK_CACHE[pi]
    s = np.arange(128)[:, None]
    q = np.arange(128)[None, :]
    mD_add = np.where(s > q, np.float32(-1e30), np.float32(0.0))
    mDB = ((q - s >= 0) & (q - s <= 64)).astype(np.float32)
    sub = (s - q >= 64).astype(np.float32)
    if pi == 0:
        mO_add = np.full((128, 128), -1e30, np.float32)
        mOB = np.concatenate([np.zeros((128, 128), np.float32), sub], axis=1)
    else:
        mO_add = np.zeros((128, 128), np.float32)
        mOB = np.concatenate([sub, np.zeros((128, 128), np.float32)], axis=1)
    out = (mD_add, mO_add, mDB.astype(np.float32), mOB)
    _MASK_CACHE[pi] = out
    return out


_PROG = None


def kernel(L0_attn_hidden_states, L0_attn_Wq, L0_attn_Wk, L0_attn_Wv,
           L0_attn_q_to_latent, L0_attn_v_up, L0_attn_Wo, L0_ffn_w_gating,
           L0_ffn_w_linear):
    global _PROG
    x = np.ascontiguousarray(L0_attn_hidden_states, dtype=np.float32)
    wq = np.ascontiguousarray(L0_attn_Wq, dtype=np.float32)
    wqT = np.ascontiguousarray(wq.T)
    wk = np.ascontiguousarray(L0_attn_Wk, dtype=np.float32)
    wv = np.ascontiguousarray(L0_attn_Wv, dtype=np.float32)
    wkT = np.ascontiguousarray(wk.T)
    wvT = np.ascontiguousarray(wv.T)
    q2l = np.ascontiguousarray(L0_attn_q_to_latent, dtype=np.float32)
    vupT = np.ascontiguousarray(np.transpose(L0_attn_v_up, (0, 2, 1)))
    woT = np.ascontiguousarray(np.asarray(L0_attn_Wo, dtype=np.float32).T)
    wg = np.ascontiguousarray(L0_ffn_w_gating, dtype=np.float32)
    wlin = np.ascontiguousarray(L0_ffn_w_linear, dtype=np.float32)
    idn = np.eye(128, dtype=np.float32)

    if _PROG is None:
        _PROG = build_program()
    nc = _PROG

    in_maps = []
    for c in range(8):
        b, pi = c // 2, c % 2
        own = np.concatenate([np.arange(128 * (2 * i + pi), 128 * (2 * i + pi) + 128)
                              for i in range(8)])
        oth = np.concatenate([np.arange(128 * (2 * i + 1 - pi),
                                        128 * (2 * i + 1 - pi) + 128)
                              for i in range(8)])
        xoT = np.ascontiguousarray(x[b][own].T)
        xrT = np.ascontiguousarray(x[b][oth].T)
        mD_add, mO_add, mDB, mOB = _masks(pi)
        in_maps.append({
            "xoT": xoT, "xrT": xrT, "wqT": wqT, "q2l": q2l, "wkT": wkT, "wvT": wvT,
            "vupT": vupT, "woT": woT, "wg": wg, "wlin": wlin, "idn": idn,
            "mD_add": mD_add, "mO_add": mO_add, "mDB_mult": mDB, "mOB_mult": mOB,
        })
    res = run_bass_kernel_spmd(nc, in_maps, core_ids=list(range(8)))
    out = np.empty((4, 2048, 2048), dtype=np.float32)
    for c in range(8):
        b, pi = c // 2, c % 2
        own = np.concatenate([np.arange(128 * (2 * i + pi), 128 * (2 * i + pi) + 128)
                              for i in range(8)])
        out[b][own] = res.results[c]["outT"].T
    return out


# version 12
# speedup vs baseline: 1.1016x; 1.0433x over previous
import numpy as np
import contextlib
import concourse.bass as bass
import concourse.mybir as mybir
from concourse.bass_utils import run_bass_kernel_spmd

F32 = mybir.dt.float32
F32R = mybir.dt.float32r
BF16 = mybir.dt.bfloat16
FP8 = mybir.dt.float8e4
AF = mybir.ActivationFunctionType
ALU = mybir.AluOpType
PM = mybir.MatmulPerfMode
AX = mybir.AxisListType

E = 2048
TS = 1024
H, L = 16, 64
HID = 8192
NCH = 16
INV = float(1.0 / np.sqrt(L))
RAH = float(0.5 / np.sqrt(L))
PAD = 78.0

ENGS = ("sync", "gpsimd", "tensor", "vector", "scalar")


class Plan:
    def __init__(self):
        self.pend = {e: [] for e in ENGS}
        self.cnt = dict(dS=0, dS2=0, dG=0, dG2=0, T=0, V=0, A=0, PL=0)
        self.rr = {"sync": 0, "gpsimd": 0}
        self.bank_started = {}
        self.barrier_waits = []
        self.barrier_pending = {e: False for e in ENGS}

    def now(self, *sems):
        return [(s, self.cnt[s]) for s in sems]

    def set_barrier(self):
        self.barrier_waits = [(s, self.cnt[s]) for s in self.cnt]
        self.barrier_pending = {e: True for e in ENGS}

    def emit(self, eng, fn, waits=(), inc=None, amt=None):
        if amt is None:
            amt = 16 if inc in ("dS", "dG", "dS2", "dG2") else 1
        w = list(waits)
        if inc in ("dS", "dG", "dS2", "dG2"):
            w = [(inc, self.cnt[inc])] + w
        if eng == "vector":
            w = [("V", self.cnt["V"])] + w
        elif eng == "scalar":
            w = [("A", self.cnt["A"])] + w
        elif eng == "gpsimd" and inc == "PL":
            w = [("PL", self.cnt["PL"])] + w
        if self.barrier_pending[eng]:
            w = self.barrier_waits + w
            self.barrier_pending[eng] = False
        self.pend[eng].append((fn, w, inc, amt))
        if inc is not None:
            self.cnt[inc] += amt
        return self.cnt.get(inc)

    def first_touch(self, bank_key):
        if bank_key in self.bank_started:
            return False
        self.bank_started[bank_key] = True
        return True

    def new_epoch(self):
        self.bank_started = {}


def build_program(debug=()):
    nc = bass.Bass("TRN2", target_bir_lowering=False, debug=False)
    dt = nc.dram_tensor
    xoT = dt("xoT", [E, TS], F32, kind="ExternalInput").ap()
    xrT = dt("xrT", [E, TS], F32, kind="ExternalInput").ap()
    wqT = dt("wqT", [E, E], F32, kind="ExternalInput").ap()
    q2li = dt("q2l", [H, 128, L], F32, kind="ExternalInput").ap()
    wkT = dt("wkT", [E, L], F32, kind="ExternalInput").ap()
    wvT = dt("wvT", [E, L], F32, kind="ExternalInput").ap()
    vupTi = dt("vupT", [H, 128, L], F32, kind="ExternalInput").ap()
    woT = dt("woT", [E, E], F32, kind="ExternalInput").ap()
    wg = dt("wg", [2, E, HID], F32, kind="ExternalInput").ap()
    wlin = dt("wlin", [HID, E], F32, kind="ExternalInput").ap()
    idn = dt("idn", [128, 128], F32, kind="ExternalInput").ap()
    mD_add = dt("mD_add", [128, 128], F32, kind="ExternalInput").ap()
    mO_add = dt("mO_add", [128, 128], F32, kind="ExternalInput").ap()
    mDB_mult = dt("mDB_mult", [128, 128], F32, kind="ExternalInput").ap()
    mOB_mult = dt("mOB_mult", [128, 256], F32, kind="ExternalInput").ap()
    outT = dt("outT", [E, TS], F32, kind="ExternalOutput").ap()
    dbg_out = {}
    for name, shape in debug:
        dbg_out[name] = dt("dbg_" + name, shape, F32, kind="ExternalOutput").ap()

    P = Plan()
    st_per = contextlib.ExitStack()
    st_kv = contextlib.ExitStack()
    st_p1 = contextlib.ExitStack()
    st_p2 = contextlib.ExitStack()
    st_p3 = contextlib.ExitStack()
    st_p4 = contextlib.ExitStack()
    st_p5 = contextlib.ExitStack()

    def hiv(ap_f32):
        return ap_f32.bitcast(BF16)[..., 1::2]

    with (
        nc.psum_tensor("pA", [128, 1024], F32) as pA,
        nc.psum_tensor("pB", [128, 1024], F32) as pB,
        nc.psum_tensor("pC", [128, 512], F32) as pC,
        nc.psum_tensor("pD", [128, 512], F32) as pD,
        nc.psum_tensor("pE", [128, 512], F32) as pE,
        nc.psum_tensor("pF", [128, 512], F32) as pF,
        nc.semaphore("dS") as sem_dS,
        nc.semaphore("dS2") as sem_dS2,
        nc.semaphore("dG") as sem_dG,
        nc.semaphore("dG2") as sem_dG2,
        nc.semaphore("T") as sem_T,
        nc.semaphore("V") as sem_V,
        nc.semaphore("A") as sem_A,
        nc.semaphore("PL") as sem_PL,
        nc.Block() as block,
    ):
        sems = {"dS": sem_dS, "dS2": sem_dS2, "dG": sem_dG, "dG2": sem_dG2,
                "T": sem_T, "V": sem_V, "A": sem_A, "PL": sem_PL}
        wms = {e: {} for e in ENGS}
        PSNAME = {id(pA): "A", id(pB): "B", id(pC): "C", id(pD): "D",
                  id(pE): "E", id(pF): "F"}

        def S(fn, waits=()):
            P.rr["sync"] ^= 1
            sem = "dS" if P.rr["sync"] else "dS2"
            return (sem, P.emit("sync", fn, waits, sem))

        def G(fn, waits=()):
            P.rr["gpsimd"] ^= 1
            sem = "dG" if P.rr["gpsimd"] else "dG2"
            return (sem, P.emit("gpsimd", fn, waits, sem))

        def T(fn, waits=()):
            return P.emit("tensor", fn, waits, "T")

        def GC(fn, waits=()):
            return P.emit("gpsimd", fn, waits, "PL")

        def V(fn, waits=()):
            return P.emit("vector", fn, waits, "V")

        def A(fn, waits=()):
            return P.emit("scalar", fn, waits, "A")

        def flush():
            for name in ENGS:
                ops = P.pend[name]
                if not ops:
                    continue
                P.pend[name] = []
                wm = wms[name]

                def body(eng, ops=ops, wm=wm):
                    for fn, waits, inc, amt in ops:
                        for s, v in waits:
                            if v > 0 and wm.get(s, 0) < v:
                                eng.wait_ge(sems[s], v)
                                wm[s] = v
                        ins = fn(eng)
                        if inc is not None:
                            ins.then_inc(sems[inc], amt)
                getattr(block, name)(body)

        def barrier():
            return P.now("dS", "dS2", "dG", "dG2", "T", "V", "A", "PL")

        def tap(name, src_ap, rows, cols):
            if name in dbg_out:
                w = barrier()
                G((lambda name, src_ap, rows, cols: lambda e: e.dma_start(
                    out=dbg_out[name][0:rows, 0:cols], in_=src_ap))(name, src_ap, rows, cols),
                  waits=w)

        def alloc(st, name, shape, dtype):
            return st.enter_context(nc.sbuf_tensor(name, shape, dtype))

        ones128 = alloc(st_per, "ones128", [128, 1], F32)
        ones128b = alloc(st_per, "ones128b", [128, 1], BF16)
        onesrow = alloc(st_per, "onesrow", [1, 128], F32)
        m65 = alloc(st_per, "m65", [65, 1], BF16)
        mDadd = alloc(st_per, "mDadd", [128, 128], BF16)
        mOadd = alloc(st_per, "mOadd", [128, 128], BF16)
        mDBm = alloc(st_per, "mDBm", [128, 128], BF16)
        mOBm = alloc(st_per, "mOBm", [128, 256], BF16)
        bcM = alloc(st_per, "bcM", [128, TS], F32)
        bcR = alloc(st_per, "bcR", [128, TS], F32)
        srow = alloc(st_per, "srow", [16, TS], F32)
        d128 = alloc(st_per, "d128", [128, 128], F32)
        q128s = alloc(st_per, "q128s", [128, 128], F32)
        b128 = alloc(st_per, "b128", [128, 128], F32)
        s1s = alloc(st_per, "s1s", [128, 128], F32)
        AR = alloc(st_per, "AR", [128, 133120], mybir.dt.uint8)
        xho_hi = AR[:, 0:32768].bitcast(BF16).rearrange("p (c t) -> p c t", c=NCH)
        xho_lo = AR[:, 32768:65536].bitcast(BF16).rearrange("p (c t) -> p c t", c=NCH)
        ctxPK1 = AR[:, 0:32768].bitcast(BF16).rearrange("p (c t) -> p c t", c=H)
        ctxPK2 = AR[:, 32768:65536].bitcast(BF16).rearrange("p (c t) -> p c t", c=H)
        ctxh = ctxPK1[0:64]
        ctxl = ctxPK2[0:64]
        act8 = AR[:, 0:65536].bitcast(FP8).rearrange("p (c t) -> p c t", c=64)
        qhPK = AR[:, 65536:98304].bitcast(BF16).rearrange("p (c t) -> p c t", c=H)
        qh65 = qhPK[0:65]
        ql65 = AR[:, 98304:131072].bitcast(BF16).rearrange("p (c t) -> p c t", c=H)[0:65]
        qsum3p = AR[:, 131072:133120].bitcast(BF16)[0:48]
        xp = AR[:, 65536:131072].bitcast(F32).rearrange("p (c t) -> p c t", c=NCH)
        kh65 = alloc(st_kv, "kh65", [65, 2 * TS], BF16)
        khl = alloc(st_kv, "khl", [128, 2 * TS], BF16)
        onesP64 = alloc(st_kv, "onesP64", [65, 64], F32)
        vtmh = alloc(st_kv, "vtmh", [128, 16, 65], BF16)
        vtml = alloc(st_kv, "vtml", [128, 16, 65], BF16)
        konT = alloc(st_kv, "konT", [65, TS], F32)
        ksc3 = alloc(st_kv, "ksc3", [3, 2 * TS], BF16)
        kstat = alloc(st_kv, "kstat", [128, 32], BF16)
        wmx16 = alloc(st_kv, "wmx16", [128, 128], F32)
        wmn16 = alloc(st_kv, "wmn16", [128, 128], F32)
        wrow = alloc(st_kv, "wrow", [1, 48], F32)
        idns = alloc(st_kv, "idns", [128, 128], F32)
        segm = alloc(st_kv, "segm", [1, 64], F32)

        G(lambda e: e.dma_start(out=mDadd[:], in_=mD_add[:, :]))
        G(lambda e: e.dma_start(out=mOadd[:], in_=mO_add[:, :]))
        G(lambda e: e.dma_start(out=mDBm[:], in_=mDB_mult[:, :]))
        G(lambda e: e.dma_start(out=mOBm[:], in_=mOB_mult[:, :]))
        c0 = P.now("dS", "dS2", "dG", "dG2")
        V(lambda e: e.memset(ones128[:], 1.0), waits=c0)
        V(lambda e: e.memset(ones128b[:], 1.0))
        V(lambda e: e.memset(onesrow[:], 1.0))
        V(lambda e: e.memset(m65[0:64, :], 1.0))
        V(lambda e: e.memset(m65[64:65, :], 0.5))
        V(lambda e: e.memset(onesP64[64:65, :], 1.0))
        P.set_barrier()

        kvT = AR[:, 65536:73728].bitcast(F32)
        kvTlo = AR[:, 73728:77824].bitcast(BF16)
        kscrow = AR[:, 77824:86016].bitcast(F32)[0:1]
        xc = AR[:, 86016:94208].bitcast(F32).rearrange("p (a t) -> p a t", a=2)
        kminb = AR[:, 94208:94272].bitcast(BF16)[0:64]
        tr1 = AR[:, 94272:98368].bitcast(BF16)[0:1]
        mrow = alloc(st_p1, "mrow", [1, TS], F32)
        rrow = alloc(st_p1, "rrow", [1, TS], F32)
        scr1 = alloc(st_p1, "scr1", [1, TS], F32)
        scr2 = alloc(st_p1, "scr2", [1, TS], F32)
        sq = alloc(st_p1, "sq", [128, TS], F32)
        uu = sq
        xrh = alloc(st_p1, "xrh", [128, 2, TS], BF16)
        xrl = alloc(st_p1, "xrl", [128, 2, TS], BF16)
        kvW = AR[:, 98368:106560].bitcast(F32).rearrange("p (c l) -> p c l", c=NCH)
        kvWlo = AR[:, 106560:110656].bitcast(BF16).rearrange("p (c l) -> p c l", c=NCH)
        kW65 = AR[:, 110656:114816].bitcast(F32).rearrange("p (c l) -> p c l", c=NCH)
        kW65lo = AR[:, 114816:116896].bitcast(BF16).rearrange("p (c l) -> p c l", c=NCH)

        S(lambda e: e.dma_start(out=idns[:], in_=idn[:, :]))
        S(lambda e: e.dma_start(out=kvW[:, :, 0:64],
                                in_=wkT.rearrange("(c p) l -> p c l", p=128)))
        S(lambda e: e.dma_start(out=kvW[:, :, 64:128],
                                in_=wvT.rearrange("(c p) l -> p c l", p=128)))
        S(lambda e: e.dma_start(out=kW65[:, :, 0:64],
                                in_=wkT.rearrange("(c p) l -> p c l", p=128)))
        wl_ = P.now("dS", "dS2")
        V(lambda e: e.tensor_tensor(out=kvWlo[:].rearrange("p c l -> p (c l)"),
                                    in0=kvW[:].rearrange("p c l -> p (c l)"),
                                    in1=hiv(kvW[:].rearrange("p c l -> p (c l)")),
                                    op=ALU.subtract), waits=wl_)
        for c in range(NCH):
            V((lambda c: lambda e: e.tensor_reduce(
                out=kW65[:, c, 64:65], in_=kW65[:, c, 0:64], axis=AX.X, op=ALU.add))(c),
              waits=wl_)
        V(lambda e: e.tensor_tensor(out=kW65lo[:].rearrange("p c l -> p (c l)"),
                                    in0=kW65[:].rearrange("p c l -> p (c l)"),
                                    in1=hiv(kW65[:].rearrange("p c l -> p (c l)")),
                                    op=ALU.subtract))

        def ln_rows(src_dram):
            P.new_epoch()
            sw = P.now("T", "V", "A")
            done = {-2: sw, -1: sw}
            for c in range(NCH):
                buf = c % 2
                ld = S((lambda c, buf: lambda e: e.dma_start(
                    out=xc[:, buf, :], in_=src_dram[128 * c:128 * (c + 1), :]))(c, buf),
                    waits=done.get(c - 2, []))
                lw = [ld]
                A((lambda buf: lambda e: e.activation(
                    out=sq[:].bitcast(BF16)[:, 0:TS], in_=xc[:, buf, :],
                    func=AF.Square))(buf),
                  waits=lw + (P.now("T") if c > 0 else []))
                for qb in range(2):
                    T((lambda c, buf, qb, st: lambda e: e.matmul(
                        pA[0:1, 512 * qb:512 * (qb + 1)], ones128b[:],
                        hiv(xc[:, buf, 512 * qb:512 * (qb + 1)]),
                        start=st, stop=False, skip_group_check=True))(
                        c, buf, qb, P.first_touch(("A", qb))), waits=lw)
                aw = P.now("A")
                for qb in range(2):
                    T((lambda qb, st: lambda e: e.matmul(
                        pA[32:33, 512 * qb:512 * (qb + 1)], ones128b[:],
                        sq[:].bitcast(BF16)[:, 512 * qb:512 * (qb + 1)],
                        start=st, stop=False, skip_group_check=True))(
                        qb, P.first_touch(("A32", qb))), waits=aw)
                done[c] = P.now("T", "A")
            tw = P.now("T")
            V(lambda e: e.tensor_scalar(out=mrow[:, :], in0=pA[0:1, :], scalar1=1.0 / E,
                                        scalar2=0.0, op0=ALU.mult, op1=ALU.add), waits=tw)
            V(lambda e: e.tensor_scalar(out=rrow[:, :], in0=pA[32:33, :],
                                        scalar1=1.0 / E, scalar2=1e-5,
                                        op0=ALU.mult, op1=ALU.add))
            V(lambda e: e.scalar_tensor_tensor(out=scr1[:, :], in0=mrow[:, :],
                                               scalar=-1.0, in1=mrow[:, :],
                                               op0=ALU.mult, op1=ALU.mult))
            V(lambda e: e.tensor_tensor(out=rrow[:, :], in0=rrow[:, :], in1=scr1[:, :],
                                        op=ALU.add))
            V(lambda e: e.reciprocal(out=scr1[:, :], in_=rrow[:, :]))
            vw = P.now("V")
            A(lambda e: e.activation(out=scr1[:, :], in_=scr1[:, :], func=AF.Sqrt),
              waits=vw)
            aw = P.now("A")
            V(lambda e: e.tensor_tensor(out=scr2[:, :], in0=scr1[:, :], in1=scr1[:, :],
                                        op=ALU.mult), waits=aw)
            V(lambda e: e.tensor_tensor(out=scr2[:, :], in0=scr2[:, :], in1=rrow[:, :],
                                        op=ALU.mult))
            V(lambda e: e.tensor_scalar(out=scr2[:, :], in0=scr2[:, :], scalar1=-0.5,
                                        scalar2=1.5, op0=ALU.mult, op1=ALU.add))
            V(lambda e: e.tensor_tensor(out=rrow[:, :], in0=scr1[:, :], in1=scr2[:, :],
                                        op=ALU.mult))
            P.new_epoch()
            vw = P.now("V")
            for srcrow, dst in ((mrow, bcM), (rrow, bcR)):
                for qb in range(2):
                    T((lambda srcrow, qb, st: lambda e: e.matmul(
                        pB[:, 512 * qb:512 * (qb + 1)], onesrow[:, :],
                        srcrow[:, 512 * qb:512 * (qb + 1)], start=st, stop=False,
                        skip_group_check=True))(srcrow, qb, True), waits=vw)
                tw = P.now("T")
                V((lambda dst: lambda e: e.tensor_copy(out=dst[:], in_=pB[:]))(dst),
                  waits=tw)
                vw = P.now("V")

        def ln_apply(src_dram, hi_dst, lo_dst, mm_fn):
            sw = P.now("T", "V", "A")
            done = {-2: sw, -1: sw}
            for c in range(NCH):
                buf = c % 2
                ld = S((lambda c, buf: lambda e: e.dma_start(
                    out=xc[:, buf, :], in_=src_dram[128 * c:128 * (c + 1), :]))(c, buf),
                    waits=done.get(c - 2, []))
                V((lambda buf: lambda e: e.tensor_tensor(
                    out=uu[:], in0=xc[:, buf, :], in1=bcM[:], op=ALU.subtract))(buf),
                  waits=[ld])
                V(lambda e: e.tensor_tensor(out=uu[:], in0=uu[:], in1=bcR[:],
                                            op=ALU.mult))
                hd, ldst = hi_dst(c), lo_dst(c)
                V((lambda hd: lambda e: e.tensor_copy(out=hd, in_=uu[:]))(hd))
                V((lambda hd, ldst: lambda e: e.tensor_tensor(
                    out=ldst, in0=uu[:], in1=hd, op=ALU.subtract))(hd, ldst))
                mm_fn(c)
                done[c] = P.now("T", "V")

        ln_rows(xoT)
        P.new_epoch()

        def latent_own_mm(c):
            vw = P.now("V")
            for j in range(8):
                ps, col = (pC, 128 * j) if j < 4 else (pD, 128 * (j - 4))
                for pi, (wap, xap) in enumerate((
                        (hiv(kvW[:, c, :]), xho_hi[:, c, 128 * j:128 * (j + 1)]),
                        (hiv(kvW[:, c, :]), xho_lo[:, c, 128 * j:128 * (j + 1)]),
                        (kvWlo[:, c, :], xho_hi[:, c, 128 * j:128 * (j + 1)]))):
                    st = P.first_touch((PSNAME[id(ps)], col // 512))
                    T((lambda ps, col, wap, xap, st: lambda e: e.matmul(
                        ps[:, col:col + 128], wap, xap, start=st, stop=False,
                        skip_group_check=True))(ps, col, wap, xap, st), waits=vw)
            for qb in range(2):
                ps = pE if qb == 0 else pF
                for pi, (wap, xap) in enumerate((
                        (hiv(kW65[:, c, :]), xho_hi[:, c, 512 * qb:512 * (qb + 1)]),
                        (hiv(kW65[:, c, :]), xho_lo[:, c, 512 * qb:512 * (qb + 1)]),
                        (kW65lo[:, c, :], xho_hi[:, c, 512 * qb:512 * (qb + 1)]))):
                    st = P.first_touch((PSNAME[id(ps)], 0))
                    T((lambda ps, wap, xap, st: lambda e: e.matmul(
                        ps[0:65, :], wap, xap, start=st, stop=False,
                        skip_group_check=True))(ps, wap, xap, st), waits=vw)

        ln_apply(xoT, lambda c: xho_hi[:, c, :], lambda c: xho_lo[:, c, :], latent_own_mm)
        tw = P.now("T")
        for j in range(8):
            ps, col = (pC, 128 * j) if j < 4 else (pD, 128 * (j - 4))
            V((lambda j, ps, col: lambda e: e.tensor_scalar(
                out=kvT[0:64, 128 * j:128 * (j + 1)], in0=ps[0:64, col:col + 128],
                scalar1=INV, scalar2=0.0, op0=ALU.mult, op1=ALU.add))(j, ps, col),
              waits=tw)
            V((lambda j, ps, col: lambda e: e.tensor_copy(
                out=kvT[64:128, 128 * j:128 * (j + 1)],
                in_=ps[64:128, col:col + 128]))(j, ps, col))
        V(lambda e: e.tensor_scalar(out=konT[:, 0:512], in0=pE[0:65, :], scalar1=INV,
                                    scalar2=0.0, op0=ALU.mult, op1=ALU.add), waits=tw)
        V(lambda e: e.tensor_scalar(out=konT[:, 512:1024], in0=pF[0:65, :], scalar1=INV,
                                    scalar2=0.0, op0=ALU.mult, op1=ALU.add))

        ln_rows(xrT)
        P.new_epoch()

        def latent_oth_mm(c):
            vw = P.now("V")
            buf = c % 2
            for j in range(8):
                ps, col = (pC, 128 * j) if j < 4 else (pD, 128 * (j - 4))
                for pi, (wap, xap) in enumerate((
                        (hiv(kvW[:, c, :]), xrh[:, buf, 128 * j:128 * (j + 1)]),
                        (hiv(kvW[:, c, :]), xrl[:, buf, 128 * j:128 * (j + 1)]),
                        (kvWlo[:, c, :], xrh[:, buf, 128 * j:128 * (j + 1)]))):
                    st = P.first_touch((PSNAME[id(ps)], col // 512))
                    T((lambda ps, col, wap, xap, st: lambda e: e.matmul(
                        ps[:, col:col + 128], wap, xap, start=st, stop=False,
                        skip_group_check=True))(ps, col, wap, xap, st), waits=vw)

        ln_apply(xrT, lambda c: xrh[:, c % 2, :], lambda c: xrl[:, c % 2, :], latent_oth_mm)
        tw = P.now("T")
        for j in range(8):
            ps, col = (pC, 128 * j) if j < 4 else (pD, 128 * (j - 4))
            V((lambda j, ps, col: lambda e: e.tensor_scalar(
                out=kvT[0:64, 1024 + 128 * j:1024 + 128 * (j + 1)],
                in0=ps[0:64, col:col + 128], scalar1=INV, scalar2=0.0,
                op0=ALU.mult, op1=ALU.add))(j, ps, col), waits=tw)
            V((lambda j, ps, col: lambda e: e.tensor_copy(
                out=kvT[64:128, 1024 + 128 * j:1024 + 128 * (j + 1)],
                in_=ps[64:128, col:col + 128]))(j, ps, col))

        V(lambda e: e.tensor_tensor(out=kvTlo[:], in0=kvT[:], in1=hiv(kvT[:]),
                                    op=ALU.subtract))
        V(lambda e: e.tensor_copy(out=kh65[0:64, :], in_=hiv(kvT[0:64, :])))
        V(lambda e: e.memset(kh65[64:65, :], 1.0))
        V(lambda e: e.tensor_copy(out=khl[0:64, :], in_=hiv(kvT[0:64, :])))
        vwk = P.now("V")
        G(lambda e: e.dma_start(out=khl[64:128, :], in_=kvTlo[0:64, :]), waits=vwk)
        P.new_epoch()
        vw = P.now("V")
        for qq in range(4):
            ps = pA if qq < 2 else pB
            col = 512 * (qq % 2)
            for half in range(2):
                src = (hiv(kvT[0:64, 512 * qq:512 * (qq + 1)]) if half == 0
                       else kvTlo[0:64, 512 * qq:512 * (qq + 1)])
                st = P.first_touch((PSNAME[id(ps)], col // 512))
                T((lambda ps, col, src, st: lambda e: e.matmul(
                    ps[0:1, col:col + 512], ones128b[0:64, :], src, start=st,
                    stop=False, skip_group_check=True))(ps, col, src, st), waits=vw)
        tw = P.now("T")
        V(lambda e: e.tensor_scalar(out=kscrow[0:1, 0:1024], in0=pA[0:1, :], scalar1=0.5,
                                    scalar2=0.0, op0=ALU.mult, op1=ALU.add), waits=tw)
        V(lambda e: e.tensor_scalar(out=kscrow[0:1, 1024:2048], in0=pB[0:1, :],
                                    scalar1=0.5, scalar2=0.0, op0=ALU.mult, op1=ALU.add))
        V(lambda e: e.tensor_copy(out=ksc3[0:1, :], in_=hiv(kscrow[:])))
        V(lambda e: e.tensor_tensor(out=tr1[0:1, :], in0=kscrow[:], in1=hiv(kscrow[:]),
                                    op=ALU.subtract))
        vw = P.now("V")
        G(lambda e: e.dma_start(out=ksc3[1:2, :], in_=tr1[0:1, :]), waits=vw)
        G(lambda e: e.dma_start(out=ksc3[2:3, :], in_=ksc3[0:1, :]), waits=vw)
        V(lambda e: e.tensor_reduce(out=segm[0:1, 0:32],
                                    in_=kscrow[:].rearrange("o (s w) -> o s w", w=64),
                                    axis=AX.X, op=ALU.max))
        V(lambda e: e.tensor_reduce(out=segm[0:1, 32:64],
                                    in_=kscrow[:].rearrange("o (s w) -> o s w", w=64),
                                    axis=AX.X, op=ALU.min))
        for r, op in ((0, ALU.max), (1, ALU.min)):
            o = 24 * r
            b = 32 * r
            V((lambda op, o, b: lambda e: e.tensor_tensor(
                out=wrow[0:1, o:o + 8], in0=segm[0:1, b + 0:b + 16:2],
                in1=segm[0:1, b + 1:b + 16:2], op=op))(op, o, b))
            V((lambda op, o, b: lambda e: e.tensor_tensor(
                out=wrow[0:1, o + 8:o + 16], in0=segm[0:1, b + 16:b + 32:2],
                in1=segm[0:1, b + 17:b + 32:2], op=op))(op, o, b))
            V((lambda op, o, b: lambda e: e.tensor_tensor(
                out=wrow[0:1, o + 16:o + 24], in0=segm[0:1, b + 14:b + 30:2],
                in1=segm[0:1, b + 15:b + 30:2], op=op))(op, o, b))
            V((lambda op, o: lambda e: e.tensor_tensor(
                out=wrow[0:1, o:o + 8], in0=wrow[0:1, o:o + 8],
                in1=wrow[0:1, o + 8:o + 16], op=op))(op, o))
            V((lambda op, o: lambda e: e.tensor_tensor(
                out=wrow[0:1, o:o + 8], in0=wrow[0:1, o:o + 8],
                in1=wrow[0:1, o + 16:o + 24], op=op))(op, o))
        P.new_epoch()
        vw = P.now("V")
        T((lambda st: lambda e: e.matmul(pC[:, 0:8], onesrow[:, :], wrow[0:1, 0:8],
                                         start=st, stop=False, skip_group_check=True))(
            P.first_touch(("C", 0))), waits=vw)
        T(lambda e: e.matmul(pC[:, 8:16], onesrow[:, :], wrow[0:1, 24:32], start=False,
                             stop=False, skip_group_check=True))
        tw = P.now("T")
        for h in range(H):
            V((lambda h: lambda e: e.tensor_copy(out=wmx16[:, 8 * h:8 * h + 8],
                                                 in_=pC[:, 0:8]))(h), waits=tw)
            V((lambda h: lambda e: e.tensor_copy(out=wmn16[:, 8 * h:8 * h + 8],
                                                 in_=pC[:, 8:16]))(h))
        for b in range(32):
            V((lambda b: lambda e: e.tensor_reduce(
                out=kstat[0:64, b:b + 1], in_=kvT[0:64, 64 * b:64 * (b + 1)],
                axis=AX.X, op=ALU.max))(b))
            V((lambda b: lambda e: e.tensor_reduce(
                out=kminb[:, b:b + 1], in_=kvT[0:64, 64 * b:64 * (b + 1)],
                axis=AX.X, op=ALU.min))(b))
        vw = P.now("V")
        G(lambda e: e.dma_start(out=kstat[64:128, :], in_=kminb[:, :]), waits=vw)
        for kt in range(16):
            vw2 = P.now("V")
            T((lambda kt: lambda e: e.transpose(
                pD[:, 0:64], kvT[64:128, 128 * kt:128 * (kt + 1)],
                idns[64:128, 64:128]))(kt), waits=vw2)
            tw = P.now("T")
            V((lambda kt: lambda e: e.tensor_copy(
                out=vtmh[:, kt, 0:64], in_=pD[:, 0:64]))(kt), waits=tw)
            V((lambda kt: lambda e: e.tensor_tensor(
                out=vtml[:, kt, 0:64], in0=pD[:, 0:64], in1=vtmh[:, kt, 0:64],
                op=ALU.subtract))(kt))
        V(lambda e: e.memset(vtmh[:, :, 64:65].rearrange("p c l -> p (c l)"), 1.0))
        V(lambda e: e.memset(vtml[:, :, 64:65].rearrange("p c l -> p (c l)"), 0.0))
        tap("xho_hi", xho_hi[:, 0, :], 128, TS)
        tap("kvT", kvT[:, :], 128, 2 * TS)
        tap("konT", konT[0:65, :], 65, TS)
        tap("kscrow", kscrow[0:1, :], 1, 2 * TS)
        flush()
        st_p1.close()
        P.set_barrier()

        qlatH = alloc(st_p2, "qlatH", [65, 2, TS], F32)
        wqc = alloc(st_p2, "wqc", [128, 2, 512], F32)
        wqlo = alloc(st_p2, "wqlo", [128, 2, 512], BF16)
        q2l65 = alloc(st_p2, "q2l65", [128, H, 65], F32)
        q2l65lo = alloc(st_p2, "q2l65lo", [128, H, 65], BF16)
        qev32 = alloc(st_p2, "qev32", [128, 512], F32)
        qevl = alloc(st_p2, "qevl", [128, 512], BF16)
        qpn = alloc(st_p2, "qpn", [128, 2, TS], BF16)
        qntmp = alloc(st_p2, "qntmp", [64, TS], BF16)
        prodx = alloc(st_p2, "prodx", [65, 2, TS], BF16)
        qloS = alloc(st_p2, "qloS", [65, TS], BF16)

        bar = barrier()
        S(lambda e: e.dma_start(out=q2l65[:, :, 0:64],
                                in_=q2li.rearrange("h d l -> d h l")), waits=bar)
        q2w = P.now("dS", "dS2")
        for h in range(H):
            V((lambda h: lambda e: e.tensor_reduce(
                out=q2l65[:, h, 64:65], in_=q2l65[:, h, 0:64], axis=AX.X, op=ALU.add))(h),
              waits=q2w + bar)
        V(lambda e: e.tensor_tensor(out=q2l65lo[:].rearrange("p h l -> p (h l)"),
                                    in0=q2l65[:].rearrange("p h l -> p (h l)"),
                                    in1=hiv(q2l65[:].rearrange("p h l -> p (h l)")),
                                    op=ALU.subtract))
        V(lambda e: e.memset(qh65[:, :, :][64:65].rearrange("o h t -> o (h t)"), 0.0))

        PSQ = [(pA, 0), (pA, 512), (pB, 0), (pB, 512), (pC, 0), (pD, 0), (pE, 0),
               (pF, 0)]
        doneQ = {}
        mach_t = {}
        rowvw = {}
        gw2D = {}
        qdma_w = {}

        def emit_eq(grp, hi_):
            h = 4 * grp + hi_
            hb = h % 2
            for qb in range(2):
                ps, col = PSQ[hi_ * 2 + qb]
                tw = P.now("T")
                A((lambda ps, col: lambda e: e.activation(
                    out=qev32[:, :], in_=ps[:, col:col + 512], func=AF.Copy))(
                    ps, col), waits=tw)
                aw = P.now("A")
                V(lambda e: e.tensor_tensor(
                    out=qevl[:, :], in0=qev32[:, :], in1=hiv(qev32[:, :]),
                    op=ALU.subtract), waits=aw)
                vw = P.now("V")
                for pi in (0, 2, 1):
                    wap = (hiv(q2l65[:, h, :]) if pi < 2 else q2l65lo[:, h, :])
                    xap = hiv(qev32[:, :]) if pi != 1 else qevl[:, :]
                    T((lambda ps, col, wap, xap, st: lambda e: e.matmul(
                        ps[0:65, col:col + 512], wap, xap, start=st, stop=False,
                        skip_group_check=True))(ps, col, wap, xap, pi == 0),
                      waits=(aw if pi != 1 else vw))
                tw = P.now("T")
                V((lambda ps, col, qb, hb: lambda e: e.tensor_copy(
                    out=qlatH[:, hb, 512 * qb:512 * (qb + 1)],
                    in_=ps[0:65, col:col + 512]))(ps, col, qb, hb), waits=tw)
            mw = mach_t.get(h - 2, [])
            V((lambda h, hb: lambda e: e.tensor_copy(
                out=qh65[:, h, :][0:64], in_=hiv(qlatH[0:64, hb, :])))(h, hb))
            vwq = P.now("V")
            G((lambda h: lambda e: e.dma_start(
                out=qhPK[64:128, h, :], in_=qhPK[0:64, h, :]))(h), waits=vwq)
            V((lambda h, hb: lambda e: e.tensor_tensor(
                out=ql65[:, h, :][0:64], in0=qlatH[0:64, hb, :],
                in1=qh65[:, h, :][0:64], op=ALU.subtract))(h, hb))
            V((lambda hb: lambda e: e.tensor_tensor(
                out=prodx[0:64, hb, :], in0=qlatH[0:64, hb, :], in1=konT[0:64, :],
                op=ALU.mult))(hb), waits=mw)
            V((lambda hb: lambda e: e.tensor_tensor(
                out=prodx[64:65, hb, :], in0=qlatH[64:65, hb, :], in1=konT[64:65, :],
                op=ALU.mult))(hb))
            V((lambda hb: lambda e: e.tensor_scalar(
                out=qpn[0:64, hb, :], in0=qlatH[0:64, hb, :], scalar1=0.0,
                scalar2=0.0, op0=ALU.max, op1=ALU.add))(hb))
            V((lambda hb: lambda e: e.tensor_scalar(
                out=qntmp[:, :], in0=qlatH[0:64, hb, :], scalar1=0.0, scalar2=0.0,
                op0=ALU.min, op1=ALU.add))(hb), waits=qdma_w.get(h - 1, []))
            V((lambda hb: lambda e: e.tensor_copy(
                out=qloS[64:65, :], in_=hiv(qlatH[64:65, hb, :])))(hb))
            vw = P.now("V")
            rowvw[h] = vw
            G((lambda hb: lambda e: e.dma_start(
                out=qpn[64:128, hb, :], in_=qntmp[:, :]))(hb), waits=vw)
            G((lambda h: lambda e: e.dma_start(
                out=qsum3p[3 * h:3 * h + 1, :], in_=qloS[64:65, :]))(h))
            G((lambda h: lambda e: e.dma_start(
                out=qsum3p[3 * h + 1:3 * h + 2, :], in_=qloS[64:65, :]))(h))
            ghw = P.now("dG", "dG2")
            V((lambda hb: lambda e: e.tensor_tensor(
                out=qloS[64:65, :], in0=qlatH[64:65, hb, :],
                in1=hiv(qlatH[64:65, hb, :]), op=ALU.subtract))(hb), waits=ghw)
            vw2 = P.now("V")
            G((lambda h: lambda e: e.dma_start(
                out=qsum3p[3 * h + 2:3 * h + 3, :], in_=qloS[64:65, :]))(h),
              waits=vw2)
            gw2D[h] = P.now("dG", "dG2")
            qdma_w[h] = gw2D[h]

        def emit_mach(grp, hi_):
            h = 4 * grp + hi_
            hb = h % 2
            ps0, col0 = PSQ[hi_ * 2]
            vw = rowvw[h]
            gw2 = gw2D[h]
            for qt in range(8):
                T((lambda ps0, col0, qt, hb, st: lambda e: e.matmul(
                    ps0[:, col0 + 256 + qt:col0 + 257 + qt],
                    prodx[:, hb, 128 * qt:128 * (qt + 1)], m65[:, :],
                    start=st, stop=False, skip_group_check=True))(
                    ps0, col0, qt, hb, qt == 0), waits=vw)
                T((lambda ps0, col0, qt, hb: lambda e: e.matmul(
                    ps0[:, col0 + 264 + qt:col0 + 265 + qt],
                    qlatH[64:65, hb, 128 * qt:128 * (qt + 1)], ones128[64:65, :],
                    start=False, stop=False, skip_group_check=True))(ps0, col0, qt, hb),
                  waits=vw)
                T((lambda ps0, col0, qt, hb: lambda e: e.matmul(
                    ps0[:, col0 + 32 * qt:col0 + 32 * (qt + 1)],
                    qpn[:, hb, 128 * qt:128 * (qt + 1)], kstat[:, :],
                    start=False, stop=False, skip_group_check=True))(ps0, col0, qt, hb),
                  waits=vw + gw2)
            mach_t[h] = [("T", P.cnt["T"])]
            tw = P.now("T")
            V((lambda ps0, col0, h: lambda e: e.tensor_reduce(
                out=b128[:, 8 * h:8 * h + 8],
                in_=ps0[:, col0:col0 + 256].rearrange("p (q b) -> p q b", b=32),
                axis=AX.X, op=ALU.max))(ps0, col0, h), waits=tw)
            V((lambda ps0, col0, h: lambda e: e.tensor_copy(
                out=d128[:, 8 * h:8 * h + 8],
                in_=ps0[:, col0 + 256:col0 + 264]))(ps0, col0, h))
            V((lambda ps0, col0, h: lambda e: e.tensor_copy(
                out=q128s[:, 8 * h:8 * h + 8],
                in_=ps0[:, col0 + 264:col0 + 272]))(ps0, col0, h))

        for grp in range(4):
            heads = list(range(4 * grp, 4 * grp + 4))
            P.new_epoch()
            for c in range(NCH):
                gc = grp * 16 + c
                buf = gc % 2
                ld = S((lambda c, buf, grp: lambda e: e.dma_start(
                    out=wqc[:, buf, :],
                    in_=wqT[128 * c:128 * (c + 1),
                            512 * grp:512 * (grp + 1)]))(c, buf, grp),
                    waits=doneQ.get(gc - 2, bar))
                V((lambda buf: lambda e: e.tensor_tensor(
                    out=wqlo[:, buf, :], in0=wqc[:, buf, :],
                    in1=hiv(wqc[:, buf, :]),
                    op=ALU.subtract))(buf), waits=[ld])
                vw = P.now("V")
                for hi_, h in enumerate(heads):
                    for qb in range(2):
                        ps, col = PSQ[hi_ * 2 + qb]
                        for pi in range(3):
                            wap = (hiv(wqc[:, buf, 128 * hi_:128 * (hi_ + 1)])
                                   if pi < 2
                                   else wqlo[:, buf, 128 * hi_:128 * (hi_ + 1)])
                            xap = (xho_hi if pi != 1 else xho_lo)[
                                :, c, 512 * qb:512 * (qb + 1)]
                            st = P.first_touch((PSNAME[id(ps)], col // 512))
                            T((lambda ps, col, wap, xap, st: lambda e: e.matmul(
                                ps[:, col:col + 512], wap, xap, start=st, stop=False,
                                skip_group_check=True))(ps, col, wap, xap, st), waits=vw)
                doneQ[gc] = P.now("T", "V")
            for hi_ in range(4):
                emit_eq(grp, hi_)
                if hi_ >= 1:
                    emit_mach(grp, hi_ - 1)
            emit_mach(grp, 3)
        flush()
        tap("qlat", qlatH[:, 1, :], 65, TS)

        V(lambda e: e.tensor_scalar(out=b128[:, :], in0=b128[:, :], scalar1=1.0,
                                    scalar2=2.0 - PAD, op0=ALU.mult, op1=ALU.add))
        V(lambda e: e.tensor_tensor(out=s1s[:, :], in0=q128s[:, :], in1=wmx16[:, :],
                                    op=ALU.mult))
        V(lambda e: e.tensor_tensor(out=wmx16[:, :], in0=q128s[:, :], in1=wmn16[:, :],
                                    op=ALU.mult))
        V(lambda e: e.tensor_tensor(out=s1s[:, :], in0=s1s[:, :], in1=wmx16[:, :],
                                    op=ALU.max))
        V(lambda e: e.tensor_tensor(out=b128[:, :], in0=b128[:, :], in1=s1s[:, :],
                                    op=ALU.add))
        V(lambda e: e.tensor_tensor(out=b128[:, :], in0=b128[:, :], in1=d128[:, :],
                                    op=ALU.max))
        V(lambda e: e.tensor_scalar(out=b128[:, :], in0=b128[:, :], scalar1=-1.0,
                                    scalar2=0.0, op0=ALU.mult, op1=ALU.add))
        vw = P.now("V")
        T(lambda e: e.matmul(pC[:, 0:128], b128[:, :], idns[:, :], start=True,
                             stop=False, skip_group_check=True, is_transpose=True),
          waits=vw)
        tw = P.now("T")
        V(lambda e: e.tensor_copy(out=s1s[:, :], in_=pC[:, 0:128]), waits=tw)
        vw = P.now("V")
        G(lambda e: e.dma_start(
            out=ql65[:, :, :][64:65].rearrange("o h (q p) -> o (h q p)", p=128),
            in_=s1s[:, :]), waits=vw)
        flush()
        tap("S128", b128[:, :], 128, 128)
        flush()
        st_p2.close()
        P.set_barrier()

        expf = alloc(st_p3, "expf", [128, 2, TS], F32)
        expl = alloc(st_p3, "expl", [128, 2, TS], BF16)
        mtmp = alloc(st_p3, "mtmp", [128, 256], F32)
        qs3b = alloc(st_p3, "qs3b", [3, 2, TS], BF16)
        rbc = alloc(st_p3, "rbc", [64, TS], F32)
        cxs = alloc(st_p3, "cxs", [64, TS], F32)
        cxu = alloc(st_p3, "cxu", [65, TS], F32)

        bar = barrier()
        P.new_epoch()
        mask_v = {}
        exp_a = {}
        lo_v = {}
        av_t = {}
        epi_v = {}
        epi_free = {}
        recipV = {}
        ktlist = []
        for j in range(8):
            ktlist.append(("own", j))
            ktlist.append(("oth", j))
        qs3w_d = {}
        seg_d = {}

        def emit_qdma(h):
            hb = h % 2
            G((lambda h, hb: lambda e: e.dma_start(
                out=qs3b[:, hb, :], in_=qsum3p[3 * h:3 * h + 3, :]))(h, hb),
              waits=(bar if h < 2 else av_t[(h - 2) * 16 + 15]))
            qs3w_d[h] = P.now("dG", "dG2")

        def emit_lr(g):
            h, idx = g // 16, g % 16
            if idx == 0:
                emit_qdma(h)
            typ, j = ktlist[idx]
            pL = pA if idx % 2 == 0 else pB
            kt = j if typ == "own" else 8 + j
            lcol = 128 * kt
            sfx = 128 * j
            W = TS - sfx
            hb = h % 2
            lw = (bar if g == 0 else []) + \
                ((exp_a[g - 2] + mask_v[g - 2]) if g >= 2 else [])
            cuts = sorted({sfx, TS} | ({512} if sfx < 512 else set())
                          | ({sfx + 512} if sfx + 512 < TS else set()))
            segs = [(cuts[i], cuts[i + 1] - cuts[i]) for i in range(len(cuts) - 1)]
            seg_d[g] = (segs, sfx, kt, W)
            seen_banks = set()
            for q0, w in segs:
                bank = (q0 - sfx) // 512
                fresh = bank not in seen_banks
                seen_banks.add(bank)
                for pi in range(2):
                    wap = (khl[:, lcol:lcol + 128] if pi == 0
                           else kh65[:, lcol:lcol + 128])
                    xap = (qhPK if pi == 0 else ql65)[:, h, q0:q0 + w]
                    T((lambda pL, q0, w, sfx, wap, xap, st: lambda e: e.matmul(
                        pL[:, q0 - sfx:q0 - sfx + w], wap, xap, start=st,
                        stop=False, skip_group_check=True))(
                        pL, q0, w, sfx, wap, xap, pi == 0 and fresh),
                      waits=lw)
            pR = pC if idx % 2 == 0 else pD
            wB = 128 if typ == "own" else min(256, W)
            T((lambda pR, lcol, hb, sfx, wB: lambda e: e.matmul(
                pR[:, 0:wB], ksc3[:, lcol:lcol + 128],
                qs3b[:, hb, sfx:sfx + wB], start=True, stop=False,
                skip_group_check=True))(pR, lcol, hb, sfx, wB),
              waits=lw + qs3w_d[h] + (mask_v[g - 2] if g >= 2 else []))
            tw = P.now("T")
            mm = mDBm[:, 0:128] if typ == "own" else mOBm[:, 0:wB]
            V((lambda pR, wB, mm: lambda e: e.tensor_tensor(
                out=mtmp[:, 0:wB], in0=pR[:, 0:wB], in1=mm, op=ALU.mult))(
                pR, wB, mm), waits=tw)
            V((lambda pL, wB: lambda e: e.tensor_tensor(
                out=pL[:, 0:wB], in0=pL[:, 0:wB], in1=mtmp[:, 0:wB],
                op=ALU.add))(pL, wB))
            ma = mDadd if typ == "own" else mOadd
            V((lambda pL, ma: lambda e: e.tensor_tensor(
                out=pL[:, 0:128], in0=pL[:, 0:128], in1=ma[:, :], op=ALU.add))(
                pL, ma))
            mask_v[g] = P.now("V")
            eb = idx % 2
            A((lambda pL, eb, W: lambda e: e.activation(
                out=expf[:, eb, 0:W], in_=pL[:, 0:W], func=AF.Exp))(pL, eb, W),
              waits=mask_v[g] + (av_t[g - 2] if g >= 2 else []))
            exp_a[g] = [("A", P.cnt["A"])]
            V((lambda eb, W: lambda e: e.tensor_tensor(
                out=expl[:, eb, 0:W], in0=expf[:, eb, 0:W],
                in1=hiv(expf[:, eb, 0:W]), op=ALU.subtract))(eb, W),
              waits=exp_a[g])
            lo_v[g] = P.now("V")

        def emit_av(g):
            h, idx = g // 16, g % 16
            segs, sfx, kt, W = seg_d.pop(g)
            eb = idx % 2
            avw0 = (epi_free.get(h - 1, []) if idx == 0 else [])
            for pi in (0, 2, 1):
                for q0, w in segs:
                    ps = pE if q0 < 512 else pF
                    pcol = q0 - (0 if q0 < 512 else 512)
                    wap = (vtmh if pi < 2 else vtml)[:, kt, :]
                    xsrc = hiv(expf[:, eb, q0 - sfx:q0 - sfx + w]) if pi != 1 \
                        else expl[:, eb, q0 - sfx:q0 - sfx + w]
                    st = (idx == 0 and pi == 0)
                    T((lambda ps, pcol, w, wap, xsrc, st: lambda e: e.matmul(
                        ps[0:65, pcol:pcol + w], wap, xsrc, start=st, stop=False,
                        skip_group_check=True))(ps, pcol, w, wap, xsrc, st),
                      waits=(exp_a[g] if pi != 1 else lo_v[g]) + avw0)
            av_t[g] = [("T", P.cnt["T"])]

        def emit_epi_head(h):
            tw = P.now("T")
            A(lambda e: e.activation(out=cxu[64:65, 0:512], in_=pE[64:65, :],
                                     func=AF.Copy, bias=1e-38, scale=1.0),
              waits=tw)
            A(lambda e: e.activation(out=cxu[64:65, 512:1024], in_=pF[64:65, :],
                                     func=AF.Copy, bias=1e-38, scale=1.0))
            aw = P.now("A")
            V(lambda e: e.reciprocal(out=cxu[64:65, :], in_=cxu[64:65, :]),
              waits=aw)
            recipV[h] = P.now("V")

        def emit_epi(h):
            vw = recipV[h]
            T(lambda e: e.matmul(pE[64:128, :], onesP64[64:65, :],
                                 cxu[64:65, 0:512], start=True, stop=False,
                                 skip_group_check=True), waits=vw)
            T(lambda e: e.matmul(pF[64:128, :], onesP64[64:65, :],
                                 cxu[64:65, 512:1024], start=True, stop=False,
                                 skip_group_check=True))
            tw = P.now("T")
            V(lambda e: e.tensor_copy(out=rbc[:, 0:512], in_=pE[64:128, :]),
              waits=tw)
            V(lambda e: e.tensor_copy(out=rbc[:, 512:1024], in_=pF[64:128, :]))
            V((lambda h: lambda e: e.tensor_tensor(
                out=cxs[:, 0:512], in0=pE[0:64, :], in1=rbc[0:64, 0:512],
                op=ALU.mult))(h), waits=epi_v.get(h - 1, []))
            V(lambda e: e.tensor_tensor(
                out=cxs[:, 512:1024], in0=pF[0:64, :], in1=rbc[0:64, 512:1024],
                op=ALU.mult))
            epi_free[h] = P.now("V")
            V((lambda h: lambda e: e.tensor_copy(
                out=ctxh[:, h, :], in_=cxs[:, :]))(h))
            V((lambda h: lambda e: e.tensor_tensor(
                out=ctxl[:, h, :], in0=cxs[:, :], in1=ctxh[:, h, :],
                op=ALU.subtract))(h))
            epi_v[h] = P.now("V")
            G((lambda h: lambda e: e.dma_start(
                out=ctxPK1[64:128, h, :], in_=ctxPK1[0:64, h, :]))(h), waits=epi_v[h])
            if h % 2 == 1:
                G((lambda h: lambda e: e.dma_start(
                    out=ctxPK2[64:128, h - 1, :], in_=ctxPK2[0:64, h, :]))(h))

        for gg in range(16 * H + 2):
            if gg >= 2:
                emit_av(gg - 2)
                if (gg - 2) % 16 == 15:
                    emit_epi_head((gg - 2) // 16)
            if gg < 16 * H:
                emit_lr(gg)
            if gg >= 2 and (gg - 2) % 16 == 15:
                emit_epi((gg - 2) // 16)
        P.new_epoch()
        flush()
        tap("ctxf", ctxh[0:64, 0, :], 64, TS)
        flush()
        st_p3.close()
        st_kv.close()
        P.set_barrier()

        woc = alloc(st_p4, "woc", [128, 16, 128], F32)
        woc2 = alloc(st_p4, "woc2", [128, 16, 128], F32)
        woclo = alloc(st_p4, "woclo", [128, 16, 128], BF16)
        woclo2 = alloc(st_p4, "woclo2", [128, 16, 128], BF16)
        vupS = alloc(st_p4, "vupS", [128, H, L], F32)
        vupSlo = alloc(st_p4, "vupSlo", [128, H, L], BF16)
        wvoPK = alloc(st_p4, "wvoPK", [128, 2, H, 128], BF16)
        wvoPR = alloc(st_p4, "wvoPR", [128, 2, 8, 128], BF16)
        wvol = alloc(st_p4, "wvol", [64, 2, H, 128], BF16)
        xoc2 = alloc(st_p4, "xoc2", [128, 2, TS], F32)

        bar = barrier()
        S(lambda e: e.dma_start(out=vupS[:, :, :],
                                in_=vupTi.rearrange("h d l -> d h l")), waits=bar)
        vw0 = P.now("dS", "dS2")
        V(lambda e: e.tensor_tensor(out=vupSlo[:].rearrange("p h l -> p (h l)"),
                                    in0=vupS[:].rearrange("p h l -> p (h l)"),
                                    in1=hiv(vupS[:].rearrange("p h l -> p (h l)")),
                                    op=ALU.subtract), waits=vw0 + bar)

        wocB = [woc, woc2]
        wocloB = [woclo, woclo2]
        wocw = {}
        bT = {}
        aT = {}
        dwD = {}
        xvD = {}
        loV = {}
        evV = {}

        def p4_dma(eo):
            ld = S((lambda eo: lambda e: e.dma_start(
                out=wocB[eo % 2][:, :, :],
                in_=woT[:, 128 * eo:128 * (eo + 1)].rearrange("(c p) m -> p c m",
                                                              p=128)))(eo),
                waits=bT.get(eo - 2, []) + (bar if eo < 2 else []))
            wocw[eo] = [ld]

        def p4_lo(eo):
            b = eo % 2
            V((lambda b: lambda e: e.tensor_tensor(
                out=wocloB[b][:].rearrange("p c m -> p (c m)"),
                in0=wocB[b][:].rearrange("p c m -> p (c m)"),
                in1=hiv(wocB[b][:].rearrange("p c m -> p (c m)")),
                op=ALU.subtract))(b), waits=wocw[eo] + bT.get(eo - 2, []))
            loV[eo] = P.now("V")

        PS4 = (pC, pD, pE, pF)

        def p4_build(eo):
            b = eo % 2
            bw = loV[eo] + evV.get(eo - 1, [])
            for h in range(H):
                ps, col = PS4[h // 4], 128 * (h % 4)
                for pi in range(3):
                    wap = (hiv(vupS[:, h, :]) if pi < 2 else vupSlo[:, h, :])
                    xap = (hiv(wocB[b][:, h, :]) if pi != 1
                           else wocloB[b][:, h, :])
                    st = (pi == 0 and h % 4 == 0)
                    T((lambda ps, col, wap, xap, st: lambda e: e.matmul(
                        ps[0:64, col:col + 128], wap, xap, start=st, stop=False,
                        skip_group_check=True))(ps, col, wap, xap, st), waits=bw)
            bT[eo] = [("T", P.cnt["T"])]
            tw = P.now("T")
            for gi, ps2 in enumerate(PS4):
                h0 = 4 * gi
                V((lambda b, ps2, h0: lambda e: e.tensor_copy(
                    out=wvoPK[0:64, b, h0:h0 + 4, :],
                    in_=ps2[0:64, 0:512]))(b, ps2, h0),
                  waits=tw + dwD.get(eo - 2, []))
                V((lambda b, ps2, h0: lambda e: e.tensor_tensor(
                    out=wvol[:, b, h0:h0 + 4, :],
                    in0=ps2[0:64, 0:512].rearrange("p (a m) -> p a m", a=4),
                    in1=wvoPK[0:64, b, h0:h0 + 4, :],
                    op=ALU.subtract))(b, ps2, h0))
            evV[eo] = P.now("V")

        def p4_pack(eo):
            b = eo % 2
            vw = P.now("V")
            V((lambda b: lambda e: e.tensor_copy(
                out=wvoPR[0:64, b, :, :],
                in_=wvoPK[0:64, b, :, :][:, 0::2, :]))(b),
              waits=vw + aT.get(eo - 2, []))
            vw = P.now("V")
            dlo = G((lambda b: lambda e: e.dma_start(
                out=wvoPK[64:128, b, :, :],
                in_=wvol[:, b, :, :]))(b),
                waits=vw + aT.get(eo - 2, []))
            dhi = G((lambda b: lambda e: e.dma_start(
                out=wvoPR[64:128, b, :, :],
                in_=wvoPK[0:64, b, :, :][:, 1::2, :]))(b))
            dwD[eo] = [dlo, dhi]

        def p4_apply(eo):
            b = eo % 2
            ld = S((lambda eo: lambda e: e.dma_start(
                out=xoc2[:, eo % 2, :], in_=xoT[128 * eo:128 * (eo + 1), :]))(eo),
                waits=xvD.get(eo - 2, []))
            for qb in range(2):
                ps = pA if qb == 0 else pB
                for h in range(H):
                    st = (h == 0)
                    T((lambda b, ps, h, qb, st: lambda e: e.matmul(
                        ps[:, 0:512], wvoPK[:, b, h, :],
                        ctxPK1[:, h, 512 * qb:512 * (qb + 1)], start=st, stop=False,
                        skip_group_check=True))(b, ps, h, qb, st),
                      waits=dwD[eo] + (xvD.get(eo - 1, []) if st else []))
                for he in range(8):
                    T((lambda b, ps, he, qb: lambda e: e.matmul(
                        ps[:, 0:512], wvoPR[:, b, he, :],
                        ctxPK2[:, 2 * he, 512 * qb:512 * (qb + 1)], start=False,
                        stop=False, skip_group_check=True))(b, ps, he, qb),
                      waits=dwD[eo])
            aT[eo] = [("T", P.cnt["T"])]
            tw = P.now("T")
            V((lambda eo: lambda e: e.tensor_tensor(
                out=xp[:, eo, 0:512], in0=pA[:, 0:512], in1=xoc2[:, eo % 2, 0:512],
                op=ALU.add))(eo), waits=tw + [ld])
            V((lambda eo: lambda e: e.tensor_tensor(
                out=xp[:, eo, 512:1024], in0=pB[:, 0:512],
                in1=xoc2[:, eo % 2, 512:1024], op=ALU.add))(eo))
            xvD[eo] = P.now("V")

        p4_dma(0)
        p4_lo(0)
        p4_build(0)
        p4_pack(0)
        p4_dma(1)
        p4_lo(1)
        for eo in range(NCH):
            if eo + 1 < NCH:
                p4_build(eo + 1)
                p4_pack(eo + 1)
            p4_apply(eo)
            if eo + 2 < NCH:
                p4_dma(eo + 2)
                p4_lo(eo + 2)
        P.new_epoch()
        flush()
        tap("xp", xp[:, 0, :], 128, TS)
        flush()
        st_p4.close()
        P.set_barrier()

        x8 = alloc(st_p5, "x8", [128, NCH, TS], FP8)
        mrow = alloc(st_p5, "mrow5", [1, TS], F32)
        rrow = alloc(st_p5, "rrow5", [1, TS], F32)
        scr1 = srow[0:1, :]
        scr2 = alloc(st_p5, "scr2_5", [1, TS], F32)
        w0c = alloc(st_p5, "w0c", [128, 3, 8, 2, 128], FP8)
        wst = alloc(st_p5, "wst", [128, 2, 8, 2, 128], F32)
        w1c = alloc(st_p5, "w1c", [128, 2, 8, 2, 128], FP8)
        wlc = alloc(st_p5, "wlc", [128, 2, 16, 2, 128], FP8)
        gb = bcM[:].bitcast(BF16)[:, 0:1024]
        f1b = AR[:, 131072:133120].bitcast(BF16)

        bar = barrier()
        P.new_epoch()
        for c in range(NCH):
            cw = bar if c == 0 else P.now("T")
            A((lambda c: lambda e: e.activation(
                out=gb[:], in_=xp[:, c, :], func=AF.Square))(c),
              waits=cw)
            for qb in range(2):
                T((lambda c, qb, st: lambda e: e.matmul(
                    pA[0:1, 512 * qb:512 * (qb + 1)], ones128b[:],
                    hiv(xp[:, c, 512 * qb:512 * (qb + 1)]),
                    start=st, stop=False, skip_group_check=True))(
                    c, qb, P.first_touch(("A", qb))), waits=bar if c == 0 else [])
            aw = P.now("A")
            for qb in range(2):
                T((lambda c, qb, st: lambda e: e.matmul(
                    pA[32:33, 512 * qb:512 * (qb + 1)], ones128b[:],
                    gb[:, 512 * qb:512 * (qb + 1)], start=st, stop=False,
                    skip_group_check=True))(c, qb, P.first_touch(("A32", qb))),
                  waits=aw)
        tw = P.now("T")
        V(lambda e: e.tensor_scalar(out=mrow[:, :], in0=pA[0:1, :], scalar1=1.0 / E,
                                    scalar2=0.0, op0=ALU.mult, op1=ALU.add), waits=tw)
        V(lambda e: e.tensor_scalar(out=rrow[:, :], in0=pA[32:33, :], scalar1=1.0 / E,
                                    scalar2=1e-5, op0=ALU.mult, op1=ALU.add))
        V(lambda e: e.scalar_tensor_tensor(out=scr1[:, :], in0=mrow[:, :], scalar=-1.0,
                                           in1=mrow[:, :], op0=ALU.mult, op1=ALU.mult))
        V(lambda e: e.tensor_tensor(out=rrow[:, :], in0=rrow[:, :], in1=scr1[:, :],
                                    op=ALU.add))
        V(lambda e: e.reciprocal(out=scr1[:, :], in_=rrow[:, :]))
        vw = P.now("V")
        A(lambda e: e.activation(out=scr1[:, :], in_=scr1[:, :], func=AF.Sqrt),
          waits=vw)
        aw = P.now("A")
        V(lambda e: e.tensor_tensor(out=scr2[:, :], in0=scr1[:, :], in1=scr1[:, :],
                                    op=ALU.mult), waits=aw)
        V(lambda e: e.tensor_tensor(out=scr2[:, :], in0=scr2[:, :], in1=rrow[:, :],
                                    op=ALU.mult))
        V(lambda e: e.tensor_scalar(out=scr2[:, :], in0=scr2[:, :], scalar1=-0.5,
                                    scalar2=1.5, op0=ALU.mult, op1=ALU.add))
        V(lambda e: e.tensor_tensor(out=rrow[:, :], in0=scr1[:, :], in1=scr2[:, :],
                                    op=ALU.mult))
        P.new_epoch()
        vw = P.now("V")
        for srcrow, dst in ((mrow, bcM), (rrow, bcR)):
            for qb in range(2):
                T((lambda srcrow, qb: lambda e: e.matmul(
                    pB[:, 512 * qb:512 * (qb + 1)], onesrow[:, :],
                    srcrow[:, 512 * qb:512 * (qb + 1)], start=True, stop=False,
                    skip_group_check=True))(srcrow, qb), waits=vw)
            tw = P.now("T")
            V((lambda dst: lambda e: e.tensor_copy(out=dst[:], in_=pB[:]))(dst), waits=tw)
            vw = P.now("V")
        for c in range(NCH):
            V((lambda c: lambda e: e.tensor_tensor(
                out=pB[:, :], in0=xp[:, c, :], in1=bcM[:], op=ALU.subtract))(c))
            V((lambda c: lambda e: e.tensor_tensor(
                out=x8[:, c, :], in0=pB[:, :], in1=bcR[:], op=ALU.mult))(c))
        flush()
        tap("x8", x8[:, 0, :].bitcast(BF16)[:, 0:512].bitcast(F32)[:, 0:128], 128, 128)

        P.new_epoch()
        sw = P.now("T", "V")
        done = {-2: sw, -1: sw}
        gelu_done = {}
        mm_done = {}
        castA = {}
        ldD = {}

        def g_dma(hc):
            buf = hc % 2
            l0 = S((lambda hc, buf: lambda e: e.dma_start(
                out=wst[:, buf, :, :, :],
                in_=wg[0, :, 128 * hc:128 * (hc + 1)].rearrange(
                    "(k j p) m -> p k j m", j=2, p=128)))(hc, buf),
                waits=castA.get(hc - 2, []))
            l1 = G((lambda hc, buf: lambda e: e.dma_start(
                out=w1c[:, buf, :, :, :],
                in_=wg[1, :, 128 * hc:128 * (hc + 1)].rearrange(
                    "(k j p) m -> p k j m", j=2, p=128)))(hc, buf),
                waits=mm_done.get(hc - 2, []))
            ldD[hc] = [l0, l1]
            A((lambda buf, b3: lambda e: e.activation(
                out=w0c[:, b3, :, :, :].rearrange("p k j m -> p (k j m)"),
                in_=wst[:, buf, :, :, :].rearrange("p k j m -> p (k j m)"),
                func=AF.Copy))(buf, hc % 3),
              waits=[l0] + mm_done.get(hc - 3, []))
            castA[hc] = [("A", P.cnt["A"])]

        g_dma(0)
        g_dma(1)
        for hc in range(64):
            buf = hc % 2
            lw = [ldD[hc][1]] + castA[hc] + \
                gelu_done.get(hc - 2, (P.now("V") if hc < 2 else []))
            if buf == 0:
                psg = [(pA, 0), (pA, 512)]
                psf = [(pB, 0), (pB, 512)]
            else:
                psg = [(pC, 0), (pD, 0)]
                psf = [(pE, 0), (pF, 0)]
            for qq in range(4):
                psG, colG = psg[qq // 2]
                psF, colF = psf[qq // 2]
                cg = colG + 256 * (qq % 2)
                cf = colF + 256 * (qq % 2)
                for kc in range(8):
                    T((lambda b3, qq, kc, psG, cg, st: lambda e: e.matmul(
                        psG[:, cg:cg + 256], w0c[:, b3, kc, :, :],
                        x8[:, 2 * kc:2 * kc + 2, 256 * qq:256 * (qq + 1)],
                        start=st, stop=False, skip_group_check=True,
                        perf_mode=PM.DoubleRow))(hc % 3, qq, kc, psG, cg,
                                                 (kc == 0 and qq % 2 == 0)), waits=lw)
                    T((lambda buf, qq, kc, psF, cf, st: lambda e: e.matmul(
                        psF[:, cf:cf + 256], w1c[:, buf, kc, :, :],
                        x8[:, 2 * kc:2 * kc + 2, 256 * qq:256 * (qq + 1)],
                        start=st, stop=False, skip_group_check=True,
                        perf_mode=PM.DoubleRow))(buf, qq, kc, psF, cf,
                                                 (kc == 0 and qq % 2 == 0)), waits=lw)
            mm_done[hc] = P.now("T")
            if hc + 2 < 64:
                g_dma(hc + 2)
            tw = mm_done[hc]
            gsrc = [(psg[0][0], psg[0][1]), (psg[1][0], psg[1][1])]
            fsrc = [(psf[0][0], psf[0][1]), (psf[1][0], psf[1][1])]
            V((lambda gsrc: lambda e: e.tensor_copy(
                out=gb[:, 0:512], in_=gsrc[0][0][:, gsrc[0][1]:gsrc[0][1] + 512]))(gsrc),
              waits=tw)
            V((lambda gsrc: lambda e: e.tensor_copy(
                out=gb[:, 512:1024], in_=gsrc[1][0][:, gsrc[1][1]:gsrc[1][1] + 512]))(gsrc))
            V((lambda fsrc: lambda e: e.tensor_copy(
                out=f1b[:, 0:512], in_=fsrc[0][0][:, fsrc[0][1]:fsrc[0][1] + 512]))(fsrc))
            V((lambda fsrc: lambda e: e.tensor_copy(
                out=f1b[:, 512:1024], in_=fsrc[1][0][:, fsrc[1][1]:fsrc[1][1] + 512]))(fsrc))
            vw = P.now("V")
            A(lambda e: e.activation(out=gb[:], in_=gb[:], func=AF.Gelu_apprx_tanh),
              waits=vw)
            aw = P.now("A")
            V(lambda e: e.tensor_tensor(out=gb[:], in0=gb[:], in1=f1b[:],
                                        op=ALU.mult), waits=aw)
            V((lambda hc: lambda e: e.tensor_copy(out=act8[:, hc, :], in_=gb[:]))(hc))
            gelu_done[hc] = P.now("V")
            done[hc] = P.now("T", "V")
        flush()

        outsW = x8[:, 0:4, :].rearrange("p a b -> p (a b)").bitcast(F32)
        P.new_epoch()
        sw = P.now("T", "V")
        done = {-2: sw, -1: sw}
        outdma = {}
        castL = {}
        ldL = {}
        fcV = {}

        def wl_dma(eo):
            buf = eo % 2
            w = done.get(eo - 2, [])
            ld1 = G((lambda eo, buf: lambda e: e.dma_start(
                out=wlc[:, buf, 0:16, :, :],
                in_=wlin[0:4096, 128 * eo:128 * (eo + 1)].rearrange(
                    "(k j p) m -> p k j m", j=2, p=128)))(eo, buf), waits=w)
            ld2 = G((lambda eo, buf: lambda e: e.dma_start(
                out=w0c[:, buf, :, :, :],
                in_=wlin[4096:6144, 128 * eo:128 * (eo + 1)].rearrange(
                    "(k j p) m -> p k j m", j=2, p=128)))(eo, buf))
            lr = S((lambda eo, buf: lambda e: e.dma_start(
                out=wst[:, buf, :, :, :],
                in_=wlin[6144:8192, 128 * eo:128 * (eo + 1)].rearrange(
                    "(k j p) m -> p k j m", j=2, p=128)))(eo, buf),
                waits=castL.get(eo - 2, castA.get(62 + eo, [])))
            ldL[eo] = [ld1, ld2]
            A((lambda buf: lambda e: e.activation(
                out=w1c[:, buf, :, :, :].rearrange("p k j m -> p (k j m)"),
                in_=wst[:, buf, :, :, :].rearrange("p k j m -> p (k j m)"),
                func=AF.Copy))(buf),
              waits=[lr] + w)
            castL[eo] = [("A", P.cnt["A"])]

        wl_dma(0)
        for eo in range(NCH):
            buf = eo % 2
            if eo + 1 < NCH:
                wl_dma(eo + 1)
            lw = ldL[eo] + castL[eo] + fcV.get(eo - 2, [])
            ps = pC if eo % 2 == 0 else pD
            for qq in range(4):
                w_mm = lw + (P.now("V") if qq >= 2 else [])
                for kc in range(32):
                    st = P.first_touch((PSNAME[id(ps)], 0)) if (qq % 2 == 0 and kc == 0) \
                        else False
                    wsrc = (wlc[:, buf, kc, :, :] if kc < 16 else
                            (w0c if kc < 24 else w1c)[:, buf, kc % 8, :, :])
                    T((lambda ps, buf, qq, kc, st, wsrc: lambda e: e.matmul(
                        ps[:, 256 * (qq % 2):256 * (qq % 2) + 256],
                        wsrc,
                        act8[:, 2 * kc:2 * kc + 2, 256 * qq:256 * (qq + 1)],
                        start=st, stop=False, skip_group_check=True,
                        perf_mode=PM.DoubleRow))(ps, buf, qq, kc, st, wsrc),
                      waits=w_mm)
                if qq % 2 == 1:
                    tw = P.now("T")
                    hb = qq // 2
                    ow = [outdma[eo - 1]] if eo - 1 in outdma else []
                    V((lambda ps: lambda e: e.tensor_copy(
                        out=f1b[:, 0:512], in_=ps[:, 0:512]))(ps), waits=tw)
                    V((lambda eo, hb: lambda e: e.tensor_tensor(
                        out=outsW[:, 512 * hb:512 * (hb + 1)],
                        in0=xp[:, eo, 512 * hb:512 * (hb + 1)], in1=f1b[:, 0:512],
                        op=ALU.add))(eo, hb), waits=ow)
                    P.new_epoch()
            vw = P.now("V")
            fcV[eo] = vw
            done[eo] = P.now("T")
            outdma[eo] = S((lambda eo: lambda e: e.dma_start(
                out=outT[128 * eo:128 * (eo + 1), :], in_=outsW[:, :]))(eo),
                waits=vw)
        fin = P.now("dS", "dS2")
        P.emit("gpsimd", lambda e: e.memset(scr2[0:1, 0:1], 0.0), fin, None)
        flush()

        for st in (st_p5, st_per):
            st.close()
    return nc


_MASK_CACHE = {}


def _masks(pi):
    if pi in _MASK_CACHE:
        return _MASK_CACHE[pi]
    s = np.arange(128)[:, None]
    q = np.arange(128)[None, :]
    mD_add = np.where(s > q, np.float32(-1e30), np.float32(0.0))
    mDB = ((q - s >= 0) & (q - s <= 64)).astype(np.float32)
    sub = (s - q >= 64).astype(np.float32)
    if pi == 0:
        mO_add = np.full((128, 128), -1e30, np.float32)
        mOB = np.concatenate([np.zeros((128, 128), np.float32), sub], axis=1)
    else:
        mO_add = np.zeros((128, 128), np.float32)
        mOB = np.concatenate([sub, np.zeros((128, 128), np.float32)], axis=1)
    out = (mD_add, mO_add, mDB.astype(np.float32), mOB)
    _MASK_CACHE[pi] = out
    return out


_PROG = None


def kernel(L0_attn_hidden_states, L0_attn_Wq, L0_attn_Wk, L0_attn_Wv,
           L0_attn_q_to_latent, L0_attn_v_up, L0_attn_Wo, L0_ffn_w_gating,
           L0_ffn_w_linear):
    global _PROG
    x = np.ascontiguousarray(L0_attn_hidden_states, dtype=np.float32)
    wq = np.ascontiguousarray(L0_attn_Wq, dtype=np.float32)
    wqT = np.ascontiguousarray(wq.T)
    wk = np.ascontiguousarray(L0_attn_Wk, dtype=np.float32)
    wv = np.ascontiguousarray(L0_attn_Wv, dtype=np.float32)
    wkT = np.ascontiguousarray(wk.T)
    wvT = np.ascontiguousarray(wv.T)
    q2l = np.ascontiguousarray(L0_attn_q_to_latent, dtype=np.float32)
    vupT = np.ascontiguousarray(np.transpose(L0_attn_v_up, (0, 2, 1)))
    woT = np.ascontiguousarray(np.asarray(L0_attn_Wo, dtype=np.float32).T)
    wg = np.ascontiguousarray(L0_ffn_w_gating, dtype=np.float32)
    wlin = np.ascontiguousarray(L0_ffn_w_linear, dtype=np.float32)
    idn = np.eye(128, dtype=np.float32)

    if _PROG is None:
        _PROG = build_program()
    nc = _PROG

    in_maps = []
    for c in range(8):
        b, pi = c // 2, c % 2
        own = np.concatenate([np.arange(128 * (2 * i + pi), 128 * (2 * i + pi) + 128)
                              for i in range(8)])
        oth = np.concatenate([np.arange(128 * (2 * i + 1 - pi),
                                        128 * (2 * i + 1 - pi) + 128)
                              for i in range(8)])
        xoT = np.ascontiguousarray(x[b][own].T)
        xrT = np.ascontiguousarray(x[b][oth].T)
        mD_add, mO_add, mDB, mOB = _masks(pi)
        in_maps.append({
            "xoT": xoT, "xrT": xrT, "wqT": wqT, "q2l": q2l, "wkT": wkT, "wvT": wvT,
            "vupT": vupT, "woT": woT, "wg": wg, "wlin": wlin, "idn": idn,
            "mD_add": mD_add, "mO_add": mO_add, "mDB_mult": mDB, "mOB_mult": mOB,
        })
    res = run_bass_kernel_spmd(nc, in_maps, core_ids=list(range(8)))
    out = np.empty((4, 2048, 2048), dtype=np.float32)
    for c in range(8):
        b, pi = c // 2, c % 2
        own = np.concatenate([np.arange(128 * (2 * i + pi), 128 * (2 * i + pi) + 128)
                              for i in range(8)])
        out[b][own] = res.results[c]["outT"].T
    return out


# version 13
# speedup vs baseline: 1.1283x; 1.0242x over previous
import numpy as np
import contextlib
import concourse.bass as bass
import concourse.mybir as mybir
from concourse.bass_utils import run_bass_kernel_spmd

F32 = mybir.dt.float32
F32R = mybir.dt.float32r
BF16 = mybir.dt.bfloat16
FP8 = mybir.dt.float8e4
AF = mybir.ActivationFunctionType
ALU = mybir.AluOpType
PM = mybir.MatmulPerfMode
AX = mybir.AxisListType

E = 2048
TS = 1024
H, L = 16, 64
HID = 8192
NCH = 16
INV = float(1.0 / np.sqrt(L))
RAH = float(0.5 / np.sqrt(L))
PAD = 78.0

ENGS = ("sync", "gpsimd", "tensor", "vector", "scalar")


class Plan:
    def __init__(self):
        self.pend = {e: [] for e in ENGS}
        self.cnt = dict(dS=0, dS2=0, dG=0, dG2=0, dA=0, dA2=0, T=0, V=0, A=0, PL=0)
        self.rr = {"sync": 0, "gpsimd": 0, "scalar": 0}
        self.bank_started = {}
        self.barrier_waits = []
        self.barrier_pending = {e: False for e in ENGS}

    def now(self, *sems):
        return [(s, self.cnt[s]) for s in sems]

    def set_barrier(self):
        self.barrier_waits = [(s, self.cnt[s]) for s in self.cnt]
        self.barrier_pending = {e: True for e in ENGS}

    def emit(self, eng, fn, waits=(), inc=None, amt=None):
        if amt is None:
            amt = 16 if inc in ("dS", "dG", "dS2", "dG2", "dA", "dA2") else 1
        w = list(waits)
        if inc in ("dS", "dG", "dS2", "dG2", "dA", "dA2"):
            w = [(inc, self.cnt[inc])] + w
        if eng == "vector":
            w = [("V", self.cnt["V"])] + w
        elif eng == "scalar":
            w = [("A", self.cnt["A"])] + w
        elif eng == "gpsimd" and inc == "PL":
            w = [("PL", self.cnt["PL"])] + w
        if self.barrier_pending[eng]:
            w = self.barrier_waits + w
            self.barrier_pending[eng] = False
        self.pend[eng].append((fn, w, inc, amt))
        if inc is not None:
            self.cnt[inc] += amt
        return self.cnt.get(inc)

    def first_touch(self, bank_key):
        if bank_key in self.bank_started:
            return False
        self.bank_started[bank_key] = True
        return True

    def new_epoch(self):
        self.bank_started = {}


def build_program(debug=()):
    nc = bass.Bass("TRN2", target_bir_lowering=False, debug=False)
    dt = nc.dram_tensor
    xoT = dt("xoT", [E, TS], F32, kind="ExternalInput").ap()
    xrT = dt("xrT", [E, TS], F32, kind="ExternalInput").ap()
    wqT = dt("wqT", [E, E], F32, kind="ExternalInput").ap()
    q2li = dt("q2l", [H, 128, L], F32, kind="ExternalInput").ap()
    wkT = dt("wkT", [E, L], F32, kind="ExternalInput").ap()
    wvT = dt("wvT", [E, L], F32, kind="ExternalInput").ap()
    vupTi = dt("vupT", [H, 128, L], F32, kind="ExternalInput").ap()
    woT = dt("woT", [E, E], F32, kind="ExternalInput").ap()
    wg = dt("wg", [2, E, HID], F32, kind="ExternalInput").ap()
    wlin = dt("wlin", [HID, E], F32, kind="ExternalInput").ap()
    idn = dt("idn", [128, 128], F32, kind="ExternalInput").ap()
    mD_add = dt("mD_add", [128, 128], F32, kind="ExternalInput").ap()
    mO_add = dt("mO_add", [128, 128], F32, kind="ExternalInput").ap()
    mDB_mult = dt("mDB_mult", [128, 128], F32, kind="ExternalInput").ap()
    mOB_mult = dt("mOB_mult", [128, 256], F32, kind="ExternalInput").ap()
    outT = dt("outT", [E, TS], F32, kind="ExternalOutput").ap()
    dbg_out = {}
    for name, shape in debug:
        dbg_out[name] = dt("dbg_" + name, shape, F32, kind="ExternalOutput").ap()

    P = Plan()
    st_per = contextlib.ExitStack()
    st_kv = contextlib.ExitStack()
    st_p1 = contextlib.ExitStack()
    st_p2 = contextlib.ExitStack()
    st_p3 = contextlib.ExitStack()
    st_p4 = contextlib.ExitStack()
    st_p5 = contextlib.ExitStack()

    def hiv(ap_f32):
        return ap_f32.bitcast(BF16)[..., 1::2]

    es = contextlib.ExitStack()
    pA = es.enter_context(nc.psum_tensor("pA", [128, 1024], F32))
    pB = es.enter_context(nc.psum_tensor("pB", [128, 1024], F32))
    pC = es.enter_context(nc.psum_tensor("pC", [128, 512], F32))
    pD = es.enter_context(nc.psum_tensor("pD", [128, 512], F32))
    pE = es.enter_context(nc.psum_tensor("pE", [128, 512], F32))
    pF = es.enter_context(nc.psum_tensor("pF", [128, 512], F32))
    sem_dS = es.enter_context(nc.semaphore("dS"))
    sem_dS2 = es.enter_context(nc.semaphore("dS2"))
    sem_dG = es.enter_context(nc.semaphore("dG"))
    sem_dG2 = es.enter_context(nc.semaphore("dG2"))
    sem_T = es.enter_context(nc.semaphore("T"))
    sem_V = es.enter_context(nc.semaphore("V"))
    sem_A = es.enter_context(nc.semaphore("A"))
    sem_PL = es.enter_context(nc.semaphore("PL"))
    sem_dA = es.enter_context(nc.semaphore("dA"))
    sem_dA2 = es.enter_context(nc.semaphore("dA2"))
    with (
        nc.Block() as block,
    ):
        sems = {"dS": sem_dS, "dS2": sem_dS2, "dG": sem_dG, "dG2": sem_dG2,
                "T": sem_T, "V": sem_V, "A": sem_A, "PL": sem_PL,
                "dA": sem_dA, "dA2": sem_dA2}
        wms = {e: {} for e in ENGS}
        PSNAME = {id(pA): "A", id(pB): "B", id(pC): "C", id(pD): "D",
                  id(pE): "E", id(pF): "F"}

        def S(fn, waits=()):
            P.rr["sync"] ^= 1
            sem = "dS" if P.rr["sync"] else "dS2"
            return (sem, P.emit("sync", fn, waits, sem))

        def G(fn, waits=()):
            P.rr["gpsimd"] ^= 1
            sem = "dG" if P.rr["gpsimd"] else "dG2"
            return (sem, P.emit("gpsimd", fn, waits, sem))

        def AD(fn, waits=()):
            P.rr["scalar"] ^= 1
            sem = "dA" if P.rr["scalar"] else "dA2"
            return (sem, P.emit("scalar", fn, waits, sem))

        def T(fn, waits=()):
            return P.emit("tensor", fn, waits, "T")

        def GC(fn, waits=()):
            return P.emit("gpsimd", fn, waits, "PL")

        def V(fn, waits=()):
            return P.emit("vector", fn, waits, "V")

        def A(fn, waits=()):
            return P.emit("scalar", fn, waits, "A")

        def flush():
            for name in ENGS:
                ops = P.pend[name]
                if not ops:
                    continue
                P.pend[name] = []
                wm = wms[name]

                def body(eng, ops=ops, wm=wm):
                    for fn, waits, inc, amt in ops:
                        for s, v in waits:
                            if v > 0 and wm.get(s, 0) < v:
                                eng.wait_ge(sems[s], v)
                                wm[s] = v
                        ins = fn(eng)
                        if inc is not None:
                            ins.then_inc(sems[inc], amt)
                getattr(block, name)(body)

        def barrier():
            return P.now("dS", "dS2", "dG", "dG2", "dA", "dA2", "T", "V", "A", "PL")

        def tap(name, src_ap, rows, cols):
            if name in dbg_out:
                w = barrier()
                G((lambda name, src_ap, rows, cols: lambda e: e.dma_start(
                    out=dbg_out[name][0:rows, 0:cols], in_=src_ap))(name, src_ap, rows, cols),
                  waits=w)

        def alloc(st, name, shape, dtype):
            return st.enter_context(nc.sbuf_tensor(name, shape, dtype))

        ones128 = alloc(st_per, "ones128", [128, 1], F32)
        ones128b = alloc(st_per, "ones128b", [128, 1], BF16)
        onesrow = alloc(st_per, "onesrow", [1, 128], F32)
        m65 = alloc(st_per, "m65", [65, 1], BF16)
        mDadd = alloc(st_per, "mDadd", [128, 128], BF16)
        mOadd = alloc(st_per, "mOadd", [128, 128], BF16)
        mDBm = alloc(st_per, "mDBm", [128, 128], BF16)
        mOBm = alloc(st_per, "mOBm", [128, 256], BF16)
        bcM = alloc(st_per, "bcM", [128, TS], F32)
        bcR = alloc(st_per, "bcR", [128, TS], F32)
        srow = alloc(st_per, "srow", [16, TS], F32)
        d128 = alloc(st_per, "d128", [128, 128], F32)
        q128s = alloc(st_per, "q128s", [128, 128], F32)
        b128 = alloc(st_per, "b128", [128, 128], F32)
        s1s = alloc(st_per, "s1s", [128, 128], F32)
        AR = alloc(st_per, "AR", [128, 133120], mybir.dt.uint8)
        xho_hi = AR[:, 0:32768].bitcast(BF16).rearrange("p (c t) -> p c t", c=NCH)
        xho_lo = AR[:, 32768:65536].bitcast(BF16).rearrange("p (c t) -> p c t", c=NCH)
        ctxPK1 = AR[:, 0:32768].bitcast(BF16).rearrange("p (c t) -> p c t", c=H)
        ctxPK2 = AR[:, 32768:65536].bitcast(BF16).rearrange("p (c t) -> p c t", c=H)
        ctxh = ctxPK1[0:64]
        ctxl = ctxPK2[0:64]
        act8 = AR[:, 0:65536].bitcast(FP8).rearrange("p (c t) -> p c t", c=64)
        qhPK = AR[:, 65536:98304].bitcast(BF16).rearrange("p (c t) -> p c t", c=H)
        qh65 = qhPK[0:65]
        ql65 = AR[:, 98304:131072].bitcast(BF16).rearrange("p (c t) -> p c t", c=H)[0:65]
        qsum3p = AR[:, 131072:133120].bitcast(BF16)[0:48]
        xp = AR[:, 65536:131072].bitcast(F32).rearrange("p (c t) -> p c t", c=NCH)
        kh65 = alloc(st_kv, "kh65", [65, 2 * TS], BF16)
        khl = alloc(st_kv, "khl", [128, 2 * TS], BF16)
        onesP64 = alloc(st_kv, "onesP64", [65, 64], F32)
        vtmh = alloc(st_kv, "vtmh", [128, 16, 65], BF16)
        vtml = alloc(st_kv, "vtml", [128, 16, 65], BF16)
        konT = alloc(st_kv, "konT", [65, TS], F32)
        ksc3 = alloc(st_kv, "ksc3", [3, 2 * TS], BF16)
        kstat = alloc(st_kv, "kstat", [128, 32], BF16)
        wmx16 = alloc(st_kv, "wmx16", [128, 128], F32)
        wmn16 = alloc(st_kv, "wmn16", [128, 128], F32)
        wrow = alloc(st_kv, "wrow", [1, 48], F32)
        idns = alloc(st_kv, "idns", [128, 128], F32)
        segm = alloc(st_kv, "segm", [1, 64], F32)

        G(lambda e: e.dma_start(out=mDadd[:], in_=mD_add[:, :]))
        G(lambda e: e.dma_start(out=mOadd[:], in_=mO_add[:, :]))
        G(lambda e: e.dma_start(out=mDBm[:], in_=mDB_mult[:, :]))
        G(lambda e: e.dma_start(out=mOBm[:], in_=mOB_mult[:, :]))
        c0 = P.now("dS", "dS2", "dG", "dG2")
        V(lambda e: e.memset(ones128[:], 1.0), waits=c0)
        V(lambda e: e.memset(ones128b[:], 1.0))
        V(lambda e: e.memset(onesrow[:], 1.0))
        V(lambda e: e.memset(m65[0:64, :], 1.0))
        V(lambda e: e.memset(m65[64:65, :], 0.5))
        V(lambda e: e.memset(onesP64[64:65, :], 1.0))
        P.set_barrier()

        kvT = AR[:, 65536:73728].bitcast(F32)
        kvTlo = AR[:, 73728:77824].bitcast(BF16)
        kscrow = AR[:, 77824:86016].bitcast(F32)[0:1]
        xc = AR[:, 86016:94208].bitcast(F32).rearrange("p (a t) -> p a t", a=2)
        kminb = AR[:, 94208:94272].bitcast(BF16)[0:64]
        tr1 = AR[:, 94272:98368].bitcast(BF16)[0:1]
        mrow = alloc(st_p1, "mrow", [1, TS], F32)
        rrow = alloc(st_p1, "rrow", [1, TS], F32)
        scr1 = alloc(st_p1, "scr1", [1, TS], F32)
        scr2 = alloc(st_p1, "scr2", [1, TS], F32)
        sq = alloc(st_p1, "sq", [128, TS], F32)
        uu = sq
        xrh = alloc(st_p1, "xrh", [128, 2, TS], BF16)
        xrl = alloc(st_p1, "xrl", [128, 2, TS], BF16)
        kvW = AR[:, 98368:106560].bitcast(F32).rearrange("p (c l) -> p c l", c=NCH)
        kvWlo = AR[:, 106560:110656].bitcast(BF16).rearrange("p (c l) -> p c l", c=NCH)
        kW65 = AR[:, 110656:114816].bitcast(F32).rearrange("p (c l) -> p c l", c=NCH)
        kW65lo = AR[:, 114816:116896].bitcast(BF16).rearrange("p (c l) -> p c l", c=NCH)

        S(lambda e: e.dma_start(out=idns[:], in_=idn[:, :]))
        S(lambda e: e.dma_start(out=kvW[:, :, 0:64],
                                in_=wkT.rearrange("(c p) l -> p c l", p=128)))
        S(lambda e: e.dma_start(out=kvW[:, :, 64:128],
                                in_=wvT.rearrange("(c p) l -> p c l", p=128)))
        S(lambda e: e.dma_start(out=kW65[:, :, 0:64],
                                in_=wkT.rearrange("(c p) l -> p c l", p=128)))
        wl_ = P.now("dS", "dS2")
        V(lambda e: e.tensor_tensor(out=kvWlo[:].rearrange("p c l -> p (c l)"),
                                    in0=kvW[:].rearrange("p c l -> p (c l)"),
                                    in1=hiv(kvW[:].rearrange("p c l -> p (c l)")),
                                    op=ALU.subtract), waits=wl_)
        for c in range(NCH):
            V((lambda c: lambda e: e.tensor_reduce(
                out=kW65[:, c, 64:65], in_=kW65[:, c, 0:64], axis=AX.X, op=ALU.add))(c),
              waits=wl_)
        V(lambda e: e.tensor_tensor(out=kW65lo[:].rearrange("p c l -> p (c l)"),
                                    in0=kW65[:].rearrange("p c l -> p (c l)"),
                                    in1=hiv(kW65[:].rearrange("p c l -> p (c l)")),
                                    op=ALU.subtract))

        def ln_rows(src_dram):
            P.new_epoch()
            sw = P.now("T", "V", "A")
            done = {-2: sw, -1: sw}
            for c in range(NCH):
                buf = c % 2
                ld = S((lambda c, buf: lambda e: e.dma_start(
                    out=xc[:, buf, :], in_=src_dram[128 * c:128 * (c + 1), :]))(c, buf),
                    waits=done.get(c - 2, []))
                lw = [ld]
                A((lambda buf: lambda e: e.activation(
                    out=sq[:].bitcast(BF16)[:, 0:TS], in_=xc[:, buf, :],
                    func=AF.Square))(buf),
                  waits=lw + (P.now("T") if c > 0 else []))
                for qb in range(2):
                    T((lambda c, buf, qb, st: lambda e: e.matmul(
                        pA[0:1, 512 * qb:512 * (qb + 1)], ones128b[:],
                        hiv(xc[:, buf, 512 * qb:512 * (qb + 1)]),
                        start=st, stop=False, skip_group_check=True))(
                        c, buf, qb, P.first_touch(("A", qb))), waits=lw)
                aw = P.now("A")
                for qb in range(2):
                    T((lambda qb, st: lambda e: e.matmul(
                        pA[32:33, 512 * qb:512 * (qb + 1)], ones128b[:],
                        sq[:].bitcast(BF16)[:, 512 * qb:512 * (qb + 1)],
                        start=st, stop=False, skip_group_check=True))(
                        qb, P.first_touch(("A32", qb))), waits=aw)
                done[c] = P.now("T", "A")
            tw = P.now("T")
            V(lambda e: e.tensor_scalar(out=mrow[:, :], in0=pA[0:1, :], scalar1=1.0 / E,
                                        scalar2=0.0, op0=ALU.mult, op1=ALU.add), waits=tw)
            V(lambda e: e.tensor_scalar(out=rrow[:, :], in0=pA[32:33, :],
                                        scalar1=1.0 / E, scalar2=1e-5,
                                        op0=ALU.mult, op1=ALU.add))
            V(lambda e: e.scalar_tensor_tensor(out=scr1[:, :], in0=mrow[:, :],
                                               scalar=-1.0, in1=mrow[:, :],
                                               op0=ALU.mult, op1=ALU.mult))
            V(lambda e: e.tensor_tensor(out=rrow[:, :], in0=rrow[:, :], in1=scr1[:, :],
                                        op=ALU.add))
            V(lambda e: e.reciprocal(out=scr1[:, :], in_=rrow[:, :]))
            vw = P.now("V")
            A(lambda e: e.activation(out=scr1[:, :], in_=scr1[:, :], func=AF.Sqrt),
              waits=vw)
            aw = P.now("A")
            V(lambda e: e.tensor_tensor(out=scr2[:, :], in0=scr1[:, :], in1=scr1[:, :],
                                        op=ALU.mult), waits=aw)
            V(lambda e: e.tensor_tensor(out=scr2[:, :], in0=scr2[:, :], in1=rrow[:, :],
                                        op=ALU.mult))
            V(lambda e: e.tensor_scalar(out=scr2[:, :], in0=scr2[:, :], scalar1=-0.5,
                                        scalar2=1.5, op0=ALU.mult, op1=ALU.add))
            V(lambda e: e.tensor_tensor(out=rrow[:, :], in0=scr1[:, :], in1=scr2[:, :],
                                        op=ALU.mult))
            P.new_epoch()
            vw = P.now("V")
            for srcrow, dst in ((mrow, bcM), (rrow, bcR)):
                for qb in range(2):
                    T((lambda srcrow, qb, st: lambda e: e.matmul(
                        pB[:, 512 * qb:512 * (qb + 1)], onesrow[:, :],
                        srcrow[:, 512 * qb:512 * (qb + 1)], start=st, stop=False,
                        skip_group_check=True))(srcrow, qb, True), waits=vw)
                tw = P.now("T")
                V((lambda dst: lambda e: e.tensor_copy(out=dst[:], in_=pB[:]))(dst),
                  waits=tw)
                vw = P.now("V")

        def ln_apply(src_dram, hi_dst, lo_dst, mm_fn):
            sw = P.now("T", "V", "A")
            done = {-2: sw, -1: sw}
            for c in range(NCH):
                buf = c % 2
                ld = S((lambda c, buf: lambda e: e.dma_start(
                    out=xc[:, buf, :], in_=src_dram[128 * c:128 * (c + 1), :]))(c, buf),
                    waits=done.get(c - 2, []))
                V((lambda buf: lambda e: e.tensor_tensor(
                    out=uu[:], in0=xc[:, buf, :], in1=bcM[:], op=ALU.subtract))(buf),
                  waits=[ld])
                V(lambda e: e.tensor_tensor(out=uu[:], in0=uu[:], in1=bcR[:],
                                            op=ALU.mult))
                hd, ldst = hi_dst(c), lo_dst(c)
                V((lambda hd: lambda e: e.tensor_copy(out=hd, in_=uu[:]))(hd))
                V((lambda hd, ldst: lambda e: e.tensor_tensor(
                    out=ldst, in0=uu[:], in1=hd, op=ALU.subtract))(hd, ldst))
                mm_fn(c)
                done[c] = P.now("T", "V")

        ln_rows(xoT)
        P.new_epoch()

        def latent_own_mm(c):
            vw = P.now("V")
            for j in range(8):
                ps, col = (pC, 128 * j) if j < 4 else (pD, 128 * (j - 4))
                for pi, (wap, xap) in enumerate((
                        (hiv(kvW[:, c, :]), xho_hi[:, c, 128 * j:128 * (j + 1)]),
                        (hiv(kvW[:, c, :]), xho_lo[:, c, 128 * j:128 * (j + 1)]),
                        (kvWlo[:, c, :], xho_hi[:, c, 128 * j:128 * (j + 1)]))):
                    st = P.first_touch((PSNAME[id(ps)], col // 512))
                    T((lambda ps, col, wap, xap, st: lambda e: e.matmul(
                        ps[:, col:col + 128], wap, xap, start=st, stop=False,
                        skip_group_check=True))(ps, col, wap, xap, st), waits=vw)
            for qb in range(2):
                ps = pE if qb == 0 else pF
                for pi, (wap, xap) in enumerate((
                        (hiv(kW65[:, c, :]), xho_hi[:, c, 512 * qb:512 * (qb + 1)]),
                        (hiv(kW65[:, c, :]), xho_lo[:, c, 512 * qb:512 * (qb + 1)]),
                        (kW65lo[:, c, :], xho_hi[:, c, 512 * qb:512 * (qb + 1)]))):
                    st = P.first_touch((PSNAME[id(ps)], 0))
                    T((lambda ps, wap, xap, st: lambda e: e.matmul(
                        ps[0:65, :], wap, xap, start=st, stop=False,
                        skip_group_check=True))(ps, wap, xap, st), waits=vw)

        ln_apply(xoT, lambda c: xho_hi[:, c, :], lambda c: xho_lo[:, c, :], latent_own_mm)
        tw = P.now("T")
        for j in range(8):
            ps, col = (pC, 128 * j) if j < 4 else (pD, 128 * (j - 4))
            V((lambda j, ps, col: lambda e: e.tensor_scalar(
                out=kvT[0:64, 128 * j:128 * (j + 1)], in0=ps[0:64, col:col + 128],
                scalar1=INV, scalar2=0.0, op0=ALU.mult, op1=ALU.add))(j, ps, col),
              waits=tw)
            V((lambda j, ps, col: lambda e: e.tensor_copy(
                out=kvT[64:128, 128 * j:128 * (j + 1)],
                in_=ps[64:128, col:col + 128]))(j, ps, col))
        V(lambda e: e.tensor_scalar(out=konT[:, 0:512], in0=pE[0:65, :], scalar1=INV,
                                    scalar2=0.0, op0=ALU.mult, op1=ALU.add), waits=tw)
        V(lambda e: e.tensor_scalar(out=konT[:, 512:1024], in0=pF[0:65, :], scalar1=INV,
                                    scalar2=0.0, op0=ALU.mult, op1=ALU.add))

        ln_rows(xrT)
        P.new_epoch()

        def latent_oth_mm(c):
            vw = P.now("V")
            buf = c % 2
            for j in range(8):
                ps, col = (pC, 128 * j) if j < 4 else (pD, 128 * (j - 4))
                for pi, (wap, xap) in enumerate((
                        (hiv(kvW[:, c, :]), xrh[:, buf, 128 * j:128 * (j + 1)]),
                        (hiv(kvW[:, c, :]), xrl[:, buf, 128 * j:128 * (j + 1)]),
                        (kvWlo[:, c, :], xrh[:, buf, 128 * j:128 * (j + 1)]))):
                    st = P.first_touch((PSNAME[id(ps)], col // 512))
                    T((lambda ps, col, wap, xap, st: lambda e: e.matmul(
                        ps[:, col:col + 128], wap, xap, start=st, stop=False,
                        skip_group_check=True))(ps, col, wap, xap, st), waits=vw)

        ln_apply(xrT, lambda c: xrh[:, c % 2, :], lambda c: xrl[:, c % 2, :], latent_oth_mm)
        tw = P.now("T")
        for j in range(8):
            ps, col = (pC, 128 * j) if j < 4 else (pD, 128 * (j - 4))
            V((lambda j, ps, col: lambda e: e.tensor_scalar(
                out=kvT[0:64, 1024 + 128 * j:1024 + 128 * (j + 1)],
                in0=ps[0:64, col:col + 128], scalar1=INV, scalar2=0.0,
                op0=ALU.mult, op1=ALU.add))(j, ps, col), waits=tw)
            V((lambda j, ps, col: lambda e: e.tensor_copy(
                out=kvT[64:128, 1024 + 128 * j:1024 + 128 * (j + 1)],
                in_=ps[64:128, col:col + 128]))(j, ps, col))

        V(lambda e: e.tensor_tensor(out=kvTlo[:], in0=kvT[:], in1=hiv(kvT[:]),
                                    op=ALU.subtract))
        V(lambda e: e.tensor_copy(out=kh65[0:64, :], in_=hiv(kvT[0:64, :])))
        V(lambda e: e.memset(kh65[64:65, :], 1.0))
        V(lambda e: e.tensor_copy(out=khl[0:64, :], in_=hiv(kvT[0:64, :])))
        vwk = P.now("V")
        G(lambda e: e.dma_start(out=khl[64:128, :], in_=kvTlo[0:64, :]), waits=vwk)
        P.new_epoch()
        vw = P.now("V")
        for qq in range(4):
            ps = pA if qq < 2 else pB
            col = 512 * (qq % 2)
            for half in range(2):
                src = (hiv(kvT[0:64, 512 * qq:512 * (qq + 1)]) if half == 0
                       else kvTlo[0:64, 512 * qq:512 * (qq + 1)])
                st = P.first_touch((PSNAME[id(ps)], col // 512))
                T((lambda ps, col, src, st: lambda e: e.matmul(
                    ps[0:1, col:col + 512], ones128b[0:64, :], src, start=st,
                    stop=False, skip_group_check=True))(ps, col, src, st), waits=vw)
        tw = P.now("T")
        V(lambda e: e.tensor_scalar(out=kscrow[0:1, 0:1024], in0=pA[0:1, :], scalar1=0.5,
                                    scalar2=0.0, op0=ALU.mult, op1=ALU.add), waits=tw)
        V(lambda e: e.tensor_scalar(out=kscrow[0:1, 1024:2048], in0=pB[0:1, :],
                                    scalar1=0.5, scalar2=0.0, op0=ALU.mult, op1=ALU.add))
        V(lambda e: e.tensor_copy(out=ksc3[0:1, :], in_=hiv(kscrow[:])))
        V(lambda e: e.tensor_tensor(out=tr1[0:1, :], in0=kscrow[:], in1=hiv(kscrow[:]),
                                    op=ALU.subtract))
        vw = P.now("V")
        G(lambda e: e.dma_start(out=ksc3[1:2, :], in_=tr1[0:1, :]), waits=vw)
        G(lambda e: e.dma_start(out=ksc3[2:3, :], in_=ksc3[0:1, :]), waits=vw)
        V(lambda e: e.tensor_reduce(out=segm[0:1, 0:32],
                                    in_=kscrow[:].rearrange("o (s w) -> o s w", w=64),
                                    axis=AX.X, op=ALU.max))
        V(lambda e: e.tensor_reduce(out=segm[0:1, 32:64],
                                    in_=kscrow[:].rearrange("o (s w) -> o s w", w=64),
                                    axis=AX.X, op=ALU.min))
        for r, op in ((0, ALU.max), (1, ALU.min)):
            o = 24 * r
            b = 32 * r
            V((lambda op, o, b: lambda e: e.tensor_tensor(
                out=wrow[0:1, o:o + 8], in0=segm[0:1, b + 0:b + 16:2],
                in1=segm[0:1, b + 1:b + 16:2], op=op))(op, o, b))
            V((lambda op, o, b: lambda e: e.tensor_tensor(
                out=wrow[0:1, o + 8:o + 16], in0=segm[0:1, b + 16:b + 32:2],
                in1=segm[0:1, b + 17:b + 32:2], op=op))(op, o, b))
            V((lambda op, o, b: lambda e: e.tensor_tensor(
                out=wrow[0:1, o + 16:o + 24], in0=segm[0:1, b + 14:b + 30:2],
                in1=segm[0:1, b + 15:b + 30:2], op=op))(op, o, b))
            V((lambda op, o: lambda e: e.tensor_tensor(
                out=wrow[0:1, o:o + 8], in0=wrow[0:1, o:o + 8],
                in1=wrow[0:1, o + 8:o + 16], op=op))(op, o))
            V((lambda op, o: lambda e: e.tensor_tensor(
                out=wrow[0:1, o:o + 8], in0=wrow[0:1, o:o + 8],
                in1=wrow[0:1, o + 16:o + 24], op=op))(op, o))
        P.new_epoch()
        vw = P.now("V")
        T((lambda st: lambda e: e.matmul(pC[:, 0:8], onesrow[:, :], wrow[0:1, 0:8],
                                         start=st, stop=False, skip_group_check=True))(
            P.first_touch(("C", 0))), waits=vw)
        T(lambda e: e.matmul(pC[:, 8:16], onesrow[:, :], wrow[0:1, 24:32], start=False,
                             stop=False, skip_group_check=True))
        tw = P.now("T")
        for h in range(H):
            V((lambda h: lambda e: e.tensor_copy(out=wmx16[:, 8 * h:8 * h + 8],
                                                 in_=pC[:, 0:8]))(h), waits=tw)
            V((lambda h: lambda e: e.tensor_copy(out=wmn16[:, 8 * h:8 * h + 8],
                                                 in_=pC[:, 8:16]))(h))
        for b in range(32):
            V((lambda b: lambda e: e.tensor_reduce(
                out=kstat[0:64, b:b + 1], in_=kvT[0:64, 64 * b:64 * (b + 1)],
                axis=AX.X, op=ALU.max))(b))
            V((lambda b: lambda e: e.tensor_reduce(
                out=kminb[:, b:b + 1], in_=kvT[0:64, 64 * b:64 * (b + 1)],
                axis=AX.X, op=ALU.min))(b))
        vw = P.now("V")
        G(lambda e: e.dma_start(out=kstat[64:128, :], in_=kminb[:, :]), waits=vw)
        for kt in range(16):
            vw2 = P.now("V")
            T((lambda kt: lambda e: e.transpose(
                pD[:, 0:64], kvT[64:128, 128 * kt:128 * (kt + 1)],
                idns[64:128, 64:128]))(kt), waits=vw2)
            tw = P.now("T")
            V((lambda kt: lambda e: e.tensor_copy(
                out=vtmh[:, kt, 0:64], in_=pD[:, 0:64]))(kt), waits=tw)
            V((lambda kt: lambda e: e.tensor_tensor(
                out=vtml[:, kt, 0:64], in0=pD[:, 0:64], in1=vtmh[:, kt, 0:64],
                op=ALU.subtract))(kt))
        V(lambda e: e.memset(vtmh[:, :, 64:65].rearrange("p c l -> p (c l)"), 1.0))
        V(lambda e: e.memset(vtml[:, :, 64:65].rearrange("p c l -> p (c l)"), 0.0))
        tap("xho_hi", xho_hi[:, 0, :], 128, TS)
        tap("kvT", kvT[:, :], 128, 2 * TS)
        tap("konT", konT[0:65, :], 65, TS)
        tap("kscrow", kscrow[0:1, :], 1, 2 * TS)
        flush()
        st_p1.close()
        P.set_barrier()

        qlatH = alloc(st_p2, "qlatH", [65, 2, TS], F32)
        wqc = alloc(st_p2, "wqc", [128, 2, 512], F32)
        wqlo = alloc(st_p2, "wqlo", [128, 2, 512], BF16)
        q2l65 = alloc(st_p2, "q2l65", [128, H, 65], F32)
        q2l65lo = alloc(st_p2, "q2l65lo", [128, H, 65], BF16)
        qev32 = alloc(st_p2, "qev32", [128, 2, 512], F32)
        qevl = alloc(st_p2, "qevl", [128, 2, 512], BF16)
        qpn = alloc(st_p2, "qpn", [128, 2, TS], BF16)
        qntmp = alloc(st_p2, "qntmp", [64, TS], BF16)
        prodx = alloc(st_p2, "prodx", [65, 2, TS], BF16)
        qloS = alloc(st_p2, "qloS", [65, TS], BF16)

        bar = barrier()
        S(lambda e: e.dma_start(out=q2l65[:, :, 0:64],
                                in_=q2li.rearrange("h d l -> d h l")), waits=bar)
        q2w = P.now("dS", "dS2")
        for h in range(H):
            V((lambda h: lambda e: e.tensor_reduce(
                out=q2l65[:, h, 64:65], in_=q2l65[:, h, 0:64], axis=AX.X, op=ALU.add))(h),
              waits=q2w + bar)
        V(lambda e: e.tensor_tensor(out=q2l65lo[:].rearrange("p h l -> p (h l)"),
                                    in0=q2l65[:].rearrange("p h l -> p (h l)"),
                                    in1=hiv(q2l65[:].rearrange("p h l -> p (h l)")),
                                    op=ALU.subtract))
        V(lambda e: e.memset(qh65[:, :, :][64:65].rearrange("o h t -> o (h t)"), 0.0))

        PSQ = [(pA, 0), (pA, 512), (pB, 0), (pB, 512), (pC, 0), (pD, 0), (pE, 0),
               (pF, 0)]
        doneQ = {}
        qTg = {}
        eqT = {}
        mach_t = {}
        rowvw = {}
        gw2D = {}
        qdma_w = {}

        def emit_eq(grp, hi_):
            h = 4 * grp + hi_
            hb = h % 2
            for qb in range(2):
                ps, col = PSQ[hi_ * 2 + qb]
                A((lambda ps, col, qb: lambda e: e.activation(
                    out=qev32[:, qb, :], in_=ps[:, col:col + 512], func=AF.Copy))(
                    ps, col, qb), waits=qTg[grp] + eqT.get((h - 1, qb), []))
                aw = P.now("A")
                V((lambda qb: lambda e: e.tensor_tensor(
                    out=qevl[:, qb, :], in0=qev32[:, qb, :],
                    in1=hiv(qev32[:, qb, :]),
                    op=ALU.subtract))(qb), waits=aw)
                vw = P.now("V")
                for pi in (0, 2, 1):
                    wap = (hiv(q2l65[:, h, :]) if pi < 2 else q2l65lo[:, h, :])
                    xap = hiv(qev32[:, qb, :]) if pi != 1 else qevl[:, qb, :]
                    T((lambda ps, col, wap, xap, st: lambda e: e.matmul(
                        ps[0:65, col:col + 512], wap, xap, start=st, stop=False,
                        skip_group_check=True))(ps, col, wap, xap, pi == 0),
                      waits=(aw if pi != 1 else vw))
                eqT[(h, qb)] = [("T", P.cnt["T"])]
                tw = P.now("T")
                V((lambda ps, col, qb, hb: lambda e: e.tensor_copy(
                    out=qlatH[:, hb, 512 * qb:512 * (qb + 1)],
                    in_=ps[0:65, col:col + 512]))(ps, col, qb, hb), waits=tw)
            mw = mach_t.get(h - 2, [])
            V((lambda h, hb: lambda e: e.tensor_copy(
                out=qh65[:, h, :][0:64], in_=hiv(qlatH[0:64, hb, :])))(h, hb))
            vwq = P.now("V")
            G((lambda h: lambda e: e.dma_start(
                out=qhPK[64:128, h, :], in_=qhPK[0:64, h, :]))(h), waits=vwq)
            V((lambda h, hb: lambda e: e.tensor_tensor(
                out=ql65[:, h, :][0:64], in0=qlatH[0:64, hb, :],
                in1=qh65[:, h, :][0:64], op=ALU.subtract))(h, hb))
            V((lambda hb: lambda e: e.tensor_tensor(
                out=prodx[0:64, hb, :], in0=qlatH[0:64, hb, :], in1=konT[0:64, :],
                op=ALU.mult))(hb), waits=mw)
            V((lambda hb: lambda e: e.tensor_tensor(
                out=prodx[64:65, hb, :], in0=qlatH[64:65, hb, :], in1=konT[64:65, :],
                op=ALU.mult))(hb))
            V((lambda hb: lambda e: e.tensor_scalar(
                out=qpn[0:64, hb, :], in0=qlatH[0:64, hb, :], scalar1=0.0,
                scalar2=0.0, op0=ALU.max, op1=ALU.add))(hb))
            V((lambda hb: lambda e: e.tensor_scalar(
                out=qntmp[:, :], in0=qlatH[0:64, hb, :], scalar1=0.0, scalar2=0.0,
                op0=ALU.min, op1=ALU.add))(hb), waits=qdma_w.get(h - 1, []))
            V((lambda hb: lambda e: e.tensor_copy(
                out=qloS[64:65, :], in_=hiv(qlatH[64:65, hb, :])))(hb))
            vw = P.now("V")
            rowvw[h] = vw
            G((lambda hb: lambda e: e.dma_start(
                out=qpn[64:128, hb, :], in_=qntmp[:, :]))(hb), waits=vw)
            G((lambda h: lambda e: e.dma_start(
                out=qsum3p[3 * h:3 * h + 1, :], in_=qloS[64:65, :]))(h))
            G((lambda h: lambda e: e.dma_start(
                out=qsum3p[3 * h + 1:3 * h + 2, :], in_=qloS[64:65, :]))(h))
            ghw = P.now("dG", "dG2")
            V((lambda hb: lambda e: e.tensor_tensor(
                out=qloS[64:65, :], in0=qlatH[64:65, hb, :],
                in1=hiv(qlatH[64:65, hb, :]), op=ALU.subtract))(hb), waits=ghw)
            vw2 = P.now("V")
            G((lambda h: lambda e: e.dma_start(
                out=qsum3p[3 * h + 2:3 * h + 3, :], in_=qloS[64:65, :]))(h),
              waits=vw2)
            gw2D[h] = P.now("dG", "dG2")
            qdma_w[h] = gw2D[h]

        def emit_mach(grp, hi_):
            h = 4 * grp + hi_
            hb = h % 2
            ps0, col0 = PSQ[hi_ * 2]
            vw = rowvw[h]
            gw2 = gw2D[h]
            for qt in range(8):
                T((lambda ps0, col0, qt, hb, st: lambda e: e.matmul(
                    ps0[:, col0 + 256 + qt:col0 + 257 + qt],
                    prodx[:, hb, 128 * qt:128 * (qt + 1)], m65[:, :],
                    start=st, stop=False, skip_group_check=True))(
                    ps0, col0, qt, hb, qt == 0), waits=vw)
                T((lambda ps0, col0, qt, hb: lambda e: e.matmul(
                    ps0[:, col0 + 264 + qt:col0 + 265 + qt],
                    qlatH[64:65, hb, 128 * qt:128 * (qt + 1)], ones128[64:65, :],
                    start=False, stop=False, skip_group_check=True))(ps0, col0, qt, hb),
                  waits=vw)
                T((lambda ps0, col0, qt, hb: lambda e: e.matmul(
                    ps0[:, col0 + 32 * qt:col0 + 32 * (qt + 1)],
                    qpn[:, hb, 128 * qt:128 * (qt + 1)], kstat[:, :],
                    start=False, stop=False, skip_group_check=True))(ps0, col0, qt, hb),
                  waits=vw + gw2)
            mach_t[h] = [("T", P.cnt["T"])]
            tw = P.now("T")
            V((lambda ps0, col0, h: lambda e: e.tensor_reduce(
                out=b128[:, 8 * h:8 * h + 8],
                in_=ps0[:, col0:col0 + 256].rearrange("p (q b) -> p q b", b=32),
                axis=AX.X, op=ALU.max))(ps0, col0, h), waits=tw)
            V((lambda ps0, col0, h: lambda e: e.tensor_copy(
                out=d128[:, 8 * h:8 * h + 8],
                in_=ps0[:, col0 + 256:col0 + 264]))(ps0, col0, h))
            V((lambda ps0, col0, h: lambda e: e.tensor_copy(
                out=q128s[:, 8 * h:8 * h + 8],
                in_=ps0[:, col0 + 264:col0 + 272]))(ps0, col0, h))

        for grp in range(4):
            heads = list(range(4 * grp, 4 * grp + 4))
            P.new_epoch()
            for c in range(NCH):
                gc = grp * 16 + c
                buf = gc % 2
                ld = S((lambda c, buf, grp: lambda e: e.dma_start(
                    out=wqc[:, buf, :],
                    in_=wqT[128 * c:128 * (c + 1),
                            512 * grp:512 * (grp + 1)]))(c, buf, grp),
                    waits=doneQ.get(gc - 2, bar))
                V((lambda buf: lambda e: e.tensor_tensor(
                    out=wqlo[:, buf, :], in0=wqc[:, buf, :],
                    in1=hiv(wqc[:, buf, :]),
                    op=ALU.subtract))(buf), waits=[ld])
                vw = P.now("V")
                for hi_, h in enumerate(heads):
                    for qb in range(2):
                        ps, col = PSQ[hi_ * 2 + qb]
                        for pi in range(3):
                            wap = (hiv(wqc[:, buf, 128 * hi_:128 * (hi_ + 1)])
                                   if pi < 2
                                   else wqlo[:, buf, 128 * hi_:128 * (hi_ + 1)])
                            xap = (xho_hi if pi != 1 else xho_lo)[
                                :, c, 512 * qb:512 * (qb + 1)]
                            st = P.first_touch((PSNAME[id(ps)], col // 512))
                            T((lambda ps, col, wap, xap, st: lambda e: e.matmul(
                                ps[:, col:col + 512], wap, xap, start=st, stop=False,
                                skip_group_check=True))(ps, col, wap, xap, st), waits=vw)
                doneQ[gc] = P.now("T", "V")
            qTg[grp] = P.now("T")
            for hi_ in range(4):
                emit_eq(grp, hi_)
                if hi_ >= 1:
                    emit_mach(grp, hi_ - 1)
            emit_mach(grp, 3)
        flush()
        tap("qlat", qlatH[:, 1, :], 65, TS)

        V(lambda e: e.tensor_scalar(out=b128[:, :], in0=b128[:, :], scalar1=1.0,
                                    scalar2=2.0 - PAD, op0=ALU.mult, op1=ALU.add))
        V(lambda e: e.tensor_tensor(out=s1s[:, :], in0=q128s[:, :], in1=wmx16[:, :],
                                    op=ALU.mult))
        V(lambda e: e.tensor_tensor(out=wmx16[:, :], in0=q128s[:, :], in1=wmn16[:, :],
                                    op=ALU.mult))
        V(lambda e: e.tensor_tensor(out=s1s[:, :], in0=s1s[:, :], in1=wmx16[:, :],
                                    op=ALU.max))
        V(lambda e: e.tensor_tensor(out=b128[:, :], in0=b128[:, :], in1=s1s[:, :],
                                    op=ALU.add))
        V(lambda e: e.tensor_tensor(out=b128[:, :], in0=b128[:, :], in1=d128[:, :],
                                    op=ALU.max))
        V(lambda e: e.tensor_scalar(out=b128[:, :], in0=b128[:, :], scalar1=-1.0,
                                    scalar2=0.0, op0=ALU.mult, op1=ALU.add))
        vw = P.now("V")
        T(lambda e: e.matmul(pC[:, 0:128], b128[:, :], idns[:, :], start=True,
                             stop=False, skip_group_check=True, is_transpose=True),
          waits=vw)
        tw = P.now("T")
        V(lambda e: e.tensor_copy(out=s1s[:, :], in_=pC[:, 0:128]), waits=tw)
        vw = P.now("V")
        G(lambda e: e.dma_start(
            out=ql65[:, :, :][64:65].rearrange("o h (q p) -> o (h q p)", p=128),
            in_=s1s[:, :]), waits=vw)
        flush()
        tap("S128", b128[:, :], 128, 128)
        flush()
        st_p2.close()
        P.set_barrier()

        expf = alloc(st_p3, "expf", [128, 2, TS], F32)
        expl = alloc(st_p3, "expl", [128, 2, TS], BF16)
        mtmp = alloc(st_p3, "mtmp", [128, 256], F32)
        qs3b = alloc(st_p3, "qs3b", [3, 2, TS], BF16)
        rbc = alloc(st_p3, "rbc", [64, TS], F32)
        cxs = alloc(st_p3, "cxs", [64, TS], F32)
        cxu = alloc(st_p3, "cxu", [65, TS], F32)

        bar = barrier()
        P.new_epoch()
        mask_v = {}
        exp_a = {}
        lo_v = {}
        av_t = {}
        epi_v = {}
        epi_free = {}
        recipV = {}
        ktlist = []
        for j in range(8):
            ktlist.append(("own", j))
            ktlist.append(("oth", j))
        qs3w_d = {}
        seg_d = {}

        def emit_qdma(h):
            hb = h % 2
            G((lambda h, hb: lambda e: e.dma_start(
                out=qs3b[:, hb, :], in_=qsum3p[3 * h:3 * h + 3, :]))(h, hb),
              waits=(bar if h < 2 else av_t[(h - 2) * 16 + 15]))
            qs3w_d[h] = P.now("dG", "dG2")

        def emit_lr(g):
            h, idx = g // 16, g % 16
            if idx == 0:
                emit_qdma(h)
            typ, j = ktlist[idx]
            pL = pA if idx % 2 == 0 else pB
            kt = j if typ == "own" else 8 + j
            lcol = 128 * kt
            sfx = 128 * j
            W = TS - sfx
            hb = h % 2
            lw = (bar if g == 0 else []) + \
                ((exp_a[g - 2] + mask_v[g - 2]) if g >= 2 else [])
            cuts = sorted({sfx, TS} | ({512} if sfx < 512 else set())
                          | ({sfx + 512} if sfx + 512 < TS else set()))
            segs = [(cuts[i], cuts[i + 1] - cuts[i]) for i in range(len(cuts) - 1)]
            seg_d[g] = (segs, sfx, kt, W)
            seen_banks = set()
            for q0, w in segs:
                bank = (q0 - sfx) // 512
                fresh = bank not in seen_banks
                seen_banks.add(bank)
                for pi in range(2):
                    wap = (khl[:, lcol:lcol + 128] if pi == 0
                           else kh65[:, lcol:lcol + 128])
                    xap = (qhPK if pi == 0 else ql65)[:, h, q0:q0 + w]
                    T((lambda pL, q0, w, sfx, wap, xap, st: lambda e: e.matmul(
                        pL[:, q0 - sfx:q0 - sfx + w], wap, xap, start=st,
                        stop=False, skip_group_check=True))(
                        pL, q0, w, sfx, wap, xap, pi == 0 and fresh),
                      waits=lw)
            pR = pC if idx % 2 == 0 else pD
            wB = 128 if typ == "own" else min(256, W)
            T((lambda pR, lcol, hb, sfx, wB: lambda e: e.matmul(
                pR[:, 0:wB], ksc3[:, lcol:lcol + 128],
                qs3b[:, hb, sfx:sfx + wB], start=True, stop=False,
                skip_group_check=True))(pR, lcol, hb, sfx, wB),
              waits=lw + qs3w_d[h] + (mask_v[g - 2] if g >= 2 else []))
            tw = P.now("T")
            mm = mDBm[:, 0:128] if typ == "own" else mOBm[:, 0:wB]
            V((lambda pR, wB, mm: lambda e: e.tensor_tensor(
                out=mtmp[:, 0:wB], in0=pR[:, 0:wB], in1=mm, op=ALU.mult))(
                pR, wB, mm), waits=tw)
            V((lambda pL, wB: lambda e: e.tensor_tensor(
                out=pL[:, 0:wB], in0=pL[:, 0:wB], in1=mtmp[:, 0:wB],
                op=ALU.add))(pL, wB))
            ma = mDadd if typ == "own" else mOadd
            V((lambda pL, ma: lambda e: e.tensor_tensor(
                out=pL[:, 0:128], in0=pL[:, 0:128], in1=ma[:, :], op=ALU.add))(
                pL, ma))
            mask_v[g] = P.now("V")
            eb = idx % 2
            A((lambda pL, eb, W: lambda e: e.activation(
                out=expf[:, eb, 0:W], in_=pL[:, 0:W], func=AF.Exp))(pL, eb, W),
              waits=mask_v[g] + (av_t[g - 2] if g >= 2 else []))
            exp_a[g] = [("A", P.cnt["A"])]
            V((lambda eb, W: lambda e: e.tensor_tensor(
                out=expl[:, eb, 0:W], in0=expf[:, eb, 0:W],
                in1=hiv(expf[:, eb, 0:W]), op=ALU.subtract))(eb, W),
              waits=exp_a[g])
            lo_v[g] = P.now("V")

        def emit_av(g):
            h, idx = g // 16, g % 16
            segs, sfx, kt, W = seg_d.pop(g)
            eb = idx % 2
            avw0 = (epi_free.get(h - 1, []) if idx == 0 else [])
            for pi in (0, 2, 1):
                for q0, w in segs:
                    ps = pE if q0 < 512 else pF
                    pcol = q0 - (0 if q0 < 512 else 512)
                    wap = (vtmh if pi < 2 else vtml)[:, kt, :]
                    xsrc = hiv(expf[:, eb, q0 - sfx:q0 - sfx + w]) if pi != 1 \
                        else expl[:, eb, q0 - sfx:q0 - sfx + w]
                    st = (idx == 0 and pi == 0)
                    T((lambda ps, pcol, w, wap, xsrc, st: lambda e: e.matmul(
                        ps[0:65, pcol:pcol + w], wap, xsrc, start=st, stop=False,
                        skip_group_check=True))(ps, pcol, w, wap, xsrc, st),
                      waits=(exp_a[g] if pi != 1 else lo_v[g]) + avw0)
            av_t[g] = [("T", P.cnt["T"])]

        def emit_epi_head(h):
            tw = P.now("T")
            A(lambda e: e.activation(out=cxu[64:65, 0:512], in_=pE[64:65, :],
                                     func=AF.Copy, bias=1e-38, scale=1.0),
              waits=tw)
            A(lambda e: e.activation(out=cxu[64:65, 512:1024], in_=pF[64:65, :],
                                     func=AF.Copy, bias=1e-38, scale=1.0))
            aw = P.now("A")
            V(lambda e: e.reciprocal(out=cxu[64:65, :], in_=cxu[64:65, :]),
              waits=aw)
            recipV[h] = P.now("V")

        def emit_epi(h):
            vw = recipV[h]
            T(lambda e: e.matmul(pE[64:128, :], onesP64[64:65, :],
                                 cxu[64:65, 0:512], start=True, stop=False,
                                 skip_group_check=True), waits=vw)
            T(lambda e: e.matmul(pF[64:128, :], onesP64[64:65, :],
                                 cxu[64:65, 512:1024], start=True, stop=False,
                                 skip_group_check=True))
            tw = P.now("T")
            V(lambda e: e.tensor_copy(out=rbc[:, 0:512], in_=pE[64:128, :]),
              waits=tw)
            V(lambda e: e.tensor_copy(out=rbc[:, 512:1024], in_=pF[64:128, :]))
            V((lambda h: lambda e: e.tensor_tensor(
                out=cxs[:, 0:512], in0=pE[0:64, :], in1=rbc[0:64, 0:512],
                op=ALU.mult))(h), waits=epi_v.get(h - 1, []))
            V(lambda e: e.tensor_tensor(
                out=cxs[:, 512:1024], in0=pF[0:64, :], in1=rbc[0:64, 512:1024],
                op=ALU.mult))
            epi_free[h] = P.now("V")
            V((lambda h: lambda e: e.tensor_copy(
                out=ctxh[:, h, :], in_=cxs[:, :]))(h))
            V((lambda h: lambda e: e.tensor_tensor(
                out=ctxl[:, h, :], in0=cxs[:, :], in1=ctxh[:, h, :],
                op=ALU.subtract))(h))
            epi_v[h] = P.now("V")
            G((lambda h: lambda e: e.dma_start(
                out=ctxPK1[64:128, h, :], in_=ctxPK1[0:64, h, :]))(h), waits=epi_v[h])
            if h % 2 == 1:
                G((lambda h: lambda e: e.dma_start(
                    out=ctxPK2[64:128, h - 1, :], in_=ctxPK2[0:64, h, :]))(h))

        for gg in range(16 * H + 2):
            if gg >= 2:
                emit_av(gg - 2)
                if (gg - 2) % 16 == 15:
                    emit_epi_head((gg - 2) // 16)
            if gg < 16 * H:
                emit_lr(gg)
            if gg >= 2 and (gg - 2) % 16 == 15:
                emit_epi((gg - 2) // 16)
        P.new_epoch()
        flush()
        tap("ctxf", ctxh[0:64, 0, :], 64, TS)
        flush()
        st_p3.close()
        st_kv.close()
        P.set_barrier()

        woc = alloc(st_p4, "woc", [128, 16, 128], F32)
        woc2 = alloc(st_p4, "woc2", [128, 16, 128], F32)
        woclo = alloc(st_p4, "woclo", [128, 16, 128], BF16)
        woclo2 = alloc(st_p4, "woclo2", [128, 16, 128], BF16)
        vupS = alloc(st_p4, "vupS", [128, H, L], F32)
        vupSlo = alloc(st_p4, "vupSlo", [128, H, L], BF16)
        wvoPK = alloc(st_p4, "wvoPK", [128, 2, H, 128], BF16)
        wvoPR = alloc(st_p4, "wvoPR", [128, 2, 8, 128], BF16)
        wvol = alloc(st_p4, "wvol", [64, 2, H, 128], BF16)
        xoc2 = alloc(st_p4, "xoc2", [128, 2, TS], F32)

        bar = barrier()
        S(lambda e: e.dma_start(out=vupS[:, :, :],
                                in_=vupTi.rearrange("h d l -> d h l")), waits=bar)
        vw0 = P.now("dS", "dS2")
        V(lambda e: e.tensor_tensor(out=vupSlo[:].rearrange("p h l -> p (h l)"),
                                    in0=vupS[:].rearrange("p h l -> p (h l)"),
                                    in1=hiv(vupS[:].rearrange("p h l -> p (h l)")),
                                    op=ALU.subtract), waits=vw0 + bar)

        wocB = [woc, woc2]
        wocloB = [woclo, woclo2]
        wocw = {}
        bT = {}
        aT = {}
        dwD = {}
        xvD = {}
        loV = {}
        evV = {}
        xow = {}

        def p4_dma(eo):
            ld = S((lambda eo: lambda e: e.dma_start(
                out=wocB[eo % 2][:, :, :],
                in_=woT[:, 128 * eo:128 * (eo + 1)].rearrange("(c p) m -> p c m",
                                                              p=128)))(eo),
                waits=bT.get(eo - 2, []) + (bar if eo < 2 else []))
            wocw[eo] = [ld]
            xow[eo] = [S((lambda eo: lambda e: e.dma_start(
                out=xoc2[:, eo % 2, :], in_=xoT[128 * eo:128 * (eo + 1), :]))(eo),
                waits=xvD.get(eo - 2, []))]

        def p4_lo(eo):
            b = eo % 2
            V((lambda b: lambda e: e.tensor_tensor(
                out=wocloB[b][:].rearrange("p c m -> p (c m)"),
                in0=wocB[b][:].rearrange("p c m -> p (c m)"),
                in1=hiv(wocB[b][:].rearrange("p c m -> p (c m)")),
                op=ALU.subtract))(b), waits=wocw[eo] + bT.get(eo - 2, []))
            loV[eo] = P.now("V")

        PS4 = (pC, pD, pE, pF)

        def p4_build(eo):
            b = eo % 2
            bw = loV[eo] + evV.get(eo - 1, [])
            for h in range(H):
                ps, col = PS4[h // 4], 128 * (h % 4)
                for pi in range(3):
                    wap = (hiv(vupS[:, h, :]) if pi < 2 else vupSlo[:, h, :])
                    xap = (hiv(wocB[b][:, h, :]) if pi != 1
                           else wocloB[b][:, h, :])
                    st = (pi == 0 and h % 4 == 0)
                    T((lambda ps, col, wap, xap, st: lambda e: e.matmul(
                        ps[0:64, col:col + 128], wap, xap, start=st, stop=False,
                        skip_group_check=True))(ps, col, wap, xap, st), waits=bw)
            bT[eo] = [("T", P.cnt["T"])]
            tw = P.now("T")
            for gi, ps2 in enumerate(PS4):
                h0 = 4 * gi
                V((lambda b, ps2, h0: lambda e: e.tensor_copy(
                    out=wvoPK[0:64, b, h0:h0 + 4, :],
                    in_=ps2[0:64, 0:512]))(b, ps2, h0),
                  waits=tw + dwD.get(eo - 2, []))
                V((lambda b, ps2, h0: lambda e: e.tensor_tensor(
                    out=wvol[:, b, h0:h0 + 4, :],
                    in0=ps2[0:64, 0:512].rearrange("p (a m) -> p a m", a=4),
                    in1=wvoPK[0:64, b, h0:h0 + 4, :],
                    op=ALU.subtract))(b, ps2, h0))
            evV[eo] = P.now("V")

        def p4_pack(eo):
            b = eo % 2
            vw = P.now("V")
            V((lambda b: lambda e: e.tensor_copy(
                out=wvoPR[0:64, b, :, :],
                in_=wvoPK[0:64, b, :, :][:, 0::2, :]))(b),
              waits=vw + aT.get(eo - 2, []))
            vw = P.now("V")
            dlo = G((lambda b: lambda e: e.dma_start(
                out=wvoPK[64:128, b, :, :],
                in_=wvol[:, b, :, :]))(b),
                waits=vw + aT.get(eo - 2, []))
            dhi = G((lambda b: lambda e: e.dma_start(
                out=wvoPR[64:128, b, :, :],
                in_=wvoPK[0:64, b, :, :][:, 1::2, :]))(b))
            dwD[eo] = [dlo, dhi]

        def p4_apply(eo):
            b = eo % 2
            ld = xow[eo][0]
            for qb in range(2):
                ps = pA if qb == 0 else pB
                for h in range(H):
                    st = (h == 0)
                    T((lambda b, ps, h, qb, st: lambda e: e.matmul(
                        ps[:, 0:512], wvoPK[:, b, h, :],
                        ctxPK1[:, h, 512 * qb:512 * (qb + 1)], start=st, stop=False,
                        skip_group_check=True))(b, ps, h, qb, st),
                      waits=dwD[eo] + (xvD.get(eo - 1, []) if st else []))
                for he in range(8):
                    T((lambda b, ps, he, qb: lambda e: e.matmul(
                        ps[:, 0:512], wvoPR[:, b, he, :],
                        ctxPK2[:, 2 * he, 512 * qb:512 * (qb + 1)], start=False,
                        stop=False, skip_group_check=True))(b, ps, he, qb),
                      waits=dwD[eo])
            aT[eo] = [("T", P.cnt["T"])]
            tw = P.now("T")
            V((lambda eo: lambda e: e.tensor_tensor(
                out=xp[:, eo, 0:512], in0=pA[:, 0:512], in1=xoc2[:, eo % 2, 0:512],
                op=ALU.add))(eo), waits=tw + [ld])
            V((lambda eo: lambda e: e.tensor_tensor(
                out=xp[:, eo, 512:1024], in0=pB[:, 0:512],
                in1=xoc2[:, eo % 2, 512:1024], op=ALU.add))(eo))
            xvD[eo] = P.now("V")

        p4_dma(0)
        p4_lo(0)
        p4_build(0)
        p4_pack(0)
        p4_dma(1)
        p4_lo(1)
        for eo in range(NCH):
            if eo + 1 < NCH:
                p4_build(eo + 1)
                p4_pack(eo + 1)
            p4_apply(eo)
            if eo + 2 < NCH:
                p4_dma(eo + 2)
                p4_lo(eo + 2)
        P.new_epoch()
        flush()
        tap("xp", xp[:, 0, :], 128, TS)
        flush()
        st_p4.close()
        P.set_barrier()

        x8 = alloc(st_p5, "x8", [128, NCH, TS], FP8)
        wst = alloc(st_p5, "wst", [128, 2, 8, 2, 128], F32)
        mrow = wst[0:1, 0, 0:4, :, :].rearrange("p k j m -> p (k j m)")
        rrow = wst[0:1, 0, 4:8, :, :].rearrange("p k j m -> p (k j m)")
        scr1 = srow[0:1, :]
        scr2 = alloc(st_p5, "scr2_5", [1, TS], F32)
        w0c = alloc(st_p5, "w0c", [128, 3, 8, 2, 128], FP8)
        w1c = alloc(st_p5, "w1c", [128, 3, 8, 2, 128], FP8)
        wlc = alloc(st_p5, "wlc", [128, 2, 16, 2, 128], FP8)
        gb = bcM[:].bitcast(BF16)[:, 0:1024]
        f1b = AR[:, 131072:133120].bitcast(BF16)

        bar = barrier()
        P.new_epoch()
        for c in range(NCH):
            cw = bar if c == 0 else P.now("T")
            A((lambda c: lambda e: e.activation(
                out=gb[:], in_=xp[:, c, :], func=AF.Square))(c),
              waits=cw)
            for qb in range(2):
                T((lambda c, qb, st: lambda e: e.matmul(
                    pA[0:1, 512 * qb:512 * (qb + 1)], ones128b[:],
                    hiv(xp[:, c, 512 * qb:512 * (qb + 1)]),
                    start=st, stop=False, skip_group_check=True))(
                    c, qb, P.first_touch(("A", qb))), waits=bar if c == 0 else [])
            aw = P.now("A")
            for qb in range(2):
                T((lambda c, qb, st: lambda e: e.matmul(
                    pA[32:33, 512 * qb:512 * (qb + 1)], ones128b[:],
                    gb[:, 512 * qb:512 * (qb + 1)], start=st, stop=False,
                    skip_group_check=True))(c, qb, P.first_touch(("A32", qb))),
                  waits=aw)
        tw = P.now("T")
        V(lambda e: e.tensor_scalar(out=mrow[:, :], in0=pA[0:1, :], scalar1=1.0 / E,
                                    scalar2=0.0, op0=ALU.mult, op1=ALU.add), waits=tw)
        V(lambda e: e.tensor_scalar(out=rrow[:, :], in0=pA[32:33, :], scalar1=1.0 / E,
                                    scalar2=1e-5, op0=ALU.mult, op1=ALU.add))
        V(lambda e: e.scalar_tensor_tensor(out=scr1[:, :], in0=mrow[:, :], scalar=-1.0,
                                           in1=mrow[:, :], op0=ALU.mult, op1=ALU.mult))
        V(lambda e: e.tensor_tensor(out=rrow[:, :], in0=rrow[:, :], in1=scr1[:, :],
                                    op=ALU.add))
        V(lambda e: e.reciprocal(out=scr1[:, :], in_=rrow[:, :]))
        vw = P.now("V")
        A(lambda e: e.activation(out=scr1[:, :], in_=scr1[:, :], func=AF.Sqrt),
          waits=vw)
        aw = P.now("A")
        V(lambda e: e.tensor_tensor(out=scr2[:, :], in0=scr1[:, :], in1=scr1[:, :],
                                    op=ALU.mult), waits=aw)
        V(lambda e: e.tensor_tensor(out=scr2[:, :], in0=scr2[:, :], in1=rrow[:, :],
                                    op=ALU.mult))
        V(lambda e: e.tensor_scalar(out=scr2[:, :], in0=scr2[:, :], scalar1=-0.5,
                                    scalar2=1.5, op0=ALU.mult, op1=ALU.add))
        V(lambda e: e.tensor_tensor(out=rrow[:, :], in0=scr1[:, :], in1=scr2[:, :],
                                    op=ALU.mult))
        P.new_epoch()
        vw = P.now("V")
        for srcrow, dst in ((mrow, bcM), (rrow, bcR)):
            for qb in range(2):
                T((lambda srcrow, qb: lambda e: e.matmul(
                    pB[:, 512 * qb:512 * (qb + 1)], onesrow[:, :],
                    srcrow[:, 512 * qb:512 * (qb + 1)], start=True, stop=False,
                    skip_group_check=True))(srcrow, qb), waits=vw)
            tw = P.now("T")
            V((lambda dst: lambda e: e.tensor_copy(out=dst[:], in_=pB[:]))(dst), waits=tw)
            vw = P.now("V")
        for c in range(NCH):
            V((lambda c: lambda e: e.tensor_tensor(
                out=pB[:, :], in0=xp[:, c, :], in1=bcM[:], op=ALU.subtract))(c))
            V((lambda c: lambda e: e.tensor_tensor(
                out=x8[:, c, :], in0=pB[:, :], in1=bcR[:], op=ALU.mult))(c))
        flush()
        tap("x8", x8[:, 0, :].bitcast(BF16)[:, 0:512].bitcast(F32)[:, 0:128], 128, 128)

        P.new_epoch()
        sw = P.now("T", "V")
        done = {-2: sw, -1: sw}
        gelu_done = {}
        mm_done = {}
        castA = {}
        ldD = {}

        def g_dma(hc):
            buf = hc % 2
            l0 = S((lambda hc, buf: lambda e: e.dma_start(
                out=wst[:, buf, :, :, :],
                in_=wg[0, :, 128 * hc:128 * (hc + 1)].rearrange(
                    "(k j p) m -> p k j m", j=2, p=128)))(hc, buf),
                waits=castA.get(hc - 2, []) +
                (P.now("T", "V", "A") if hc < 2 else []))
            l1 = G((lambda hc, b3: lambda e: e.dma_start(
                out=w1c[:, b3, :, :, :],
                in_=wg[1, :, 128 * hc:128 * (hc + 1)].rearrange(
                    "(k j p) m -> p k j m", j=2, p=128)))(hc, hc % 3),
                waits=mm_done.get(hc - 3, []))
            ldD[hc] = [l0, l1]
            A((lambda buf, b3: lambda e: e.activation(
                out=w0c[:, b3, :, :, :].rearrange("p k j m -> p (k j m)"),
                in_=wst[:, buf, :, :, :].rearrange("p k j m -> p (k j m)"),
                func=AF.Copy))(buf, hc % 3),
              waits=[l0] + mm_done.get(hc - 3, []))
            castA[hc] = [("A", P.cnt["A"])]

        g_dma(0)
        g_dma(1)
        for hc in range(64):
            buf = hc % 2
            lw = [ldD[hc][1]] + castA[hc] + \
                gelu_done.get(hc - 2, (P.now("V") if hc < 2 else []))
            if buf == 0:
                psg = [(pA, 0), (pA, 512)]
                psf = [(pB, 0), (pB, 512)]
            else:
                psg = [(pC, 0), (pD, 0)]
                psf = [(pE, 0), (pF, 0)]
            for qq in range(4):
                psG, colG = psg[qq // 2]
                psF, colF = psf[qq // 2]
                cg = colG + 256 * (qq % 2)
                cf = colF + 256 * (qq % 2)
                for kc in range(8):
                    T((lambda b3, qq, kc, psG, cg, st: lambda e: e.matmul(
                        psG[:, cg:cg + 256], w0c[:, b3, kc, :, :],
                        x8[:, 2 * kc:2 * kc + 2, 256 * qq:256 * (qq + 1)],
                        start=st, stop=False, skip_group_check=True,
                        perf_mode=PM.DoubleRow))(hc % 3, qq, kc, psG, cg,
                                                 (kc == 0 and qq % 2 == 0)), waits=lw)
                    T((lambda b3, qq, kc, psF, cf, st: lambda e: e.matmul(
                        psF[:, cf:cf + 256], w1c[:, b3, kc, :, :],
                        x8[:, 2 * kc:2 * kc + 2, 256 * qq:256 * (qq + 1)],
                        start=st, stop=False, skip_group_check=True,
                        perf_mode=PM.DoubleRow))(hc % 3, qq, kc, psF, cf,
                                                 (kc == 0 and qq % 2 == 0)), waits=lw)
            mm_done[hc] = P.now("T")
            if hc + 2 < 64:
                g_dma(hc + 2)
            tw = mm_done[hc]
            gsrc = [(psg[0][0], psg[0][1]), (psg[1][0], psg[1][1])]
            fsrc = [(psf[0][0], psf[0][1]), (psf[1][0], psf[1][1])]
            V((lambda gsrc: lambda e: e.tensor_copy(
                out=gb[:, 0:512], in_=gsrc[0][0][:, gsrc[0][1]:gsrc[0][1] + 512]))(gsrc),
              waits=tw)
            V((lambda gsrc: lambda e: e.tensor_copy(
                out=gb[:, 512:1024], in_=gsrc[1][0][:, gsrc[1][1]:gsrc[1][1] + 512]))(gsrc))
            V((lambda fsrc: lambda e: e.tensor_copy(
                out=f1b[:, 0:512], in_=fsrc[0][0][:, fsrc[0][1]:fsrc[0][1] + 512]))(fsrc))
            V((lambda fsrc: lambda e: e.tensor_copy(
                out=f1b[:, 512:1024], in_=fsrc[1][0][:, fsrc[1][1]:fsrc[1][1] + 512]))(fsrc))
            vw = P.now("V")
            A(lambda e: e.activation(out=gb[:], in_=gb[:], func=AF.Gelu_apprx_tanh),
              waits=vw)
            aw = P.now("A")
            V(lambda e: e.tensor_tensor(out=gb[:], in0=gb[:], in1=f1b[:],
                                        op=ALU.mult), waits=aw)
            V((lambda hc: lambda e: e.tensor_copy(out=act8[:, hc, :], in_=gb[:]))(hc))
            gelu_done[hc] = P.now("V")
            done[hc] = P.now("T", "V")
        flush()

        outsW = x8[:, 0:4, :].rearrange("p a b -> p (a b)").bitcast(F32)
        P.new_epoch()
        sw = P.now("T", "V")
        done = {-2: sw, -1: sw}
        outdma = {}
        castL = {}
        ldL = {}
        fcV = {}

        def wl_dma(eo):
            buf = eo % 2
            w = done.get(eo - 2, [])
            gate_g = mm_done.get(63, []) if eo < 2 else []
            ld1 = G((lambda eo, buf: lambda e: e.dma_start(
                out=wlc[:, buf, 0:16, :, :],
                in_=wlin[0:4096, 128 * eo:128 * (eo + 1)].rearrange(
                    "(k j p) m -> p k j m", j=2, p=128)))(eo, buf), waits=w)
            lrS = S((lambda eo: lambda e: e.dma_start(
                out=wst[:, 0, :, :, :],
                in_=wlin[4096:6144, 128 * eo:128 * (eo + 1)].rearrange(
                    "(k j p) m -> p k j m", j=2, p=128)))(eo),
                waits=castL.get(eo - 1, castA.get(62, [])))
            lrA = AD((lambda eo: lambda e: e.dma_start(
                out=wst[:, 1, :, :, :],
                in_=wlin[6144:8192, 128 * eo:128 * (eo + 1)].rearrange(
                    "(k j p) m -> p k j m", j=2, p=128)))(eo),
                waits=castL.get(eo - 1, castA.get(63, [])))
            ldL[eo] = [ld1]
            A((lambda buf: lambda e: e.activation(
                out=w0c[:, buf, :, :, :].rearrange("p k j m -> p (k j m)"),
                in_=wst[:, 0, :, :, :].rearrange("p k j m -> p (k j m)"),
                func=AF.Copy))(buf),
              waits=[lrS, lrA] + w + gate_g)
            A((lambda buf: lambda e: e.activation(
                out=w1c[:, buf, :, :, :].rearrange("p k j m -> p (k j m)"),
                in_=wst[:, 1, :, :, :].rearrange("p k j m -> p (k j m)"),
                func=AF.Copy))(buf))
            castL[eo] = [("A", P.cnt["A"])]

        wl_dma(0)
        for eo in range(NCH):
            buf = eo % 2
            if eo + 1 < NCH:
                wl_dma(eo + 1)
            lw = ldL[eo] + castL[eo] + fcV.get(eo - 2, [])
            ps = pC if eo % 2 == 0 else pD
            for qq in range(4):
                w_mm = lw + (P.now("V") if qq >= 2 else [])
                for kc in range(32):
                    st = P.first_touch((PSNAME[id(ps)], 0)) if (qq % 2 == 0 and kc == 0) \
                        else False
                    wsrc = (wlc[:, buf, kc, :, :] if kc < 16 else
                            (w0c if kc < 24 else w1c)[:, buf, kc % 8, :, :])
                    T((lambda ps, buf, qq, kc, st, wsrc: lambda e: e.matmul(
                        ps[:, 256 * (qq % 2):256 * (qq % 2) + 256],
                        wsrc,
                        act8[:, 2 * kc:2 * kc + 2, 256 * qq:256 * (qq + 1)],
                        start=st, stop=False, skip_group_check=True,
                        perf_mode=PM.DoubleRow))(ps, buf, qq, kc, st, wsrc),
                      waits=w_mm)
                if qq % 2 == 1:
                    tw = P.now("T")
                    hb = qq // 2
                    ow = [outdma[eo - 1]] if eo - 1 in outdma else []
                    V((lambda ps: lambda e: e.tensor_copy(
                        out=f1b[:, 0:512], in_=ps[:, 0:512]))(ps), waits=tw)
                    V((lambda eo, hb: lambda e: e.tensor_tensor(
                        out=outsW[:, 512 * hb:512 * (hb + 1)],
                        in0=xp[:, eo, 512 * hb:512 * (hb + 1)], in1=f1b[:, 0:512],
                        op=ALU.add))(eo, hb), waits=ow)
                    P.new_epoch()
            vw = P.now("V")
            fcV[eo] = vw
            done[eo] = P.now("T")
            outdma[eo] = S((lambda eo: lambda e: e.dma_start(
                out=outT[128 * eo:128 * (eo + 1), :], in_=outsW[:, :]))(eo),
                waits=vw)
        fin = P.now("dS", "dS2")
        P.emit("gpsimd", lambda e: e.memset(scr2[0:1, 0:1], 0.0), fin, None)
        flush()

        for st in (st_p5, st_per):
            st.close()
    es.close()
    return nc


_MASK_CACHE = {}


def _masks(pi):
    if pi in _MASK_CACHE:
        return _MASK_CACHE[pi]
    s = np.arange(128)[:, None]
    q = np.arange(128)[None, :]
    mD_add = np.where(s > q, np.float32(-1e30), np.float32(0.0))
    mDB = ((q - s >= 0) & (q - s <= 64)).astype(np.float32)
    sub = (s - q >= 64).astype(np.float32)
    if pi == 0:
        mO_add = np.full((128, 128), -1e30, np.float32)
        mOB = np.concatenate([np.zeros((128, 128), np.float32), sub], axis=1)
    else:
        mO_add = np.zeros((128, 128), np.float32)
        mOB = np.concatenate([sub, np.zeros((128, 128), np.float32)], axis=1)
    out = (mD_add, mO_add, mDB.astype(np.float32), mOB)
    _MASK_CACHE[pi] = out
    return out


_PROG = None


def kernel(L0_attn_hidden_states, L0_attn_Wq, L0_attn_Wk, L0_attn_Wv,
           L0_attn_q_to_latent, L0_attn_v_up, L0_attn_Wo, L0_ffn_w_gating,
           L0_ffn_w_linear):
    global _PROG
    x = np.ascontiguousarray(L0_attn_hidden_states, dtype=np.float32)
    wq = np.ascontiguousarray(L0_attn_Wq, dtype=np.float32)
    wqT = np.ascontiguousarray(wq.T)
    wk = np.ascontiguousarray(L0_attn_Wk, dtype=np.float32)
    wv = np.ascontiguousarray(L0_attn_Wv, dtype=np.float32)
    wkT = np.ascontiguousarray(wk.T)
    wvT = np.ascontiguousarray(wv.T)
    q2l = np.ascontiguousarray(L0_attn_q_to_latent, dtype=np.float32)
    vupT = np.ascontiguousarray(np.transpose(L0_attn_v_up, (0, 2, 1)))
    woT = np.ascontiguousarray(np.asarray(L0_attn_Wo, dtype=np.float32).T)
    wg = np.ascontiguousarray(L0_ffn_w_gating, dtype=np.float32)
    wlin = np.ascontiguousarray(L0_ffn_w_linear, dtype=np.float32)
    idn = np.eye(128, dtype=np.float32)

    if _PROG is None:
        _PROG = build_program()
    nc = _PROG

    in_maps = []
    for c in range(8):
        b, pi = c // 2, c % 2
        own = np.concatenate([np.arange(128 * (2 * i + pi), 128 * (2 * i + pi) + 128)
                              for i in range(8)])
        oth = np.concatenate([np.arange(128 * (2 * i + 1 - pi),
                                        128 * (2 * i + 1 - pi) + 128)
                              for i in range(8)])
        xoT = np.ascontiguousarray(x[b][own].T)
        xrT = np.ascontiguousarray(x[b][oth].T)
        mD_add, mO_add, mDB, mOB = _masks(pi)
        in_maps.append({
            "xoT": xoT, "xrT": xrT, "wqT": wqT, "q2l": q2l, "wkT": wkT, "wvT": wvT,
            "vupT": vupT, "woT": woT, "wg": wg, "wlin": wlin, "idn": idn,
            "mD_add": mD_add, "mO_add": mO_add, "mDB_mult": mDB, "mOB_mult": mOB,
        })
    res = run_bass_kernel_spmd(nc, in_maps, core_ids=list(range(8)))
    out = np.empty((4, 2048, 2048), dtype=np.float32)
    for c in range(8):
        b, pi = c // 2, c % 2
        own = np.concatenate([np.arange(128 * (2 * i + pi), 128 * (2 * i + pi) + 128)
                              for i in range(8)])
        out[b][own] = res.results[c]["outT"].T
    return out


# version 14
# speedup vs baseline: 1.1310x; 1.0024x over previous
import numpy as np
import contextlib
import concourse.bass as bass
import concourse.mybir as mybir
from concourse.bass_utils import run_bass_kernel_spmd

F32 = mybir.dt.float32
F32R = mybir.dt.float32r
BF16 = mybir.dt.bfloat16
FP8 = mybir.dt.float8e4
AF = mybir.ActivationFunctionType
ALU = mybir.AluOpType
PM = mybir.MatmulPerfMode
AX = mybir.AxisListType

E = 2048
TS = 1024
H, L = 16, 64
HID = 8192
NCH = 16
INV = float(1.0 / np.sqrt(L))
RAH = float(0.5 / np.sqrt(L))
PAD = 78.0

ENGS = ("sync", "gpsimd", "tensor", "vector", "scalar")


class Plan:
    def __init__(self):
        self.pend = {e: [] for e in ENGS}
        self.cnt = dict(dS=0, dS2=0, dG=0, dG2=0, dA=0, dA2=0, T=0, V=0, A=0, PL=0)
        self.rr = {"sync": 0, "gpsimd": 0, "scalar": 0}
        self.bank_started = {}
        self.barrier_waits = []
        self.barrier_pending = {e: False for e in ENGS}

    def now(self, *sems):
        return [(s, self.cnt[s]) for s in sems]

    def set_barrier(self):
        self.barrier_waits = [(s, self.cnt[s]) for s in self.cnt]
        self.barrier_pending = {e: True for e in ENGS}

    def emit(self, eng, fn, waits=(), inc=None, amt=None):
        if amt is None:
            amt = 16 if inc in ("dS", "dG", "dS2", "dG2", "dA", "dA2") else 1
        w = list(waits)
        if inc in ("dS", "dG", "dS2", "dG2", "dA", "dA2"):
            w = [(inc, self.cnt[inc])] + w
        if eng == "vector":
            w = [("V", self.cnt["V"])] + w
        elif eng == "scalar":
            w = [("A", self.cnt["A"])] + w
        elif eng == "gpsimd" and inc == "PL":
            w = [("PL", self.cnt["PL"])] + w
        if self.barrier_pending[eng]:
            w = self.barrier_waits + w
            self.barrier_pending[eng] = False
        self.pend[eng].append((fn, w, inc, amt))
        if inc is not None:
            self.cnt[inc] += amt
        return self.cnt.get(inc)

    def first_touch(self, bank_key):
        if bank_key in self.bank_started:
            return False
        self.bank_started[bank_key] = True
        return True

    def new_epoch(self):
        self.bank_started = {}


def build_program(debug=()):
    nc = bass.Bass("TRN2", target_bir_lowering=False, debug=False)
    dt = nc.dram_tensor
    xoT = dt("xoT", [E, TS], F32, kind="ExternalInput").ap()
    xrT = dt("xrT", [E, TS], F32, kind="ExternalInput").ap()
    wqT = dt("wqT", [E, E], F32, kind="ExternalInput").ap()
    q2li = dt("q2l", [H, 128, L], F32, kind="ExternalInput").ap()
    wkT = dt("wkT", [E, L], F32, kind="ExternalInput").ap()
    wvT = dt("wvT", [E, L], F32, kind="ExternalInput").ap()
    vupTi = dt("vupT", [H, 128, L], F32, kind="ExternalInput").ap()
    woT = dt("woT", [E, E], F32, kind="ExternalInput").ap()
    wg = dt("wg", [2, E, HID], F32, kind="ExternalInput").ap()
    wlin = dt("wlin", [HID, E], F32, kind="ExternalInput").ap()
    idn = dt("idn", [128, 128], F32, kind="ExternalInput").ap()
    mD_add = dt("mD_add", [128, 128], F32, kind="ExternalInput").ap()
    mO_add = dt("mO_add", [128, 128], F32, kind="ExternalInput").ap()
    mDB_mult = dt("mDB_mult", [128, 128], F32, kind="ExternalInput").ap()
    mOB_mult = dt("mOB_mult", [128, 256], F32, kind="ExternalInput").ap()
    outT = dt("outT", [E, TS], F32, kind="ExternalOutput").ap()
    dbg_out = {}
    for name, shape in debug:
        dbg_out[name] = dt("dbg_" + name, shape, F32, kind="ExternalOutput").ap()

    P = Plan()
    st_per = contextlib.ExitStack()
    st_kv = contextlib.ExitStack()
    st_p1 = contextlib.ExitStack()
    st_p2 = contextlib.ExitStack()
    st_p3 = contextlib.ExitStack()
    st_p4 = contextlib.ExitStack()
    st_p5 = contextlib.ExitStack()

    def hiv(ap_f32):
        return ap_f32.bitcast(BF16)[..., 1::2]

    es = contextlib.ExitStack()
    pA = es.enter_context(nc.psum_tensor("pA", [128, 1024], F32))
    pB = es.enter_context(nc.psum_tensor("pB", [128, 1024], F32))
    pC = es.enter_context(nc.psum_tensor("pC", [128, 512], F32))
    pD = es.enter_context(nc.psum_tensor("pD", [128, 512], F32))
    pE = es.enter_context(nc.psum_tensor("pE", [128, 512], F32))
    pF = es.enter_context(nc.psum_tensor("pF", [128, 512], F32))
    sem_dS = es.enter_context(nc.semaphore("dS"))
    sem_dS2 = es.enter_context(nc.semaphore("dS2"))
    sem_dG = es.enter_context(nc.semaphore("dG"))
    sem_dG2 = es.enter_context(nc.semaphore("dG2"))
    sem_T = es.enter_context(nc.semaphore("T"))
    sem_V = es.enter_context(nc.semaphore("V"))
    sem_A = es.enter_context(nc.semaphore("A"))
    sem_PL = es.enter_context(nc.semaphore("PL"))
    sem_dA = es.enter_context(nc.semaphore("dA"))
    sem_dA2 = es.enter_context(nc.semaphore("dA2"))
    with (
        nc.Block() as block,
    ):
        sems = {"dS": sem_dS, "dS2": sem_dS2, "dG": sem_dG, "dG2": sem_dG2,
                "T": sem_T, "V": sem_V, "A": sem_A, "PL": sem_PL,
                "dA": sem_dA, "dA2": sem_dA2}
        wms = {e: {} for e in ENGS}
        PSNAME = {id(pA): "A", id(pB): "B", id(pC): "C", id(pD): "D",
                  id(pE): "E", id(pF): "F"}

        def S(fn, waits=()):
            P.rr["sync"] ^= 1
            sem = "dS" if P.rr["sync"] else "dS2"
            return (sem, P.emit("sync", fn, waits, sem))

        def G(fn, waits=()):
            P.rr["gpsimd"] ^= 1
            sem = "dG" if P.rr["gpsimd"] else "dG2"
            return (sem, P.emit("gpsimd", fn, waits, sem))

        def AD(fn, waits=()):
            P.rr["scalar"] ^= 1
            sem = "dA" if P.rr["scalar"] else "dA2"
            return (sem, P.emit("scalar", fn, waits, sem))

        def T(fn, waits=()):
            return P.emit("tensor", fn, waits, "T")

        def GC(fn, waits=()):
            return P.emit("gpsimd", fn, waits, "PL")

        def V(fn, waits=()):
            return P.emit("vector", fn, waits, "V")

        def A(fn, waits=()):
            return P.emit("scalar", fn, waits, "A")

        def flush():
            for name in ENGS:
                ops = P.pend[name]
                if not ops:
                    continue
                P.pend[name] = []
                wm = wms[name]

                def body(eng, ops=ops, wm=wm):
                    for fn, waits, inc, amt in ops:
                        for s, v in waits:
                            if v > 0 and wm.get(s, 0) < v:
                                eng.wait_ge(sems[s], v)
                                wm[s] = v
                        ins = fn(eng)
                        if inc is not None:
                            ins.then_inc(sems[inc], amt)
                getattr(block, name)(body)

        def barrier():
            return P.now("dS", "dS2", "dG", "dG2", "dA", "dA2", "T", "V", "A", "PL")

        def tap(name, src_ap, rows, cols):
            if name in dbg_out:
                w = barrier()
                G((lambda name, src_ap, rows, cols: lambda e: e.dma_start(
                    out=dbg_out[name][0:rows, 0:cols], in_=src_ap))(name, src_ap, rows, cols),
                  waits=w)

        def alloc(st, name, shape, dtype):
            return st.enter_context(nc.sbuf_tensor(name, shape, dtype))

        ones128 = alloc(st_per, "ones128", [128, 1], F32)
        ones128b = alloc(st_per, "ones128b", [128, 1], BF16)
        onesrow = alloc(st_per, "onesrow", [1, 128], F32)
        m65 = alloc(st_per, "m65", [65, 1], BF16)
        mDadd = alloc(st_per, "mDadd", [128, 128], BF16)
        mOadd = alloc(st_per, "mOadd", [128, 128], BF16)
        mDBm = alloc(st_per, "mDBm", [128, 128], BF16)
        mOBm = alloc(st_per, "mOBm", [128, 256], BF16)
        bcM = alloc(st_per, "bcM", [128, TS], F32)
        bcR = alloc(st_per, "bcR", [128, TS], F32)
        srow = alloc(st_per, "srow", [16, TS], F32)
        d128 = alloc(st_per, "d128", [128, 128], F32)
        q128s = alloc(st_per, "q128s", [128, 128], F32)
        b128 = alloc(st_per, "b128", [128, 128], F32)
        s1s = alloc(st_per, "s1s", [128, 128], F32)
        AR = alloc(st_per, "AR", [128, 133120], mybir.dt.uint8)
        xho_hi = AR[:, 0:32768].bitcast(BF16).rearrange("p (c t) -> p c t", c=NCH)
        xho_lo = AR[:, 32768:65536].bitcast(BF16).rearrange("p (c t) -> p c t", c=NCH)
        ctxPK1 = AR[:, 0:32768].bitcast(BF16).rearrange("p (c t) -> p c t", c=H)
        ctxPK2 = AR[:, 32768:65536].bitcast(BF16).rearrange("p (c t) -> p c t", c=H)
        ctxh = ctxPK1[0:64]
        ctxl = ctxPK2[0:64]
        act8 = AR[:, 0:65536].bitcast(FP8).rearrange("p (c t) -> p c t", c=64)
        qhPK = AR[:, 65536:98304].bitcast(BF16).rearrange("p (c t) -> p c t", c=H)
        qh65 = qhPK[0:65]
        ql65 = AR[:, 98304:131072].bitcast(BF16).rearrange("p (c t) -> p c t", c=H)[0:65]
        qsum3p = AR[:, 131072:133120].bitcast(BF16)[0:48]
        xp = AR[:, 65536:131072].bitcast(F32).rearrange("p (c t) -> p c t", c=NCH)
        kh65 = alloc(st_kv, "kh65", [65, 2 * TS], BF16)
        khl = alloc(st_kv, "khl", [128, 2 * TS], BF16)
        onesP64 = alloc(st_kv, "onesP64", [65, 64], F32)
        vtmh = alloc(st_kv, "vtmh", [128, 16, 65], BF16)
        vtml = alloc(st_kv, "vtml", [128, 16, 65], BF16)
        konT = alloc(st_kv, "konT", [65, TS], F32)
        ksc3 = alloc(st_kv, "ksc3", [3, 2 * TS], BF16)
        kstat = alloc(st_kv, "kstat", [128, 32], BF16)
        wmx16 = alloc(st_kv, "wmx16", [128, 128], F32)
        wmn16 = alloc(st_kv, "wmn16", [128, 128], F32)
        wrow = alloc(st_kv, "wrow", [1, 48], F32)
        idns = alloc(st_kv, "idns", [128, 128], F32)
        segm = alloc(st_kv, "segm", [1, 64], F32)

        G(lambda e: e.dma_start(out=mDadd[:], in_=mD_add[:, :]))
        G(lambda e: e.dma_start(out=mOadd[:], in_=mO_add[:, :]))
        G(lambda e: e.dma_start(out=mDBm[:], in_=mDB_mult[:, :]))
        G(lambda e: e.dma_start(out=mOBm[:], in_=mOB_mult[:, :]))
        c0 = P.now("dS", "dS2", "dG", "dG2")
        V(lambda e: e.memset(ones128[:], 1.0), waits=c0)
        V(lambda e: e.memset(ones128b[:], 1.0))
        V(lambda e: e.memset(onesrow[:], 1.0))
        V(lambda e: e.memset(m65[0:64, :], 1.0))
        V(lambda e: e.memset(m65[64:65, :], 0.5))
        V(lambda e: e.memset(onesP64[64:65, :], 1.0))
        P.set_barrier()

        kvT = AR[:, 65536:73728].bitcast(F32)
        kvTlo = AR[:, 73728:77824].bitcast(BF16)
        kscrow = AR[:, 77824:86016].bitcast(F32)[0:1]
        xc = AR[:, 86016:94208].bitcast(F32).rearrange("p (a t) -> p a t", a=2)
        kminb = AR[:, 94208:94272].bitcast(BF16)[0:64]
        tr1 = AR[:, 94272:98368].bitcast(BF16)[0:1]
        mrow = alloc(st_p1, "mrow", [1, TS], F32)
        rrow = alloc(st_p1, "rrow", [1, TS], F32)
        scr1 = alloc(st_p1, "scr1", [1, TS], F32)
        scr2 = alloc(st_p1, "scr2", [1, TS], F32)
        sq = alloc(st_p1, "sq", [128, TS], F32)
        uu = sq
        xrh = alloc(st_p1, "xrh", [128, 2, TS], BF16)
        xrl = alloc(st_p1, "xrl", [128, 2, TS], BF16)
        kvW = AR[:, 98368:106560].bitcast(F32).rearrange("p (c l) -> p c l", c=NCH)
        kvWlo = AR[:, 106560:110656].bitcast(BF16).rearrange("p (c l) -> p c l", c=NCH)
        kW65 = AR[:, 110656:114816].bitcast(F32).rearrange("p (c l) -> p c l", c=NCH)
        kW65lo = AR[:, 114816:116896].bitcast(BF16).rearrange("p (c l) -> p c l", c=NCH)

        S(lambda e: e.dma_start(out=idns[:], in_=idn[:, :]))
        S(lambda e: e.dma_start(out=kvW[:, :, 0:64],
                                in_=wkT.rearrange("(c p) l -> p c l", p=128)))
        S(lambda e: e.dma_start(out=kvW[:, :, 64:128],
                                in_=wvT.rearrange("(c p) l -> p c l", p=128)))
        S(lambda e: e.dma_start(out=kW65[:, :, 0:64],
                                in_=wkT.rearrange("(c p) l -> p c l", p=128)))
        wl_ = P.now("dS", "dS2")
        V(lambda e: e.tensor_tensor(out=kvWlo[:].rearrange("p c l -> p (c l)"),
                                    in0=kvW[:].rearrange("p c l -> p (c l)"),
                                    in1=hiv(kvW[:].rearrange("p c l -> p (c l)")),
                                    op=ALU.subtract), waits=wl_)
        for c in range(NCH):
            V((lambda c: lambda e: e.tensor_reduce(
                out=kW65[:, c, 64:65], in_=kW65[:, c, 0:64], axis=AX.X, op=ALU.add))(c),
              waits=wl_)
        V(lambda e: e.tensor_tensor(out=kW65lo[:].rearrange("p c l -> p (c l)"),
                                    in0=kW65[:].rearrange("p c l -> p (c l)"),
                                    in1=hiv(kW65[:].rearrange("p c l -> p (c l)")),
                                    op=ALU.subtract))

        def ln_rows(src_dram):
            P.new_epoch()
            sw = P.now("T", "V", "A")
            done = {-2: sw, -1: sw}
            for c in range(NCH):
                buf = c % 2
                ld = S((lambda c, buf: lambda e: e.dma_start(
                    out=xc[:, buf, :], in_=src_dram[128 * c:128 * (c + 1), :]))(c, buf),
                    waits=done.get(c - 2, []))
                lw = [ld]
                A((lambda buf: lambda e: e.activation(
                    out=sq[:].bitcast(BF16)[:, 0:TS], in_=xc[:, buf, :],
                    func=AF.Square))(buf),
                  waits=lw + (P.now("T") if c > 0 else []))
                for qb in range(2):
                    T((lambda c, buf, qb, st: lambda e: e.matmul(
                        pA[0:1, 512 * qb:512 * (qb + 1)], ones128b[:],
                        hiv(xc[:, buf, 512 * qb:512 * (qb + 1)]),
                        start=st, stop=False, skip_group_check=True))(
                        c, buf, qb, P.first_touch(("A", qb))), waits=lw)
                aw = P.now("A")
                for qb in range(2):
                    T((lambda qb, st: lambda e: e.matmul(
                        pA[32:33, 512 * qb:512 * (qb + 1)], ones128b[:],
                        sq[:].bitcast(BF16)[:, 512 * qb:512 * (qb + 1)],
                        start=st, stop=False, skip_group_check=True))(
                        qb, P.first_touch(("A32", qb))), waits=aw)
                done[c] = P.now("T", "A")
            tw = P.now("T")
            V(lambda e: e.tensor_scalar(out=mrow[:, :], in0=pA[0:1, :], scalar1=1.0 / E,
                                        scalar2=0.0, op0=ALU.mult, op1=ALU.add), waits=tw)
            V(lambda e: e.tensor_scalar(out=rrow[:, :], in0=pA[32:33, :],
                                        scalar1=1.0 / E, scalar2=1e-5,
                                        op0=ALU.mult, op1=ALU.add))
            V(lambda e: e.scalar_tensor_tensor(out=scr1[:, :], in0=mrow[:, :],
                                               scalar=-1.0, in1=mrow[:, :],
                                               op0=ALU.mult, op1=ALU.mult))
            V(lambda e: e.tensor_tensor(out=rrow[:, :], in0=rrow[:, :], in1=scr1[:, :],
                                        op=ALU.add))
            V(lambda e: e.reciprocal(out=scr1[:, :], in_=rrow[:, :]))
            vw = P.now("V")
            A(lambda e: e.activation(out=scr1[:, :], in_=scr1[:, :], func=AF.Sqrt),
              waits=vw)
            aw = P.now("A")
            V(lambda e: e.tensor_tensor(out=scr2[:, :], in0=scr1[:, :], in1=scr1[:, :],
                                        op=ALU.mult), waits=aw)
            V(lambda e: e.tensor_tensor(out=scr2[:, :], in0=scr2[:, :], in1=rrow[:, :],
                                        op=ALU.mult))
            V(lambda e: e.tensor_scalar(out=scr2[:, :], in0=scr2[:, :], scalar1=-0.5,
                                        scalar2=1.5, op0=ALU.mult, op1=ALU.add))
            V(lambda e: e.tensor_tensor(out=rrow[:, :], in0=scr1[:, :], in1=scr2[:, :],
                                        op=ALU.mult))
            P.new_epoch()
            vw = P.now("V")
            for srcrow, dst in ((mrow, bcM), (rrow, bcR)):
                for qb in range(2):
                    T((lambda srcrow, qb, st: lambda e: e.matmul(
                        pB[:, 512 * qb:512 * (qb + 1)], onesrow[:, :],
                        srcrow[:, 512 * qb:512 * (qb + 1)], start=st, stop=False,
                        skip_group_check=True))(srcrow, qb, True), waits=vw)
                tw = P.now("T")
                V((lambda dst: lambda e: e.tensor_copy(out=dst[:], in_=pB[:]))(dst),
                  waits=tw)
                vw = P.now("V")

        def ln_apply(src_dram, hi_dst, lo_dst, mm_fn):
            sw = P.now("T", "V", "A")
            done = {-2: sw, -1: sw}
            for c in range(NCH):
                buf = c % 2
                ld = S((lambda c, buf: lambda e: e.dma_start(
                    out=xc[:, buf, :], in_=src_dram[128 * c:128 * (c + 1), :]))(c, buf),
                    waits=done.get(c - 2, []))
                V((lambda buf: lambda e: e.tensor_tensor(
                    out=uu[:], in0=xc[:, buf, :], in1=bcM[:], op=ALU.subtract))(buf),
                  waits=[ld])
                V(lambda e: e.tensor_tensor(out=uu[:], in0=uu[:], in1=bcR[:],
                                            op=ALU.mult))
                hd, ldst = hi_dst(c), lo_dst(c)
                V((lambda hd: lambda e: e.tensor_copy(out=hd, in_=uu[:]))(hd))
                V((lambda hd, ldst: lambda e: e.tensor_tensor(
                    out=ldst, in0=uu[:], in1=hd, op=ALU.subtract))(hd, ldst))
                mm_fn(c)
                done[c] = P.now("T", "V")

        ln_rows(xoT)
        P.new_epoch()

        def latent_own_mm(c):
            vw = P.now("V")
            for j in range(8):
                ps, col = (pC, 128 * j) if j < 4 else (pD, 128 * (j - 4))
                for pi, (wap, xap) in enumerate((
                        (hiv(kvW[:, c, :]), xho_hi[:, c, 128 * j:128 * (j + 1)]),
                        (hiv(kvW[:, c, :]), xho_lo[:, c, 128 * j:128 * (j + 1)]),
                        (kvWlo[:, c, :], xho_hi[:, c, 128 * j:128 * (j + 1)]))):
                    st = P.first_touch((PSNAME[id(ps)], col // 512))
                    T((lambda ps, col, wap, xap, st: lambda e: e.matmul(
                        ps[:, col:col + 128], wap, xap, start=st, stop=False,
                        skip_group_check=True))(ps, col, wap, xap, st), waits=vw)
            for qb in range(2):
                ps = pE if qb == 0 else pF
                for pi, (wap, xap) in enumerate((
                        (hiv(kW65[:, c, :]), xho_hi[:, c, 512 * qb:512 * (qb + 1)]),
                        (hiv(kW65[:, c, :]), xho_lo[:, c, 512 * qb:512 * (qb + 1)]),
                        (kW65lo[:, c, :], xho_hi[:, c, 512 * qb:512 * (qb + 1)]))):
                    st = P.first_touch((PSNAME[id(ps)], 0))
                    T((lambda ps, wap, xap, st: lambda e: e.matmul(
                        ps[0:65, :], wap, xap, start=st, stop=False,
                        skip_group_check=True))(ps, wap, xap, st), waits=vw)

        ln_apply(xoT, lambda c: xho_hi[:, c, :], lambda c: xho_lo[:, c, :], latent_own_mm)
        tw = P.now("T")
        for j in range(8):
            ps, col = (pC, 128 * j) if j < 4 else (pD, 128 * (j - 4))
            V((lambda j, ps, col: lambda e: e.tensor_scalar(
                out=kvT[0:64, 128 * j:128 * (j + 1)], in0=ps[0:64, col:col + 128],
                scalar1=INV, scalar2=0.0, op0=ALU.mult, op1=ALU.add))(j, ps, col),
              waits=tw)
            V((lambda j, ps, col: lambda e: e.tensor_copy(
                out=kvT[64:128, 128 * j:128 * (j + 1)],
                in_=ps[64:128, col:col + 128]))(j, ps, col))
        V(lambda e: e.tensor_scalar(out=konT[:, 0:512], in0=pE[0:65, :], scalar1=INV,
                                    scalar2=0.0, op0=ALU.mult, op1=ALU.add), waits=tw)
        V(lambda e: e.tensor_scalar(out=konT[:, 512:1024], in0=pF[0:65, :], scalar1=INV,
                                    scalar2=0.0, op0=ALU.mult, op1=ALU.add))

        ln_rows(xrT)
        P.new_epoch()

        def latent_oth_mm(c):
            vw = P.now("V")
            buf = c % 2
            for j in range(8):
                ps, col = (pC, 128 * j) if j < 4 else (pD, 128 * (j - 4))
                for pi, (wap, xap) in enumerate((
                        (hiv(kvW[:, c, :]), xrh[:, buf, 128 * j:128 * (j + 1)]),
                        (hiv(kvW[:, c, :]), xrl[:, buf, 128 * j:128 * (j + 1)]),
                        (kvWlo[:, c, :], xrh[:, buf, 128 * j:128 * (j + 1)]))):
                    st = P.first_touch((PSNAME[id(ps)], col // 512))
                    T((lambda ps, col, wap, xap, st: lambda e: e.matmul(
                        ps[:, col:col + 128], wap, xap, start=st, stop=False,
                        skip_group_check=True))(ps, col, wap, xap, st), waits=vw)

        ln_apply(xrT, lambda c: xrh[:, c % 2, :], lambda c: xrl[:, c % 2, :], latent_oth_mm)
        tw = P.now("T")
        for j in range(8):
            ps, col = (pC, 128 * j) if j < 4 else (pD, 128 * (j - 4))
            V((lambda j, ps, col: lambda e: e.tensor_scalar(
                out=kvT[0:64, 1024 + 128 * j:1024 + 128 * (j + 1)],
                in0=ps[0:64, col:col + 128], scalar1=INV, scalar2=0.0,
                op0=ALU.mult, op1=ALU.add))(j, ps, col), waits=tw)
            V((lambda j, ps, col: lambda e: e.tensor_copy(
                out=kvT[64:128, 1024 + 128 * j:1024 + 128 * (j + 1)],
                in_=ps[64:128, col:col + 128]))(j, ps, col))

        V(lambda e: e.tensor_tensor(out=kvTlo[:], in0=kvT[:], in1=hiv(kvT[:]),
                                    op=ALU.subtract))
        V(lambda e: e.tensor_copy(out=kh65[0:64, :], in_=hiv(kvT[0:64, :])))
        V(lambda e: e.memset(kh65[64:65, :], 1.0))
        V(lambda e: e.tensor_copy(out=khl[0:64, :], in_=hiv(kvT[0:64, :])))
        vwk = P.now("V")
        G(lambda e: e.dma_start(out=khl[64:128, :], in_=kvTlo[0:64, :]), waits=vwk)
        P.new_epoch()
        vw = P.now("V")
        for qq in range(4):
            ps = pA if qq < 2 else pB
            col = 512 * (qq % 2)
            for half in range(2):
                src = (hiv(kvT[0:64, 512 * qq:512 * (qq + 1)]) if half == 0
                       else kvTlo[0:64, 512 * qq:512 * (qq + 1)])
                st = P.first_touch((PSNAME[id(ps)], col // 512))
                T((lambda ps, col, src, st: lambda e: e.matmul(
                    ps[0:1, col:col + 512], ones128b[0:64, :], src, start=st,
                    stop=False, skip_group_check=True))(ps, col, src, st), waits=vw)
        tw = P.now("T")
        V(lambda e: e.tensor_scalar(out=kscrow[0:1, 0:1024], in0=pA[0:1, :], scalar1=0.5,
                                    scalar2=0.0, op0=ALU.mult, op1=ALU.add), waits=tw)
        V(lambda e: e.tensor_scalar(out=kscrow[0:1, 1024:2048], in0=pB[0:1, :],
                                    scalar1=0.5, scalar2=0.0, op0=ALU.mult, op1=ALU.add))
        V(lambda e: e.tensor_copy(out=ksc3[0:1, :], in_=hiv(kscrow[:])))
        V(lambda e: e.tensor_tensor(out=tr1[0:1, :], in0=kscrow[:], in1=hiv(kscrow[:]),
                                    op=ALU.subtract))
        vw = P.now("V")
        G(lambda e: e.dma_start(out=ksc3[1:2, :], in_=tr1[0:1, :]), waits=vw)
        G(lambda e: e.dma_start(out=ksc3[2:3, :], in_=ksc3[0:1, :]), waits=vw)
        V(lambda e: e.tensor_reduce(out=segm[0:1, 0:32],
                                    in_=kscrow[:].rearrange("o (s w) -> o s w", w=64),
                                    axis=AX.X, op=ALU.max))
        V(lambda e: e.tensor_reduce(out=segm[0:1, 32:64],
                                    in_=kscrow[:].rearrange("o (s w) -> o s w", w=64),
                                    axis=AX.X, op=ALU.min))
        for r, op in ((0, ALU.max), (1, ALU.min)):
            o = 24 * r
            b = 32 * r
            V((lambda op, o, b: lambda e: e.tensor_tensor(
                out=wrow[0:1, o:o + 8], in0=segm[0:1, b + 0:b + 16:2],
                in1=segm[0:1, b + 1:b + 16:2], op=op))(op, o, b))
            V((lambda op, o, b: lambda e: e.tensor_tensor(
                out=wrow[0:1, o + 8:o + 16], in0=segm[0:1, b + 16:b + 32:2],
                in1=segm[0:1, b + 17:b + 32:2], op=op))(op, o, b))
            V((lambda op, o, b: lambda e: e.tensor_tensor(
                out=wrow[0:1, o + 16:o + 24], in0=segm[0:1, b + 14:b + 30:2],
                in1=segm[0:1, b + 15:b + 30:2], op=op))(op, o, b))
            V((lambda op, o: lambda e: e.tensor_tensor(
                out=wrow[0:1, o:o + 8], in0=wrow[0:1, o:o + 8],
                in1=wrow[0:1, o + 8:o + 16], op=op))(op, o))
            V((lambda op, o: lambda e: e.tensor_tensor(
                out=wrow[0:1, o:o + 8], in0=wrow[0:1, o:o + 8],
                in1=wrow[0:1, o + 16:o + 24], op=op))(op, o))
        P.new_epoch()
        vw = P.now("V")
        T((lambda st: lambda e: e.matmul(pC[:, 0:8], onesrow[:, :], wrow[0:1, 0:8],
                                         start=st, stop=False, skip_group_check=True))(
            P.first_touch(("C", 0))), waits=vw)
        T(lambda e: e.matmul(pC[:, 8:16], onesrow[:, :], wrow[0:1, 24:32], start=False,
                             stop=False, skip_group_check=True))
        tw = P.now("T")
        for h in range(H):
            V((lambda h: lambda e: e.tensor_copy(out=wmx16[:, 8 * h:8 * h + 8],
                                                 in_=pC[:, 0:8]))(h), waits=tw)
            V((lambda h: lambda e: e.tensor_copy(out=wmn16[:, 8 * h:8 * h + 8],
                                                 in_=pC[:, 8:16]))(h))
        for b in range(32):
            V((lambda b: lambda e: e.tensor_reduce(
                out=kstat[0:64, b:b + 1], in_=kvT[0:64, 64 * b:64 * (b + 1)],
                axis=AX.X, op=ALU.max))(b))
            V((lambda b: lambda e: e.tensor_reduce(
                out=kminb[:, b:b + 1], in_=kvT[0:64, 64 * b:64 * (b + 1)],
                axis=AX.X, op=ALU.min))(b))
        vw = P.now("V")
        G(lambda e: e.dma_start(out=kstat[64:128, :], in_=kminb[:, :]), waits=vw)
        gwk = P.now("dG", "dG2")
        V(lambda e: e.tensor_scalar(out=kstat[64:128, :], in0=kstat[64:128, :],
                                    scalar1=-1.0, scalar2=0.0, op0=ALU.mult,
                                    op1=ALU.add), waits=gwk)
        for kt in range(16):
            vw2 = P.now("V")
            T((lambda kt: lambda e: e.transpose(
                pD[:, 0:64], kvT[64:128, 128 * kt:128 * (kt + 1)],
                idns[64:128, 64:128]))(kt), waits=vw2)
            tw = P.now("T")
            V((lambda kt: lambda e: e.tensor_copy(
                out=vtmh[:, kt, 0:64], in_=pD[:, 0:64]))(kt), waits=tw)
            V((lambda kt: lambda e: e.tensor_tensor(
                out=vtml[:, kt, 0:64], in0=pD[:, 0:64], in1=vtmh[:, kt, 0:64],
                op=ALU.subtract))(kt))
        V(lambda e: e.memset(vtmh[:, :, 64:65].rearrange("p c l -> p (c l)"), 1.0))
        V(lambda e: e.memset(vtml[:, :, 64:65].rearrange("p c l -> p (c l)"), 0.0))
        tap("xho_hi", xho_hi[:, 0, :], 128, TS)
        tap("kvT", kvT[:, :], 128, 2 * TS)
        tap("konT", konT[0:65, :], 65, TS)
        tap("kscrow", kscrow[0:1, :], 1, 2 * TS)
        flush()
        st_p1.close()
        P.set_barrier()

        qlatH = alloc(st_p2, "qlatH", [65, 2, TS], F32)
        wqc = alloc(st_p2, "wqc", [128, 2, 512], F32)
        wqlo = alloc(st_p2, "wqlo", [128, 2, 512], BF16)
        q2l65 = alloc(st_p2, "q2l65", [128, H, 65], F32)
        q2l65lo = alloc(st_p2, "q2l65lo", [128, H, 65], BF16)
        qev32 = alloc(st_p2, "qev32", [128, 2, 512], F32)
        qevl = alloc(st_p2, "qevl", [128, 2, 512], BF16)
        qpn = alloc(st_p2, "qpn", [128, 2, TS], BF16)
        qntmp = alloc(st_p2, "qntmp", [64, TS], BF16)
        prodx = alloc(st_p2, "prodx", [65, 2, TS], BF16)
        qloS = alloc(st_p2, "qloS", [65, TS], BF16)

        bar = barrier()
        S(lambda e: e.dma_start(out=q2l65[:, :, 0:64],
                                in_=q2li.rearrange("h d l -> d h l")), waits=bar)
        q2w = P.now("dS", "dS2")
        for h in range(H):
            V((lambda h: lambda e: e.tensor_reduce(
                out=q2l65[:, h, 64:65], in_=q2l65[:, h, 0:64], axis=AX.X, op=ALU.add))(h),
              waits=q2w + bar)
        V(lambda e: e.tensor_tensor(out=q2l65lo[:].rearrange("p h l -> p (h l)"),
                                    in0=q2l65[:].rearrange("p h l -> p (h l)"),
                                    in1=hiv(q2l65[:].rearrange("p h l -> p (h l)")),
                                    op=ALU.subtract))
        V(lambda e: e.memset(qh65[:, :, :][64:65].rearrange("o h t -> o (h t)"), 0.0))

        PSQ = [(pA, 0), (pA, 512), (pB, 0), (pB, 512), (pC, 0), (pD, 0), (pE, 0),
               (pF, 0)]
        doneQ = {}
        qTg = {}
        eqT = {}
        mach_t = {}
        rowvw = {}
        rowaw = {}
        gw2D = {}
        qdma_w = {}

        def emit_eq(grp, hi_):
            h = 4 * grp + hi_
            hb = h % 2
            for qb in range(2):
                ps, col = PSQ[hi_ * 2 + qb]
                A((lambda ps, col, qb: lambda e: e.activation(
                    out=qev32[:, qb, :], in_=ps[:, col:col + 512], func=AF.Copy))(
                    ps, col, qb), waits=qTg[grp] + eqT.get((h - 1, qb), []))
                aw = P.now("A")
                V((lambda qb: lambda e: e.tensor_tensor(
                    out=qevl[:, qb, :], in0=qev32[:, qb, :],
                    in1=hiv(qev32[:, qb, :]),
                    op=ALU.subtract))(qb), waits=aw)
                vw = P.now("V")
                for pi in (0, 2, 1):
                    wap = (hiv(q2l65[:, h, :]) if pi < 2 else q2l65lo[:, h, :])
                    xap = hiv(qev32[:, qb, :]) if pi != 1 else qevl[:, qb, :]
                    T((lambda ps, col, wap, xap, st: lambda e: e.matmul(
                        ps[0:65, col:col + 512], wap, xap, start=st, stop=False,
                        skip_group_check=True))(ps, col, wap, xap, pi == 0),
                      waits=(aw if pi != 1 else vw))
                eqT[(h, qb)] = [("T", P.cnt["T"])]
                tw = P.now("T")
                V((lambda ps, col, qb, hb: lambda e: e.tensor_copy(
                    out=qlatH[:, hb, 512 * qb:512 * (qb + 1)],
                    in_=ps[0:65, col:col + 512]))(ps, col, qb, hb), waits=tw)
            mw = mach_t.get(h - 2, [])
            qdw = qdma_w.get(h - 1, [])
            vw_ql = P.now("V")
            A((lambda h, hb: lambda e: e.activation(
                out=qh65[:, h, :][0:64], in_=hiv(qlatH[0:64, hb, :]),
                func=AF.Copy))(h, hb), waits=vw_ql)
            A((lambda hb: lambda e: e.activation(
                out=qpn[0:64, hb, :], in_=qlatH[0:64, hb, :],
                func=AF.Relu))(hb), waits=mw)
            A((lambda hb: lambda e: e.activation(
                out=qntmp[:, :], in_=qlatH[0:64, hb, :], func=AF.Relu,
                scale=-1.0))(hb), waits=qdw)
            A((lambda hb: lambda e: e.activation(
                out=qloS[64:65, :], in_=hiv(qlatH[64:65, hb, :]),
                func=AF.Copy))(hb), waits=qdw)
            rowaw[h] = P.now("A")
            G((lambda h: lambda e: e.dma_start(
                out=qhPK[64:128, h, :], in_=qhPK[0:64, h, :]))(h), waits=rowaw[h])
            V((lambda h, hb: lambda e: e.tensor_tensor(
                out=ql65[:, h, :][0:64], in0=qlatH[0:64, hb, :],
                in1=qh65[:, h, :][0:64], op=ALU.subtract))(h, hb), waits=rowaw[h])
            V((lambda hb: lambda e: e.tensor_tensor(
                out=prodx[0:64, hb, :], in0=qlatH[0:64, hb, :], in1=konT[0:64, :],
                op=ALU.mult))(hb), waits=mw)
            V((lambda hb: lambda e: e.tensor_tensor(
                out=prodx[64:65, hb, :], in0=qlatH[64:65, hb, :], in1=konT[64:65, :],
                op=ALU.mult))(hb))
            vw = P.now("V")
            rowvw[h] = vw
            G((lambda hb: lambda e: e.dma_start(
                out=qpn[64:128, hb, :], in_=qntmp[:, :]))(hb), waits=rowaw[h])
            G((lambda h: lambda e: e.dma_start(
                out=qsum3p[3 * h:3 * h + 1, :], in_=qloS[64:65, :]))(h))
            G((lambda h: lambda e: e.dma_start(
                out=qsum3p[3 * h + 1:3 * h + 2, :], in_=qloS[64:65, :]))(h))
            ghw = P.now("dG", "dG2")
            V((lambda hb: lambda e: e.tensor_tensor(
                out=qloS[64:65, :], in0=qlatH[64:65, hb, :],
                in1=hiv(qlatH[64:65, hb, :]), op=ALU.subtract))(hb), waits=ghw)
            vw2 = P.now("V")
            G((lambda h: lambda e: e.dma_start(
                out=qsum3p[3 * h + 2:3 * h + 3, :], in_=qloS[64:65, :]))(h),
              waits=vw2)
            gw2D[h] = P.now("dG", "dG2")
            qdma_w[h] = gw2D[h]

        def emit_mach(grp, hi_):
            h = 4 * grp + hi_
            hb = h % 2
            ps0, col0 = PSQ[hi_ * 2]
            vw = rowvw[h] + rowaw[h]
            gw2 = gw2D[h]
            for qt in range(8):
                T((lambda ps0, col0, qt, hb, st: lambda e: e.matmul(
                    ps0[:, col0 + 256 + qt:col0 + 257 + qt],
                    prodx[:, hb, 128 * qt:128 * (qt + 1)], m65[:, :],
                    start=st, stop=False, skip_group_check=True))(
                    ps0, col0, qt, hb, qt == 0), waits=vw)
                T((lambda ps0, col0, qt, hb: lambda e: e.matmul(
                    ps0[:, col0 + 264 + qt:col0 + 265 + qt],
                    qlatH[64:65, hb, 128 * qt:128 * (qt + 1)], ones128[64:65, :],
                    start=False, stop=False, skip_group_check=True))(ps0, col0, qt, hb),
                  waits=vw)
                T((lambda ps0, col0, qt, hb: lambda e: e.matmul(
                    ps0[:, col0 + 32 * qt:col0 + 32 * (qt + 1)],
                    qpn[:, hb, 128 * qt:128 * (qt + 1)], kstat[:, :],
                    start=False, stop=False, skip_group_check=True))(ps0, col0, qt, hb),
                  waits=vw + gw2)
            mach_t[h] = [("T", P.cnt["T"])]
            tw = P.now("T")
            V((lambda ps0, col0, h: lambda e: e.tensor_reduce(
                out=b128[:, 8 * h:8 * h + 8],
                in_=ps0[:, col0:col0 + 256].rearrange("p (q b) -> p q b", b=32),
                axis=AX.X, op=ALU.max))(ps0, col0, h), waits=tw)
            V((lambda ps0, col0, h: lambda e: e.tensor_copy(
                out=d128[:, 8 * h:8 * h + 8],
                in_=ps0[:, col0 + 256:col0 + 264]))(ps0, col0, h))
            V((lambda ps0, col0, h: lambda e: e.tensor_copy(
                out=q128s[:, 8 * h:8 * h + 8],
                in_=ps0[:, col0 + 264:col0 + 272]))(ps0, col0, h))

        for grp in range(4):
            heads = list(range(4 * grp, 4 * grp + 4))
            P.new_epoch()
            for c in range(NCH):
                gc = grp * 16 + c
                buf = gc % 2
                ld = S((lambda c, buf, grp: lambda e: e.dma_start(
                    out=wqc[:, buf, :],
                    in_=wqT[128 * c:128 * (c + 1),
                            512 * grp:512 * (grp + 1)]))(c, buf, grp),
                    waits=doneQ.get(gc - 2, bar))
                V((lambda buf: lambda e: e.tensor_tensor(
                    out=wqlo[:, buf, :], in0=wqc[:, buf, :],
                    in1=hiv(wqc[:, buf, :]),
                    op=ALU.subtract))(buf), waits=[ld])
                vw = P.now("V")
                for hi_, h in enumerate(heads):
                    for qb in range(2):
                        ps, col = PSQ[hi_ * 2 + qb]
                        for pi in range(3):
                            wap = (hiv(wqc[:, buf, 128 * hi_:128 * (hi_ + 1)])
                                   if pi < 2
                                   else wqlo[:, buf, 128 * hi_:128 * (hi_ + 1)])
                            xap = (xho_hi if pi != 1 else xho_lo)[
                                :, c, 512 * qb:512 * (qb + 1)]
                            st = P.first_touch((PSNAME[id(ps)], col // 512))
                            T((lambda ps, col, wap, xap, st: lambda e: e.matmul(
                                ps[:, col:col + 512], wap, xap, start=st, stop=False,
                                skip_group_check=True))(ps, col, wap, xap, st), waits=vw)
                doneQ[gc] = P.now("T", "V")
            qTg[grp] = P.now("T")
            for hi_ in range(4):
                emit_eq(grp, hi_)
                if hi_ >= 1:
                    emit_mach(grp, hi_ - 1)
            emit_mach(grp, 3)
        flush()
        tap("qlat", qlatH[:, 1, :], 65, TS)

        V(lambda e: e.tensor_scalar(out=b128[:, :], in0=b128[:, :], scalar1=1.0,
                                    scalar2=2.0 - PAD, op0=ALU.mult, op1=ALU.add))
        V(lambda e: e.tensor_tensor(out=s1s[:, :], in0=q128s[:, :], in1=wmx16[:, :],
                                    op=ALU.mult))
        V(lambda e: e.tensor_tensor(out=wmx16[:, :], in0=q128s[:, :], in1=wmn16[:, :],
                                    op=ALU.mult))
        V(lambda e: e.tensor_tensor(out=s1s[:, :], in0=s1s[:, :], in1=wmx16[:, :],
                                    op=ALU.max))
        V(lambda e: e.tensor_tensor(out=b128[:, :], in0=b128[:, :], in1=s1s[:, :],
                                    op=ALU.add))
        V(lambda e: e.tensor_tensor(out=b128[:, :], in0=b128[:, :], in1=d128[:, :],
                                    op=ALU.max))
        V(lambda e: e.tensor_scalar(out=b128[:, :], in0=b128[:, :], scalar1=-1.0,
                                    scalar2=0.0, op0=ALU.mult, op1=ALU.add))
        vw = P.now("V")
        T(lambda e: e.matmul(pC[:, 0:128], b128[:, :], idns[:, :], start=True,
                             stop=False, skip_group_check=True, is_transpose=True),
          waits=vw)
        tw = P.now("T")
        V(lambda e: e.tensor_copy(out=s1s[:, :], in_=pC[:, 0:128]), waits=tw)
        vw = P.now("V")
        G(lambda e: e.dma_start(
            out=ql65[:, :, :][64:65].rearrange("o h (q p) -> o (h q p)", p=128),
            in_=s1s[:, :]), waits=vw)
        flush()
        tap("S128", b128[:, :], 128, 128)
        flush()
        st_p2.close()
        P.set_barrier()

        expf = alloc(st_p3, "expf", [128, 2, TS], F32)
        expl = alloc(st_p3, "expl", [128, 2, TS], BF16)
        mtmp = alloc(st_p3, "mtmp", [128, 256], F32)
        qs3b = alloc(st_p3, "qs3b", [3, 2, TS], BF16)
        rbc = alloc(st_p3, "rbc", [64, TS], F32)
        cxs = alloc(st_p3, "cxs", [64, TS], F32)
        cxu = alloc(st_p3, "cxu", [65, TS], F32)

        bar = barrier()
        P.new_epoch()
        mask_v = {}
        exp_a = {}
        lo_v = {}
        av_t = {}
        epi_v = {}
        epi_free = {}
        recipV = {}
        ktlist = []
        for j in range(8):
            ktlist.append(("own", j))
            ktlist.append(("oth", j))
        qs3w_d = {}
        seg_d = {}

        def emit_qdma(h):
            hb = h % 2
            G((lambda h, hb: lambda e: e.dma_start(
                out=qs3b[:, hb, :], in_=qsum3p[3 * h:3 * h + 3, :]))(h, hb),
              waits=(bar if h < 2 else av_t[(h - 2) * 16 + 15]))
            qs3w_d[h] = P.now("dG", "dG2")

        def emit_lr(g):
            h, idx = g // 16, g % 16
            if idx == 0:
                emit_qdma(h)
            typ, j = ktlist[idx]
            pL = pA if idx % 2 == 0 else pB
            kt = j if typ == "own" else 8 + j
            lcol = 128 * kt
            sfx = 128 * j
            W = TS - sfx
            hb = h % 2
            lw = (bar if g == 0 else []) + \
                ((exp_a[g - 2] + mask_v[g - 2]) if g >= 2 else [])
            cuts = sorted({sfx, TS} | ({512} if sfx < 512 else set())
                          | ({sfx + 512} if sfx + 512 < TS else set()))
            segs = [(cuts[i], cuts[i + 1] - cuts[i]) for i in range(len(cuts) - 1)]
            seg_d[g] = (segs, sfx, kt, W)
            seen_banks = set()
            for q0, w in segs:
                bank = (q0 - sfx) // 512
                fresh = bank not in seen_banks
                seen_banks.add(bank)
                for pi in range(2):
                    wap = (khl[:, lcol:lcol + 128] if pi == 0
                           else kh65[:, lcol:lcol + 128])
                    xap = (qhPK if pi == 0 else ql65)[:, h, q0:q0 + w]
                    T((lambda pL, q0, w, sfx, wap, xap, st: lambda e: e.matmul(
                        pL[:, q0 - sfx:q0 - sfx + w], wap, xap, start=st,
                        stop=False, skip_group_check=True))(
                        pL, q0, w, sfx, wap, xap, pi == 0 and fresh),
                      waits=lw)
            pR = pC if idx % 2 == 0 else pD
            wB = 128 if typ == "own" else min(256, W)
            T((lambda pR, lcol, hb, sfx, wB: lambda e: e.matmul(
                pR[:, 0:wB], ksc3[:, lcol:lcol + 128],
                qs3b[:, hb, sfx:sfx + wB], start=True, stop=False,
                skip_group_check=True))(pR, lcol, hb, sfx, wB),
              waits=lw + qs3w_d[h] + (mask_v[g - 2] if g >= 2 else []))
            tw = P.now("T")
            mm = mDBm[:, 0:128] if typ == "own" else mOBm[:, 0:wB]
            V((lambda pR, wB, mm: lambda e: e.tensor_tensor(
                out=mtmp[:, 0:wB], in0=pR[:, 0:wB], in1=mm, op=ALU.mult))(
                pR, wB, mm), waits=tw)
            V((lambda pL, wB: lambda e: e.tensor_tensor(
                out=pL[:, 0:wB], in0=pL[:, 0:wB], in1=mtmp[:, 0:wB],
                op=ALU.add))(pL, wB))
            ma = mDadd if typ == "own" else mOadd
            V((lambda pL, ma: lambda e: e.tensor_tensor(
                out=pL[:, 0:128], in0=pL[:, 0:128], in1=ma[:, :], op=ALU.add))(
                pL, ma))
            mask_v[g] = P.now("V")
            eb = idx % 2
            A((lambda pL, eb, W: lambda e: e.activation(
                out=expf[:, eb, 0:W], in_=pL[:, 0:W], func=AF.Exp))(pL, eb, W),
              waits=mask_v[g] + (av_t[g - 2] if g >= 2 else []))
            exp_a[g] = [("A", P.cnt["A"])]
            V((lambda eb, W: lambda e: e.tensor_tensor(
                out=expl[:, eb, 0:W], in0=expf[:, eb, 0:W],
                in1=hiv(expf[:, eb, 0:W]), op=ALU.subtract))(eb, W),
              waits=exp_a[g])
            lo_v[g] = P.now("V")

        def emit_av(g):
            h, idx = g // 16, g % 16
            segs, sfx, kt, W = seg_d.pop(g)
            eb = idx % 2
            avw0 = (epi_free.get(h - 1, []) if idx == 0 else [])
            for pi in (0, 2, 1):
                for q0, w in segs:
                    ps = pE if q0 < 512 else pF
                    pcol = q0 - (0 if q0 < 512 else 512)
                    wap = (vtmh if pi < 2 else vtml)[:, kt, :]
                    xsrc = hiv(expf[:, eb, q0 - sfx:q0 - sfx + w]) if pi != 1 \
                        else expl[:, eb, q0 - sfx:q0 - sfx + w]
                    st = (idx == 0 and pi == 0)
                    T((lambda ps, pcol, w, wap, xsrc, st: lambda e: e.matmul(
                        ps[0:65, pcol:pcol + w], wap, xsrc, start=st, stop=False,
                        skip_group_check=True))(ps, pcol, w, wap, xsrc, st),
                      waits=(exp_a[g] if pi != 1 else lo_v[g]) + avw0)
            av_t[g] = [("T", P.cnt["T"])]

        def emit_epi_head(h):
            tw = P.now("T")
            A(lambda e: e.activation(out=cxu[64:65, 0:512], in_=pE[64:65, :],
                                     func=AF.Copy, bias=1e-38, scale=1.0),
              waits=tw)
            A(lambda e: e.activation(out=cxu[64:65, 512:1024], in_=pF[64:65, :],
                                     func=AF.Copy, bias=1e-38, scale=1.0))
            aw = P.now("A")
            V(lambda e: e.reciprocal(out=cxu[64:65, :], in_=cxu[64:65, :]),
              waits=aw)
            recipV[h] = P.now("V")

        def emit_epi(h):
            vw = recipV[h]
            T(lambda e: e.matmul(pE[64:128, :], onesP64[64:65, :],
                                 cxu[64:65, 0:512], start=True, stop=False,
                                 skip_group_check=True), waits=vw)
            T(lambda e: e.matmul(pF[64:128, :], onesP64[64:65, :],
                                 cxu[64:65, 512:1024], start=True, stop=False,
                                 skip_group_check=True))
            tw = P.now("T")
            V(lambda e: e.tensor_copy(out=rbc[:, 0:512], in_=pE[64:128, :]),
              waits=tw)
            V(lambda e: e.tensor_copy(out=rbc[:, 512:1024], in_=pF[64:128, :]))
            V((lambda h: lambda e: e.tensor_tensor(
                out=cxs[:, 0:512], in0=pE[0:64, :], in1=rbc[0:64, 0:512],
                op=ALU.mult))(h), waits=epi_v.get(h - 1, []))
            V(lambda e: e.tensor_tensor(
                out=cxs[:, 512:1024], in0=pF[0:64, :], in1=rbc[0:64, 512:1024],
                op=ALU.mult))
            epi_free[h] = P.now("V")
            V((lambda h: lambda e: e.tensor_copy(
                out=ctxh[:, h, :], in_=cxs[:, :]))(h))
            V((lambda h: lambda e: e.tensor_tensor(
                out=ctxl[:, h, :], in0=cxs[:, :], in1=ctxh[:, h, :],
                op=ALU.subtract))(h))
            epi_v[h] = P.now("V")
            G((lambda h: lambda e: e.dma_start(
                out=ctxPK1[64:128, h, :], in_=ctxPK1[0:64, h, :]))(h), waits=epi_v[h])
            if h % 2 == 1:
                G((lambda h: lambda e: e.dma_start(
                    out=ctxPK2[64:128, h - 1, :], in_=ctxPK2[0:64, h, :]))(h))

        for gg in range(16 * H + 2):
            if gg >= 2:
                emit_av(gg - 2)
                if (gg - 2) % 16 == 15:
                    emit_epi_head((gg - 2) // 16)
            if gg < 16 * H:
                emit_lr(gg)
            if gg >= 2 and (gg - 2) % 16 == 15:
                emit_epi((gg - 2) // 16)
        P.new_epoch()
        flush()
        tap("ctxf", ctxh[0:64, 0, :], 64, TS)
        flush()
        st_p3.close()
        st_kv.close()
        P.set_barrier()

        woc = alloc(st_p4, "woc", [128, 16, 128], F32)
        woc2 = alloc(st_p4, "woc2", [128, 16, 128], F32)
        woclo = alloc(st_p4, "woclo", [128, 16, 128], BF16)
        woclo2 = alloc(st_p4, "woclo2", [128, 16, 128], BF16)
        vupS = alloc(st_p4, "vupS", [128, H, L], F32)
        vupSlo = alloc(st_p4, "vupSlo", [128, H, L], BF16)
        wvoPK = alloc(st_p4, "wvoPK", [128, 2, H, 128], BF16)
        wvoPR = alloc(st_p4, "wvoPR", [128, 2, 8, 128], BF16)
        wvol = alloc(st_p4, "wvol", [64, 2, H, 128], BF16)
        xoc2 = alloc(st_p4, "xoc2", [128, 2, TS], F32)

        bar = barrier()
        S(lambda e: e.dma_start(out=vupS[:, :, :],
                                in_=vupTi.rearrange("h d l -> d h l")), waits=bar)
        vw0 = P.now("dS", "dS2")
        V(lambda e: e.tensor_tensor(out=vupSlo[:].rearrange("p h l -> p (h l)"),
                                    in0=vupS[:].rearrange("p h l -> p (h l)"),
                                    in1=hiv(vupS[:].rearrange("p h l -> p (h l)")),
                                    op=ALU.subtract), waits=vw0 + bar)

        wocB = [woc, woc2]
        wocloB = [woclo, woclo2]
        wocw = {}
        bT = {}
        aT = {}
        dwD = {}
        xvD = {}
        loV = {}
        evV = {}
        xow = {}

        def p4_dma(eo):
            ld = S((lambda eo: lambda e: e.dma_start(
                out=wocB[eo % 2][:, :, :],
                in_=woT[:, 128 * eo:128 * (eo + 1)].rearrange("(c p) m -> p c m",
                                                              p=128)))(eo),
                waits=bT.get(eo - 2, []) + (bar if eo < 2 else []))
            wocw[eo] = [ld]
            xow[eo] = [S((lambda eo: lambda e: e.dma_start(
                out=xoc2[:, eo % 2, :], in_=xoT[128 * eo:128 * (eo + 1), :]))(eo),
                waits=xvD.get(eo - 2, []))]

        def p4_lo(eo):
            b = eo % 2
            V((lambda b: lambda e: e.tensor_tensor(
                out=wocloB[b][:].rearrange("p c m -> p (c m)"),
                in0=wocB[b][:].rearrange("p c m -> p (c m)"),
                in1=hiv(wocB[b][:].rearrange("p c m -> p (c m)")),
                op=ALU.subtract))(b), waits=wocw[eo] + bT.get(eo - 2, []))
            loV[eo] = P.now("V")

        PS4 = (pC, pD, pE, pF)

        def p4_build(eo):
            b = eo % 2
            bw = loV[eo] + evV.get(eo - 1, [])
            for h in range(H):
                ps, col = PS4[h // 4], 128 * (h % 4)
                for pi in range(3):
                    wap = (hiv(vupS[:, h, :]) if pi < 2 else vupSlo[:, h, :])
                    xap = (hiv(wocB[b][:, h, :]) if pi != 1
                           else wocloB[b][:, h, :])
                    st = (pi == 0 and h % 4 == 0)
                    T((lambda ps, col, wap, xap, st: lambda e: e.matmul(
                        ps[0:64, col:col + 128], wap, xap, start=st, stop=False,
                        skip_group_check=True))(ps, col, wap, xap, st), waits=bw)
            bT[eo] = [("T", P.cnt["T"])]
            tw = P.now("T")
            for gi, ps2 in enumerate(PS4):
                h0 = 4 * gi
                A((lambda b, ps2, h0: lambda e: e.activation(
                    out=wvoPK[0:64, b, h0:h0 + 4, :].rearrange(
                        "p a m -> p (a m)"),
                    in_=ps2[0:64, 0:512], func=AF.Copy))(b, ps2, h0),
                  waits=tw + dwD.get(eo - 2, []))
            evaw = P.now("A")
            for gi, ps2 in enumerate(PS4):
                h0 = 4 * gi
                V((lambda b, ps2, h0: lambda e: e.tensor_tensor(
                    out=wvol[:, b, h0:h0 + 4, :],
                    in0=ps2[0:64, 0:512].rearrange("p (a m) -> p a m", a=4),
                    in1=wvoPK[0:64, b, h0:h0 + 4, :],
                    op=ALU.subtract))(b, ps2, h0), waits=evaw)
            evV[eo] = P.now("V", "A")

        def p4_pack(eo):
            b = eo % 2
            A((lambda b: lambda e: e.activation(
                out=wvoPR[0:64, b, :, :],
                in_=wvoPK[0:64, b, :, :][:, 0::2, :],
                func=AF.Copy))(b),
              waits=aT.get(eo - 2, []))
            vw = P.now("V") + P.now("A")
            dlo = G((lambda b: lambda e: e.dma_start(
                out=wvoPK[64:128, b, :, :],
                in_=wvol[:, b, :, :]))(b),
                waits=vw + aT.get(eo - 2, []))
            dhi = G((lambda b: lambda e: e.dma_start(
                out=wvoPR[64:128, b, :, :],
                in_=wvoPK[0:64, b, :, :][:, 1::2, :]))(b))
            dwD[eo] = [dlo, dhi]

        def p4_apply(eo):
            b = eo % 2
            ld = xow[eo][0]
            for qb in range(2):
                ps = pA if qb == 0 else pB
                for h in range(H):
                    st = (h == 0)
                    T((lambda b, ps, h, qb, st: lambda e: e.matmul(
                        ps[:, 0:512], wvoPK[:, b, h, :],
                        ctxPK1[:, h, 512 * qb:512 * (qb + 1)], start=st, stop=False,
                        skip_group_check=True))(b, ps, h, qb, st),
                      waits=dwD[eo] + (xvD.get(eo - 1, []) if st else []))
                for he in range(8):
                    T((lambda b, ps, he, qb: lambda e: e.matmul(
                        ps[:, 0:512], wvoPR[:, b, he, :],
                        ctxPK2[:, 2 * he, 512 * qb:512 * (qb + 1)], start=False,
                        stop=False, skip_group_check=True))(b, ps, he, qb),
                      waits=dwD[eo])
            aT[eo] = [("T", P.cnt["T"])]
            tw = P.now("T")
            V((lambda eo: lambda e: e.tensor_tensor(
                out=xp[:, eo, 0:512], in0=pA[:, 0:512], in1=xoc2[:, eo % 2, 0:512],
                op=ALU.add))(eo), waits=tw + [ld])
            V((lambda eo: lambda e: e.tensor_tensor(
                out=xp[:, eo, 512:1024], in0=pB[:, 0:512],
                in1=xoc2[:, eo % 2, 512:1024], op=ALU.add))(eo))
            xvD[eo] = P.now("V")

        p4_dma(0)
        p4_lo(0)
        p4_build(0)
        p4_pack(0)
        p4_dma(1)
        p4_lo(1)
        for eo in range(NCH):
            if eo + 1 < NCH:
                p4_build(eo + 1)
                p4_pack(eo + 1)
            p4_apply(eo)
            if eo + 2 < NCH:
                p4_dma(eo + 2)
                p4_lo(eo + 2)
        P.new_epoch()
        flush()
        tap("xp", xp[:, 0, :], 128, TS)
        flush()
        st_p4.close()
        P.set_barrier()

        x8 = alloc(st_p5, "x8", [128, NCH, TS], FP8)
        wst = alloc(st_p5, "wst", [128, 2, 8, 2, 128], F32)
        mrow = wst[0:1, 0, 0:4, :, :].rearrange("p k j m -> p (k j m)")
        rrow = wst[0:1, 0, 4:8, :, :].rearrange("p k j m -> p (k j m)")
        scr1 = srow[0:1, :]
        scr2 = alloc(st_p5, "scr2_5", [1, TS], F32)
        w0c = alloc(st_p5, "w0c", [128, 3, 8, 2, 128], FP8)
        w1c = alloc(st_p5, "w1c", [128, 3, 8, 2, 128], FP8)
        wlc = alloc(st_p5, "wlc", [128, 2, 16, 2, 128], FP8)
        gb = bcM[:].bitcast(BF16)[:, 0:1024]
        f1b = AR[:, 131072:133120].bitcast(BF16)

        bar = barrier()
        P.new_epoch()
        for c in range(NCH):
            cw = bar if c == 0 else P.now("T")
            A((lambda c: lambda e: e.activation(
                out=gb[:], in_=xp[:, c, :], func=AF.Square))(c),
              waits=cw)
            for qb in range(2):
                T((lambda c, qb, st: lambda e: e.matmul(
                    pA[0:1, 512 * qb:512 * (qb + 1)], ones128b[:],
                    hiv(xp[:, c, 512 * qb:512 * (qb + 1)]),
                    start=st, stop=False, skip_group_check=True))(
                    c, qb, P.first_touch(("A", qb))), waits=bar if c == 0 else [])
            aw = P.now("A")
            for qb in range(2):
                T((lambda c, qb, st: lambda e: e.matmul(
                    pA[32:33, 512 * qb:512 * (qb + 1)], ones128b[:],
                    gb[:, 512 * qb:512 * (qb + 1)], start=st, stop=False,
                    skip_group_check=True))(c, qb, P.first_touch(("A32", qb))),
                  waits=aw)
        tw = P.now("T")
        V(lambda e: e.tensor_scalar(out=mrow[:, :], in0=pA[0:1, :], scalar1=1.0 / E,
                                    scalar2=0.0, op0=ALU.mult, op1=ALU.add), waits=tw)
        V(lambda e: e.tensor_scalar(out=rrow[:, :], in0=pA[32:33, :], scalar1=1.0 / E,
                                    scalar2=1e-5, op0=ALU.mult, op1=ALU.add))
        V(lambda e: e.scalar_tensor_tensor(out=scr1[:, :], in0=mrow[:, :], scalar=-1.0,
                                           in1=mrow[:, :], op0=ALU.mult, op1=ALU.mult))
        V(lambda e: e.tensor_tensor(out=rrow[:, :], in0=rrow[:, :], in1=scr1[:, :],
                                    op=ALU.add))
        V(lambda e: e.reciprocal(out=scr1[:, :], in_=rrow[:, :]))
        vw = P.now("V")
        A(lambda e: e.activation(out=scr1[:, :], in_=scr1[:, :], func=AF.Sqrt),
          waits=vw)
        aw = P.now("A")
        V(lambda e: e.tensor_tensor(out=scr2[:, :], in0=scr1[:, :], in1=scr1[:, :],
                                    op=ALU.mult), waits=aw)
        V(lambda e: e.tensor_tensor(out=scr2[:, :], in0=scr2[:, :], in1=rrow[:, :],
                                    op=ALU.mult))
        V(lambda e: e.tensor_scalar(out=scr2[:, :], in0=scr2[:, :], scalar1=-0.5,
                                    scalar2=1.5, op0=ALU.mult, op1=ALU.add))
        V(lambda e: e.tensor_tensor(out=rrow[:, :], in0=scr1[:, :], in1=scr2[:, :],
                                    op=ALU.mult))
        P.new_epoch()
        vw = P.now("V")
        for srcrow, dst in ((mrow, bcM), (rrow, bcR)):
            for qb in range(2):
                T((lambda srcrow, qb: lambda e: e.matmul(
                    pB[:, 512 * qb:512 * (qb + 1)], onesrow[:, :],
                    srcrow[:, 512 * qb:512 * (qb + 1)], start=True, stop=False,
                    skip_group_check=True))(srcrow, qb), waits=vw)
            tw = P.now("T")
            V((lambda dst: lambda e: e.tensor_copy(out=dst[:], in_=pB[:]))(dst), waits=tw)
            vw = P.now("V")
        for c in range(NCH):
            V((lambda c: lambda e: e.tensor_tensor(
                out=pB[:, :], in0=xp[:, c, :], in1=bcM[:], op=ALU.subtract))(c))
            V((lambda c: lambda e: e.tensor_tensor(
                out=x8[:, c, :], in0=pB[:, :], in1=bcR[:], op=ALU.mult))(c))
        flush()
        tap("x8", x8[:, 0, :].bitcast(BF16)[:, 0:512].bitcast(F32)[:, 0:128], 128, 128)

        P.new_epoch()
        sw = P.now("T", "V")
        done = {-2: sw, -1: sw}
        gelu_done = {}
        mm_done = {}
        castA = {}
        ldD = {}

        def g_dma(hc):
            buf = hc % 2
            l0 = S((lambda hc, buf: lambda e: e.dma_start(
                out=wst[:, buf, :, :, :],
                in_=wg[0, :, 128 * hc:128 * (hc + 1)].rearrange(
                    "(k j p) m -> p k j m", j=2, p=128)))(hc, buf),
                waits=castA.get(hc - 2, []) +
                (P.now("T", "V", "A") if hc < 2 else []))
            l1 = G((lambda hc, b3: lambda e: e.dma_start(
                out=w1c[:, b3, :, :, :],
                in_=wg[1, :, 128 * hc:128 * (hc + 1)].rearrange(
                    "(k j p) m -> p k j m", j=2, p=128)))(hc, hc % 3),
                waits=mm_done.get(hc - 3, []))
            ldD[hc] = [l0, l1]
            A((lambda buf, b3: lambda e: e.activation(
                out=w0c[:, b3, :, :, :].rearrange("p k j m -> p (k j m)"),
                in_=wst[:, buf, :, :, :].rearrange("p k j m -> p (k j m)"),
                func=AF.Copy))(buf, hc % 3),
              waits=[l0] + mm_done.get(hc - 3, []))
            castA[hc] = [("A", P.cnt["A"])]

        g_dma(0)
        g_dma(1)
        for hc in range(64):
            buf = hc % 2
            lw = [ldD[hc][1]] + castA[hc] + \
                gelu_done.get(hc - 2, (P.now("V") if hc < 2 else []))
            if buf == 0:
                psg = [(pA, 0), (pA, 512)]
                psf = [(pB, 0), (pB, 512)]
            else:
                psg = [(pC, 0), (pD, 0)]
                psf = [(pE, 0), (pF, 0)]
            for qq in range(4):
                psG, colG = psg[qq // 2]
                psF, colF = psf[qq // 2]
                cg = colG + 256 * (qq % 2)
                cf = colF + 256 * (qq % 2)
                for kc in range(8):
                    T((lambda b3, qq, kc, psG, cg, st: lambda e: e.matmul(
                        psG[:, cg:cg + 256], w0c[:, b3, kc, :, :],
                        x8[:, 2 * kc:2 * kc + 2, 256 * qq:256 * (qq + 1)],
                        start=st, stop=False, skip_group_check=True,
                        perf_mode=PM.DoubleRow))(hc % 3, qq, kc, psG, cg,
                                                 (kc == 0 and qq % 2 == 0)), waits=lw)
                    T((lambda b3, qq, kc, psF, cf, st: lambda e: e.matmul(
                        psF[:, cf:cf + 256], w1c[:, b3, kc, :, :],
                        x8[:, 2 * kc:2 * kc + 2, 256 * qq:256 * (qq + 1)],
                        start=st, stop=False, skip_group_check=True,
                        perf_mode=PM.DoubleRow))(hc % 3, qq, kc, psF, cf,
                                                 (kc == 0 and qq % 2 == 0)), waits=lw)
            mm_done[hc] = P.now("T")
            if hc + 2 < 64:
                g_dma(hc + 2)
            tw = mm_done[hc]
            gsrc = [(psg[0][0], psg[0][1]), (psg[1][0], psg[1][1])]
            fsrc = [(psf[0][0], psf[0][1]), (psf[1][0], psf[1][1])]
            V((lambda gsrc: lambda e: e.tensor_copy(
                out=gb[:, 0:512], in_=gsrc[0][0][:, gsrc[0][1]:gsrc[0][1] + 512]))(gsrc),
              waits=tw)
            V((lambda gsrc: lambda e: e.tensor_copy(
                out=gb[:, 512:1024], in_=gsrc[1][0][:, gsrc[1][1]:gsrc[1][1] + 512]))(gsrc))
            V((lambda fsrc: lambda e: e.tensor_copy(
                out=f1b[:, 0:512], in_=fsrc[0][0][:, fsrc[0][1]:fsrc[0][1] + 512]))(fsrc))
            V((lambda fsrc: lambda e: e.tensor_copy(
                out=f1b[:, 512:1024], in_=fsrc[1][0][:, fsrc[1][1]:fsrc[1][1] + 512]))(fsrc))
            vw = P.now("V")
            A(lambda e: e.activation(out=gb[:], in_=gb[:], func=AF.Gelu_apprx_tanh),
              waits=vw)
            aw = P.now("A")
            V(lambda e: e.tensor_tensor(out=gb[:], in0=gb[:], in1=f1b[:],
                                        op=ALU.mult), waits=aw)
            V((lambda hc: lambda e: e.tensor_copy(out=act8[:, hc, :], in_=gb[:]))(hc))
            gelu_done[hc] = P.now("V")
            done[hc] = P.now("T", "V")
        flush()

        outsW = x8[:, 0:4, :].rearrange("p a b -> p (a b)").bitcast(F32)
        P.new_epoch()
        sw = P.now("T", "V")
        done = {-2: sw, -1: sw}
        outdma = {}
        castL = {}
        ldL = {}
        fcV = {}

        def wl_dma(eo):
            buf = eo % 2
            w = done.get(eo - 2, [])
            gate_g = mm_done.get(63, []) if eo < 2 else []
            ld1 = G((lambda eo, buf: lambda e: e.dma_start(
                out=wlc[:, buf, 0:16, :, :],
                in_=wlin[0:4096, 128 * eo:128 * (eo + 1)].rearrange(
                    "(k j p) m -> p k j m", j=2, p=128)))(eo, buf), waits=w)
            lrS = S((lambda eo: lambda e: e.dma_start(
                out=wst[:, 0, :, :, :],
                in_=wlin[4096:6144, 128 * eo:128 * (eo + 1)].rearrange(
                    "(k j p) m -> p k j m", j=2, p=128)))(eo),
                waits=castL.get(eo - 1, castA.get(62, [])))
            lrA = AD((lambda eo: lambda e: e.dma_start(
                out=wst[:, 1, :, :, :],
                in_=wlin[6144:8192, 128 * eo:128 * (eo + 1)].rearrange(
                    "(k j p) m -> p k j m", j=2, p=128)))(eo),
                waits=castL.get(eo - 1, castA.get(63, [])))
            ldL[eo] = [ld1]
            A((lambda buf: lambda e: e.activation(
                out=w0c[:, buf, :, :, :].rearrange("p k j m -> p (k j m)"),
                in_=wst[:, 0, :, :, :].rearrange("p k j m -> p (k j m)"),
                func=AF.Copy))(buf),
              waits=[lrS, lrA] + w + gate_g)
            A((lambda buf: lambda e: e.activation(
                out=w1c[:, buf, :, :, :].rearrange("p k j m -> p (k j m)"),
                in_=wst[:, 1, :, :, :].rearrange("p k j m -> p (k j m)"),
                func=AF.Copy))(buf))
            castL[eo] = [("A", P.cnt["A"])]

        wl_dma(0)
        for eo in range(NCH):
            buf = eo % 2
            if eo + 1 < NCH:
                wl_dma(eo + 1)
            lw = ldL[eo] + fcV.get(eo - 2, [])
            ps = pC if eo % 2 == 0 else pD
            for qq in range(4):
                w_mm = lw + (P.now("V") if qq >= 2 else [])
                for kc in range(32):
                    st = P.first_touch((PSNAME[id(ps)], 0)) if (qq % 2 == 0 and kc == 0) \
                        else False
                    wsrc = (wlc[:, buf, kc, :, :] if kc < 16 else
                            (w0c if kc < 24 else w1c)[:, buf, kc % 8, :, :])
                    T((lambda ps, buf, qq, kc, st, wsrc: lambda e: e.matmul(
                        ps[:, 256 * (qq % 2):256 * (qq % 2) + 256],
                        wsrc,
                        act8[:, 2 * kc:2 * kc + 2, 256 * qq:256 * (qq + 1)],
                        start=st, stop=False, skip_group_check=True,
                        perf_mode=PM.DoubleRow))(ps, buf, qq, kc, st, wsrc),
                      waits=w_mm + (castL[eo] if kc >= 16 else []))
                if qq % 2 == 1:
                    tw = P.now("T")
                    hb = qq // 2
                    ow = [outdma[eo - 1]] if eo - 1 in outdma else []
                    V((lambda ps: lambda e: e.tensor_copy(
                        out=f1b[:, 0:512], in_=ps[:, 0:512]))(ps), waits=tw)
                    V((lambda eo, hb: lambda e: e.tensor_tensor(
                        out=outsW[:, 512 * hb:512 * (hb + 1)],
                        in0=xp[:, eo, 512 * hb:512 * (hb + 1)], in1=f1b[:, 0:512],
                        op=ALU.add))(eo, hb), waits=ow)
                    P.new_epoch()
            vw = P.now("V")
            fcV[eo] = vw
            done[eo] = P.now("T")
            outdma[eo] = S((lambda eo: lambda e: e.dma_start(
                out=outT[128 * eo:128 * (eo + 1), :], in_=outsW[:, :]))(eo),
                waits=vw)
        fin = P.now("dS", "dS2")
        P.emit("gpsimd", lambda e: e.memset(scr2[0:1, 0:1], 0.0), fin, None)
        flush()

        for st in (st_p5, st_per):
            st.close()
    es.close()
    return nc


_MASK_CACHE = {}


def _masks(pi):
    if pi in _MASK_CACHE:
        return _MASK_CACHE[pi]
    s = np.arange(128)[:, None]
    q = np.arange(128)[None, :]
    mD_add = np.where(s > q, np.float32(-1e30), np.float32(0.0))
    mDB = ((q - s >= 0) & (q - s <= 64)).astype(np.float32)
    sub = (s - q >= 64).astype(np.float32)
    if pi == 0:
        mO_add = np.full((128, 128), -1e30, np.float32)
        mOB = np.concatenate([np.zeros((128, 128), np.float32), sub], axis=1)
    else:
        mO_add = np.zeros((128, 128), np.float32)
        mOB = np.concatenate([sub, np.zeros((128, 128), np.float32)], axis=1)
    out = (mD_add, mO_add, mDB.astype(np.float32), mOB)
    _MASK_CACHE[pi] = out
    return out


_PROG = None


def kernel(L0_attn_hidden_states, L0_attn_Wq, L0_attn_Wk, L0_attn_Wv,
           L0_attn_q_to_latent, L0_attn_v_up, L0_attn_Wo, L0_ffn_w_gating,
           L0_ffn_w_linear):
    global _PROG
    x = np.ascontiguousarray(L0_attn_hidden_states, dtype=np.float32)
    wq = np.ascontiguousarray(L0_attn_Wq, dtype=np.float32)
    wqT = np.ascontiguousarray(wq.T)
    wk = np.ascontiguousarray(L0_attn_Wk, dtype=np.float32)
    wv = np.ascontiguousarray(L0_attn_Wv, dtype=np.float32)
    wkT = np.ascontiguousarray(wk.T)
    wvT = np.ascontiguousarray(wv.T)
    q2l = np.ascontiguousarray(L0_attn_q_to_latent, dtype=np.float32)
    vupT = np.ascontiguousarray(np.transpose(L0_attn_v_up, (0, 2, 1)))
    woT = np.ascontiguousarray(np.asarray(L0_attn_Wo, dtype=np.float32).T)
    wg = np.ascontiguousarray(L0_ffn_w_gating, dtype=np.float32)
    wlin = np.ascontiguousarray(L0_ffn_w_linear, dtype=np.float32)
    idn = np.eye(128, dtype=np.float32)

    if _PROG is None:
        _PROG = build_program()
    nc = _PROG

    in_maps = []
    for c in range(8):
        b, pi = c // 2, c % 2
        own = np.concatenate([np.arange(128 * (2 * i + pi), 128 * (2 * i + pi) + 128)
                              for i in range(8)])
        oth = np.concatenate([np.arange(128 * (2 * i + 1 - pi),
                                        128 * (2 * i + 1 - pi) + 128)
                              for i in range(8)])
        xoT = np.ascontiguousarray(x[b][own].T)
        xrT = np.ascontiguousarray(x[b][oth].T)
        mD_add, mO_add, mDB, mOB = _masks(pi)
        in_maps.append({
            "xoT": xoT, "xrT": xrT, "wqT": wqT, "q2l": q2l, "wkT": wkT, "wvT": wvT,
            "vupT": vupT, "woT": woT, "wg": wg, "wlin": wlin, "idn": idn,
            "mD_add": mD_add, "mO_add": mO_add, "mDB_mult": mDB, "mOB_mult": mOB,
        })
    res = run_bass_kernel_spmd(nc, in_maps, core_ids=list(range(8)))
    out = np.empty((4, 2048, 2048), dtype=np.float32)
    for c in range(8):
        b, pi = c // 2, c % 2
        own = np.concatenate([np.arange(128 * (2 * i + pi), 128 * (2 * i + pi) + 128)
                              for i in range(8)])
        out[b][own] = res.results[c]["outT"].T
    return out
